# version 04
# baseline (speedup 1.0000x reference)
import numpy as np
import ml_dtypes
import concourse.bass as bass
import concourse.mybir as mybir
from concourse.bass_utils import run_bass_kernel_spmd
from contextlib import ExitStack

F32 = mybir.dt.float32
BF16 = mybir.dt.bfloat16
AF = mybir.ActivationFunctionType
ALU = mybir.AluOpType
AX = mybir.AxisListType

S = 2048
D = 1024
NT = 16
NSEQ = 2
H = 16
DH = 64
E = 8
DFF = 4096
NDC = 8
LN_EPS = 1e-5
QK_EPS = 1e-6

ENGINES = ("tensor", "scalar", "vector", "gpsimd", "sync")


class Prog:

    def __init__(self, nc, stack):
        self.nc = nc
        self.ops = {e: [] for e in ENGINES}
        self.sems = {}
        self.cnt = {}
        self.stack = stack

    def sem(self, name):
        self.sems[name] = self.stack.enter_context(self.nc.semaphore(name))
        self.cnt[name] = 0

    def op(self, eng, fn, waits=(), inc=None, incn=1):
        waits = [(s, v) for (s, v) in waits if v > 0]
        if inc in ("dmax", "dmaw", "st") and self.cnt[inc] > 0:
            waits = waits + [(inc, self.cnt[inc])]
        self.ops[eng].append((waits, fn, inc, incn))
        if inc is not None:
            self.cnt[inc] += incn
            return self.cnt[inc]
        return None

    def w(self, *names):
        return [(n, self.cnt[n]) for n in names]

    def emit(self, block):
        for eng in ENGINES:
            body = self.ops[eng]
            if not body:
                continue

            def mk(body):
                def closure(Eng):
                    last = {}
                    for waits, fn, inc, incn in body:
                        for s, v in waits:
                            if last.get(s, -1) < v:
                                Eng.wait_ge(self.sems[s], v)
                                last[s] = v
                        ins = fn(Eng) if fn is not None else None
                        if inc is not None:
                            ins.then_inc(self.sems[inc], incn)
                return closure

            getattr(block, eng)(mk(body))


def build_aux():
    bf = ml_dtypes.bfloat16
    aux = {}
    aux["id128_bf"] = np.eye(128, dtype=bf)
    aux["id128_f32"] = np.eye(128, dtype=np.float32)
    idhi = np.zeros((128, 64), dtype=np.float32)
    idhi[64:128] = np.eye(64)
    aux["idhi_bf"] = idhi.astype(bf)
    aux["ones1_bf"] = np.ones((1, 128), dtype=bf)
    aux["ones_f32"] = np.ones((128, 64), dtype=np.float32)
    blocks = np.zeros((128, 2), dtype=np.float32)
    blocks[0:64, 0] = 1.0
    blocks[64:128, 1] = 1.0
    aux["blocks2"] = blocks
    aux["blocks2T"] = np.ascontiguousarray(blocks.T)
    ma = np.zeros((144, 3 * 128), dtype=np.float32)
    for v in range(3):
        jl = np.arange(144)[:, None]
        il = np.arange(128)[None, :]
        ok = (jl >= il) & (jl <= il + 16)
        if v == 0:
            ok = ok & (jl >= 8)
        if v == 2:
            ok = ok & (jl <= 135)
        ma[:, v * 128:(v + 1) * 128] = ok.astype(np.float32)
    aux["maskA_a"] = ma[0:128].astype(bf)
    aux["maskA_b"] = ma[128:144].astype(bf)
    return aux


class KB:
    def __init__(self, dbg_stage=0, stop_after=0, p2_sub=0):
        self.dbg_stage = dbg_stage
        self.stop_after = stop_after
        self.p2_sub = p2_sub

    def mm_group(self, psum_ap, pairs, first_waits=(), inc="pe"):
        p = self.p
        n = len(pairs)
        v = None
        for i, (l, r) in enumerate(pairs):
            wts = list(first_waits) if i == 0 else ()

            def fn(Eng, o=psum_ap, l=l, r=r, st=(i == 0), sp=(i == n - 1)):
                return Eng.matmul(o, l, r, start=st, stop=sp, skip_group_check=True)

            if i == n - 1:
                v = p.op("tensor", fn, waits=wts, inc=inc)
            else:
                p.op("tensor", fn, waits=wts)
        return v

    def ln_stats(self, src_ap, waits_sc=(), waits_ve=()):
        p = self.p
        st = self.STAT
        p.op("scalar", lambda E, s=src_ap: E.activation(self.SQTR, s, AF.Square,
                                                        accum_out=st[:, 1:2]),
             waits=list(waits_sc) + p.w("dve"), inc="act")
        p.op("scalar", lambda E, s=src_ap: E.activation(self.SQTR, s, AF.Identity,
                                                        accum_out=st[:, 0:1]),
             waits=list(waits_sc), inc="act")
        p.op("vector", lambda E: E.tensor_scalar(st[:, 2:3], st[:, 0:1], 1.0 / D, None, ALU.mult),
             waits=list(waits_ve) + p.w("act"))
        p.op("vector", lambda E: E.tensor_scalar(st[:, 4:5], st[:, 1:2], 1.0 / D, None, ALU.mult),
             inc="dve")
        p.op("scalar", lambda E: E.activation(st[:, 5:6], st[:, 2:3], AF.Square),
             waits=p.w("dve"), inc="act")
        p.op("vector", lambda E: E.tensor_tensor(st[:, 6:7], st[:, 4:5], st[:, 5:6], ALU.subtract),
             waits=p.w("act"), inc="dve")
        p.op("scalar", lambda E: E.activation(st[:, 5:6], st[:, 6:7], AF.Ln, bias=LN_EPS),
             waits=p.w("dve"), inc="act")
        p.op("scalar", lambda E: E.activation(st[:, 3:4], st[:, 5:6], AF.Exp, scale=-0.5),
             waits=p.w("act"), inc="act")
        return st[:, 2:3], st[:, 3:4]

    def tile_to_fm(self, src_bf, bank, dst, first_waits=(), scale=None, bias=None):
        p = self.p
        pbb = self.pb_bf[bank]
        for dc in range(NDC):
            wts = first_waits if dc == 0 else ()

            def fn(Eng, o=pbb[:, dc * 128:(dc + 1) * 128],
                   i=src_bf[:, dc * 128:(dc + 1) * 128]):
                return Eng.transpose(o, i, self.id128)

            p.op("tensor", fn, waits=wts, inc="pe" if dc == NDC - 1 else None)
        if scale is None:
            p.op("scalar", lambda E: E.copy(dst, pbb.rearrange("p (dc c) -> p dc c", c=128)),
                 waits=p.w("pe"), inc="act")
        else:
            for dc in range(NDC):
                def fn(Eng, o=dst[dc], i=pbb[:, dc * 128:(dc + 1) * 128],
                       sc=scale[dc], bi=bias[dc]):
                    return Eng.activation(o, i, AF.Identity, scale=sc, bias=bi)

                p.op("scalar", fn, waits=p.w("pe") if dc == 0 else (),
                     inc="act" if dc == NDC - 1 else None)

    def build(self):
        nc = bass.Bass("TRN2", target_bir_lowering=False)
        self.nc = nc
        stack = ExitStack()
        self.stack = stack
        p = Prog(nc, stack)
        self.p = p
        for sname in ("pe", "act", "dve", "pool", "dmaw", "dmax", "st",
                      "dx0", "dx1", "dw0", "dw1"):
            p.sem(sname)

        dr = lambda name, shape, dt, kind: nc.dram_tensor(name, shape, dt, kind=kind).ap()
        self.x_in = dr("x", [NSEQ, NT, 128, D], F32, "ExternalInput")
        shp = {"L0_attn_w_qkv": [3 * D, D], "L0_attn_b_qkv": [3 * D],
               "L0_attn_w_out": [D, D], "L0_attn_b_out": [D],
               "L0_ffn_ln1_g": [D], "L0_ffn_ln1_b": [D],
               "L0_ffn_w_qkv": [D, 3 * D], "L0_ffn_b_qkv": [3 * D],
               "L0_ffn_w_o": [D, D], "L0_ffn_b_o": [D],
               "L0_ffn_ln2_g": [D], "L0_ffn_ln2_b": [D],
               "L0_ffn_gate_w": [D, E], "L0_ffn_gate_b": [E],
               "L0_ffn_W1": [E, D, DFF], "L0_ffn_b1": [E, DFF],
               "L0_ffn_W2": [E, DFF, D], "L0_ffn_b2": [E, D]}
        self.win = {n: dr(n, s_, F32, "ExternalInput") for n, s_ in shp.items()}
        aux_shapes = {"id128_bf": ([128, 128], BF16), "id128_f32": ([128, 128], F32),
                      "idhi_bf": ([128, 64], BF16), "ones1_bf": ([1, 128], BF16),
                      "ones_f32": ([128, 64], F32), "blocks2": ([128, 2], F32),
                      "blocks2T": ([2, 128], F32), "maskA_a": ([128, 384], BF16),
                      "maskA_b": ([16, 384], BF16)}
        self.auxin = {n: dr(n, sh, dt, "ExternalInput") for n, (sh, dt) in aux_shapes.items()}
        self.out = dr("out", [NSEQ, NT, 128, D], F32, "ExternalOutput")
        if self.dbg_stage:
            self.dbg = dr("dbg", [NSEQ, NT, 128, D], F32, "ExternalOutput")
        self.x1scr = dr("x1scr", [NSEQ, NT, 128, D], F32, "Internal")
        self.uscr = dr("uscr", [NSEQ, NT, 128, D], F32, "Internal")
        self.waT = dr("waT", [D, 3 * D], BF16, "Internal")
        self.woaT = dr("woaT", [D, D], BF16, "Internal")

        self.pb = [stack.enter_context(nc.psum_tensor(f"pb{i}", [128, 512], F32)).ap()
                   for i in range(8)]
        self.pb_bf = [b.bitcast(BF16) for b in self.pb]

        sb = lambda name, shape, dt: stack.enter_context(nc.sbuf_tensor("s_" + name, shape, dt)).ap()
        self.id128 = sb("id128", [128, 128], BF16)
        self.idf32 = sb("idf32", [128, 128], F32)
        self.idhi = sb("idhi", [128, 64], BF16)
        self.ones1 = sb("ones1", [1, 128], BF16)
        self.onesf = sb("onesf", [128, 64], F32)
        self.blocks2 = sb("blocks2", [128, 2], F32)
        self.blocks2T = sb("blocks2T", [2, 128], F32)
        self.maskAa = sb("maskAa", [128, 384], BF16)
        self.maskAb = sb("maskAb", [16, 384], BF16)
        self.bA_c = sb("bA_c", [128, 24], F32)
        self.lnp_c = sb("lnp_c", [128, 32], F32)
        self.bB_c = sb("bB_c", [128, 16], F32)
        self.b1_c = sb("b1_c", [128, 256], F32)
        self.b2_bf = sb("b2_bf", [8, 1024], BF16)
        self.boutA_row = sb("boutA_row", [1, 1024], BF16)
        self.boB_row = sb("boB_row", [1, 1024], BF16)
        self.bvB_row = sb("bvB_row", [1, 1024], BF16)
        self.gateb_row = sb("gateb_row", [1, 8], BF16)
        self.gatew_bf = sb("gatew_bf", [128, 64], BF16)
        self.gatew_f32 = sb("gatew_f32", [128, 64], F32)
        self.gatew_lo = sb("gatew_lo", [128, 64], BF16)
        self.ew_c = sb("ew_c", [128, 128], F32)
        self.STAT = sb("STAT", [128, 16], F32)
        self.SQT = sb("SQT", [128, 1024], F32)
        self.SQTR = self.SQT
        self.XSTG = [sb(f"XSTG{i}", [128, 1024], F32) for i in range(2)]
        self.USTG = [sb(f"USTG{i}", [128, 1024], F32) for i in range(2)]
        self.ABF = sb("ABF", [128, 1024], BF16)
        self.ewT = sb("ewT", [8, 128], BF16)
        self.ew_bf = sb("ew_bf", [128, 8], BF16)
        self.CONSTV = sb("CONSTV", [128, 8], F32)

        cvals = [LN_EPS, QK_EPS, -0.5, 0.5, -1.0, 0.125, 0.0]
        for j, v in enumerate(cvals):
            p.op("vector", lambda E, j=j, v=v: E.memset(self.CONSTV[:, j:j + 1], v),
                 inc="dve")
            nc.const_aps.register(F32, v, self.CONSTV[:, j:j + 1])
        cbar = p.w("dve")
        for eng in ("tensor", "scalar", "gpsimd", "sync"):
            p.op(eng, None, waits=cbar)

        self.p0_prep()
        for s_ in range(NSEQ):
            self.p1(s_)
            if self.stop_after == 1:
                continue
            self.p2(s_)
            if self.stop_after == 2:
                continue
            self.p3(s_)
        p.op("sync", None, waits=p.w("st", "dmax"))

        with nc.Block() as block:
            p.emit(block)
        stack.close()
        return nc

    def p0_prep(self):
        p, nc = self.p, self.nc
        self._p0stack = ExitStack()
        self.prow24 = self._p0stack.enter_context(nc.sbuf_tensor("s_prow24", [24, 128], F32)).ap()
        self.prow32 = self._p0stack.enter_context(nc.sbuf_tensor("s_prow32", [32, 128], F32)).ap()
        for dst, src in [(self.id128, self.auxin["id128_bf"]), (self.idf32, self.auxin["id128_f32"]),
                         (self.idhi, self.auxin["idhi_bf"]), (self.ones1, self.auxin["ones1_bf"]),
                         (self.onesf, self.auxin["ones_f32"]), (self.blocks2, self.auxin["blocks2"]),
                         (self.blocks2T, self.auxin["blocks2T"]), (self.maskAa, self.auxin["maskA_a"]),
                         (self.maskAb, self.auxin["maskA_b"])]:
            p.op("sync", lambda E, d=dst, s=src: E.dma_start(out=d, in_=s), inc="dmax", incn=16)
        for dst, src in [(self.b2_bf, self.win["L0_ffn_b2"]),
                         (self.boutA_row, self.win["L0_attn_b_out"].rearrange("(o d) -> o d", o=1)),
                         (self.boB_row, self.win["L0_ffn_b_o"].rearrange("(o d) -> o d", o=1)),
                         (self.bvB_row, self.win["L0_ffn_b_qkv"].rearrange("(o d) -> o d", o=1)[:, 2048:3072]),
                         (self.gateb_row, self.win["L0_ffn_gate_b"].rearrange("(o d) -> o d", o=1)),
                         (self.gatew_bf.rearrange("p (dc e) -> p dc e", e=8),
                          self.win["L0_ffn_gate_w"].rearrange("(dc p) e -> p dc e", p=128))]:
            p.op("gpsimd", lambda E, d=dst, s=src: E.dma_start(out=d, in_=s,
                                                               allow_slow_non_contiguous=True),
                 inc="dmaw", incn=16)
        p.op("sync", lambda E: E.dma_start(
            out=self.gatew_f32.rearrange("p (dc e) -> p dc e", e=8),
            in_=self.win["L0_ffn_gate_w"].rearrange("(dc p) e -> p dc e", p=128),
            allow_slow_non_contiguous=True), inc="dmax", incn=16)
        p.op("vector", lambda E: E.tensor_tensor(self.gatew_lo, self.gatew_f32,
                                                 self.gatew_bf, ALU.subtract),
             waits=p.w("dmax", "dmaw"), inc="dve")

        p.op("sync", lambda E: E.dma_start(
            out=self.prow24, in_=self.win["L0_attn_b_qkv"].rearrange("(c k) -> c k", k=128)),
             inc="dmax", incn=16)
        p.op("tensor", lambda E: E.transpose(self.pb[1][:, 0:24], self.prow24, self.idf32[0:24, 0:24]),
             waits=p.w("dmax"), inc="pe")
        p.op("scalar", lambda E: E.copy(self.bA_c, self.pb[1][:, 0:24]), waits=p.w("pe"), inc="act")
        for i, nm in enumerate(["L0_ffn_ln1_g", "L0_ffn_ln1_b", "L0_ffn_ln2_g", "L0_ffn_ln2_b"]):
            p.op("sync", lambda E, i=i, nm=nm: E.dma_start(
                out=self.prow32[i * 8:(i + 1) * 8, :],
                in_=self.win[nm].rearrange("(c k) -> c k", k=128)), inc="dmax", incn=16)
        p.op("tensor", lambda E: E.transpose(self.pb[1][:, 32:64], self.prow32, self.idf32[0:32, 0:32]),
             waits=p.w("dmax", "act"), inc="pe")
        p.op("scalar", lambda E: E.copy(self.lnp_c, self.pb[1][:, 32:64]), waits=p.w("pe"), inc="act")
        p.op("sync", lambda E: E.dma_start(
            out=self.prow24[0:16, :],
            in_=self.win["L0_ffn_b_qkv"].rearrange("(c k) -> c k", k=128)[0:16]),
             waits=p.w("act"), inc="dmax", incn=16)
        p.op("tensor", lambda E: E.transpose(self.pb[1][:, 64:80], self.prow24[0:16, :],
                                             self.idf32[0:16, 0:16]),
             waits=p.w("dmax", "act"), inc="pe")
        p.op("scalar", lambda E: E.copy(self.bB_c, self.pb[1][:, 64:80]), waits=p.w("pe"), inc="act")
        for e in range(E):
            p.op("sync", lambda E_, e=e: E_.dma_start(
                out=self.prow32, in_=self.win["L0_ffn_b1"][e].rearrange("(c k) -> c k", k=128)),
                 waits=p.w("act"), inc="dmax", incn=16)
            p.op("tensor", lambda E_: E_.transpose(self.pb[1][:, 96:128], self.prow32,
                                                   self.idf32[0:32, 0:32]),
                 waits=p.w("dmax", "act"), inc="pe")
            p.op("scalar", lambda E_, e=e: E_.copy(self.b1_c[:, e * 32:(e + 1) * 32],
                                                   self.pb[1][:, 96:128]),
                 waits=p.w("pe"), inc="act")

        with ExitStack() as ps:
            wstg = ps.enter_context(self.nc.sbuf_tensor("p0wstg", [128, 1024], BF16)).ap()
            wstg2 = ps.enter_context(self.nc.sbuf_tensor("p0wstg2", [128, 1024], BF16)).ap()

            for (wsrc, wdst, nog) in [(self.win["L0_attn_w_qkv"], self.waT, 24),
                                      (self.win["L0_attn_w_out"], self.woaT, 8)]:
                for og in range(nog):
                    p.op("gpsimd", lambda E, s=wsrc[og * 128:(og + 1) * 128, :]:
                         E.dma_start(out=wstg, in_=s),
                         waits=p.w("pe"), inc="dmaw", incn=16)
                    for dc in range(NDC):
                        def fn(Eng, dc=dc):
                            return Eng.transpose(self.pb_bf[0][:, dc * 128:(dc + 1) * 128],
                                                 wstg[:, dc * 128:(dc + 1) * 128], self.id128)
                        p.op("tensor", fn,
                             waits=p.w("dmaw", "act") if dc == 0 else (),
                             inc="pe" if dc == NDC - 1 else None)
                    p.op("scalar", lambda E: E.copy(wstg2, self.pb_bf[0][:]),
                         waits=p.w("pe", "dmax"), inc="act")
                    p.op("sync", lambda E, og=og, wdst=wdst: E.dma_start(
                        out=wdst.rearrange("(dc p) o -> p dc o", p=128)[:, :, og * 128:(og + 1) * 128],
                        in_=wstg2.rearrange("p (dc o) -> p dc o", o=128)),
                         waits=p.w("act"), inc="dmax", incn=16)
            p.op("sync", None, waits=p.w("dmax"))
            p.op("gpsimd", None, waits=p.w("dmax"))
        self._p0stack.close()

    def p1(self, s):
        p, nc = self.p, self.nc
        KAW = 1048
        VFW = 1160
        with ExitStack() as ps:
            sb = lambda name, shape, dt: ps.enter_context(
                nc.sbuf_tensor(f"{name}_s{s}", shape, dt)).ap()
            A_fm = sb("A_fm", [128, NDC * 2048], BF16)
            KA = sb("KA", [128, 8 * 2 * KAW], BF16)
            VA = sb("VA", [128, 16 * 2 * 9 * 65], BF16)
            WQA = sb("WQA", [128, 8192], BF16)
            WOA = sb("WOA", [128, 8192], BF16)
            WBUF = [sb(f"WBUF{i}", [128, 1024], BF16) for i in range(2)]
            VSTG = sb("VSTG", [128, 2 * VFW], BF16)
            K2 = sb("K2", [128, 512], F32)
            FAC = sb("FAC", [2, 512], F32)
            QBUF = sb("QBUF", [128, 1024], BF16)
            Q2ALL = sb("Q2ALL", [128, 1024], F32)
            QFA = Q2ALL[0:2, :]
            PTA = [sb(f"PTA{i}", [128, 128], BF16) for i in range(2)]
            PTB = [sb(f"PTB{i}", [16, 128], BF16) for i in range(2)]
            OTM2 = [sb(f"OTM{i}", [128, 1024], BF16) for i in range(2)]
            OFM = sb("OFM", [128, 1024], BF16)
            ASTG = sb("ASTG", [128, 1024], F32)
            X1STG = [sb(f"X1STG{i}", [128, 1024], F32) for i in range(2)]
            RD = [sb(f"RD{i}", [128, 1], F32) for i in range(2)]

            kaV = KA.rearrange("p (oc par c) -> p oc par c", par=2, c=KAW)
            vaV = VA.rearrange("p (t c) -> p t c", c=65)
            afmV = A_fm.rearrange("p (dc t) -> p dc t", t=2048)
            wqaV = WQA.rearrange("p (dc o) -> p dc o", o=1024)
            woaV = WOA.rearrange("p (dc o) -> p dc o", o=1024)

            p.op("gpsimd", lambda E: E.dma_start(
                out=wqaV, in_=self.waT.rearrange("(dc p) o -> p dc o", p=128)[:, :, 0:1024]),
                 waits=p.w("dmax"), inc="dmaw", incn=16)
            p.op("gpsimd", lambda E: E.dma_start(
                out=woaV, in_=self.woaT.rearrange("(dc p) o -> p dc o", p=128)),
                 inc="dmaw", incn=16)

            consumed = {}
            xv = {}
            for t in range(NT):
                wts = [("dve", consumed[t - 2])] if t - 2 in consumed else ()
                xv[t] = p.op("sync", lambda E, t=t, b=self.XSTG[t % 2]:
                             E.dma_start(out=b, in_=self.x_in[s, t]),
                             waits=wts, inc=f"dx{t % 2}", incn=16)
                buf = self.XSTG[t % 2]
                m, rstd = self.ln_stats(buf, waits_sc=[(f"dx{t % 2}", xv[t])],
                                        waits_ve=[(f"dx{t % 2}", xv[t])])
                consumed[t] = p.op("vector", lambda E, b=buf, m=m, r=rstd:
                                   E.tensor_scalar(self.ABF, b, m, r, ALU.subtract, ALU.mult),
                                   waits=p.w("act"), inc="dve")
                if self.dbg_stage == 11:
                    p.op("vector", lambda E: E.tensor_copy(self.SQT, self.ABF), inc="dve")
                    p.op("sync", lambda E, t=t: E.dma_start(out=self.dbg[s, t], in_=self.SQT),
                         waits=p.w("dve"), inc="st", incn=16)
                    p.op("vector", None, waits=p.w("st"))
                self.tile_to_fm(self.ABF, 0, afmV[:, :, t * 128:(t + 1) * 128],
                                first_waits=p.w("dve", "act"))

            p.op("vector", lambda E: E.memset(kaV[:, :, :, 0:8], 0.0), inc="dve")
            p.op("vector", lambda E: E.memset(kaV[:, :, :, 8 + 1024:KAW], 0.0), inc="dve")
            p.op("vector", lambda E: E.memset(vaV[:, :, 64:65], 1.0), inc="dve")

            self.p1_evac = {}
            for oc in range(16):
                og = 8 + oc
                wbuf = WBUF[oc % 2]
                ld = p.op("gpsimd", lambda E, og=og, w=wbuf: E.dma_start(
                    out=w.rearrange("p (dc o) -> p dc o", o=128),
                    in_=self.waT.rearrange("(dc p) o -> p dc o", p=128)[:, :, og * 128:(og + 1) * 128]),
                     waits=[("pe", p.cnt["pe"])], inc=f"dw{oc % 2}", incn=16)
                is_v = oc >= 8
                if is_v:
                    p.op("vector", lambda E: E.memset(VSTG, 0.0), waits=p.w("pe"), inc="dve")
                for tc in range(4):
                    bank = 1 + (tc % 2)
                    pairs = [(wbuf.rearrange("p (dc o) -> p dc o", o=128)[:, dc, :],
                              afmV[:, dc, tc * 512:(tc + 1) * 512]) for dc in range(NDC)]
                    key = (oc, tc - 2) if tc >= 2 else (oc - 1, tc + 2)
                    wts = [(f"dw{oc % 2}", ld)]
                    if key in self.p1_evac:
                        wts.append(("act", self.p1_evac[key]))
                    else:
                        wts += p.w("act", "dve")
                    self.mm_group(self.pb[bank], pairs, first_waits=wts)
                    psum = self.pb[bank]
                    bias = self.bA_c[:, og:og + 1]
                    if not is_v:
                        p.op("scalar", lambda E, ps_=psum, b=bias: E.activation(
                            K2, ps_, AF.Square, bias=b), waits=p.w("pe"), inc="act")
                        kslc = kaV[:, oc, :, 8 + tc * 256: 8 + (tc + 1) * 256]
                        pv = psum.rearrange("p (j par) -> p par j", par=2)
                        self.p1_evac[(oc, tc)] = p.op("scalar", lambda E, o=kslc, i=pv, b=bias:
                                                      E.activation(o, i, AF.Identity, bias=b),
                                                      inc="act")
                        p.op("tensor", lambda E: E.matmul(self.pb[3][0:2, :], self.blocks2, K2,
                                                          start=True, stop=True,
                                                          skip_group_check=True),
                             waits=p.w("act"), inc="pe")
                        p.op("scalar", lambda E: E.activation(FAC, self.pb[3][0:2, :], AF.Ln),
                             waits=p.w("pe"), inc="act")
                        p.op("scalar", lambda E: E.activation(FAC, FAC, AF.Exp, scale=0.5),
                             waits=p.w("act"), inc="act")
                        p.op("scalar", lambda E: E.activation(FAC, FAC, AF.Ln, bias=QK_EPS),
                             waits=p.w("act"), inc="act")
                        p.op("scalar", lambda E: E.activation(FAC, FAC, AF.Exp, scale=-1.0),
                             waits=p.w("act"), inc="act")
                        p.op("tensor", lambda E: E.matmul(self.pb[4], self.blocks2T, FAC,
                                                          start=True, stop=True,
                                                          skip_group_check=True),
                             waits=p.w("act", "dve"), inc="pe")
                        pbv = self.pb[4].rearrange("p (j par) -> p par j", par=2)
                        p.op("vector", lambda E, o=kslc, i1=pbv: E.tensor_tensor(o, o, i1, ALU.mult),
                             waits=p.w("pe"), inc="dve")
                    else:
                        vslc = VSTG.rearrange("p (par c) -> p par c", par=2)[
                            :, :, 8 + tc * 256: 8 + (tc + 1) * 256]
                        pv = psum.rearrange("p (j par) -> p par j", par=2)
                        self.p1_evac[(oc, tc)] = p.op("scalar", lambda E, o=vslc, i=pv, b=bias:
                                                      E.activation(o, i, AF.Identity, bias=b),
                                                      waits=p.w("pe"), inc="act")
                if is_v:
                    hpair = oc - 8
                    for hh in range(2):
                        h = 2 * hpair + hh
                        base = 64 * hh
                        ident = self.id128[0:64, 0:64] if hh == 0 else self.idhi[64:128, :]
                        for par in range(2):
                            for r in range(9):
                                ti = (h * 2 + par) * 9 + r
                                src = VSTG[base:base + 64,
                                           par * VFW + r * 128: par * VFW + (r + 1) * 128]
                                nb = (par * 9 + r) % 2
                                p.op("tensor", lambda E, o=self.pb_bf[5 + nb][:, 0:64], i=src,
                                     idt=ident: E.transpose(o, i, idt),
                                     waits=p.w("act", "dve"), inc="pe")
                                p.op("scalar", lambda E, o=vaV[:, ti, 0:64],
                                     i=self.pb_bf[5 + nb][:, 0:64]: E.copy(o, i),
                                     waits=p.w("pe"), inc="act")

            self.p1_astg_evac = 0
            self.p1_ttd = 0
            self.p1_tev = 0
            x1w = {}
            for par in range(2):
                for r in range(8):
                    xsrc = self.x_in[s].rearrange("t p d -> (t p) d")[par::2].rearrange(
                        "(r p) d -> r p d", p=128)[r]
                    xbuf = self.XSTG[(par * 8 + r) % 2]
                    px = (par * 8 + r) % 2
                    xv1 = p.op("sync", lambda E, o=xbuf, i=xsrc: E.dma_start(out=o, in_=i),
                               waits=p.w("dve"), inc=f"dx{px}", incn=16)
                    qev = {}
                    for oc in range(8):
                        bank = 1 + (oc % 2)
                        if oc >= 2:
                            wts = [("act", qev[oc - 2])]
                        else:
                            wts = [("act", self.p1_astg_evac)] if self.p1_astg_evac else \
                                  p.w("act", "dmaw", "dve")
                        pairs = [(wqaV[:, dc, oc * 128:(oc + 1) * 128],
                                  afmV[:, dc, 256 * r + par: 256 * r + par + 255: 2])
                                 for dc in range(NDC)]
                        g = self.mm_group(self.pb[bank][:, 0:128], pairs, first_waits=wts)
                        bias = self.bA_c[:, oc:oc + 1]
                        p.op("scalar", lambda E, oc=oc, b=bias, bank=bank: E.activation(
                            QBUF[:, oc * 128:(oc + 1) * 128], self.pb[bank][:, 0:128],
                            AF.Identity, bias=b), waits=[("pe", g)], inc="act")
                        qev[oc] = p.op("scalar", lambda E, oc=oc, b=bias, bank=bank: E.activation(
                            Q2ALL[:, oc * 128:(oc + 1) * 128], self.pb[bank][:, 0:128],
                            AF.Square, bias=b), inc="act")
                    ssq = []
                    for hf in range(2):
                        bank = 1 + hf
                        ssq.append(p.op("tensor", lambda E, hf=hf, bank=bank: E.matmul(
                            self.pb[bank][0:2, :], self.blocks2, Q2ALL[:, hf * 512:(hf + 1) * 512],
                            start=True, stop=True, skip_group_check=True),
                             waits=[("act", qev[7])], inc="pe"))
                    for hf in range(2):
                        p.op("scalar", lambda E, hf=hf, bank=1 + hf: E.copy(
                            QFA[:, hf * 512:(hf + 1) * 512], self.pb[bank][0:2, :]),
                             waits=[("pe", ssq[1])], inc="act")
                    p.op("scalar", lambda E: E.activation(QFA, QFA, AF.Ln), waits=p.w("act"),
                         inc="act")
                    p.op("scalar", lambda E: E.activation(QFA, QFA, AF.Exp, scale=0.5),
                         waits=p.w("act"), inc="act")
                    p.op("scalar", lambda E: E.activation(QFA, QFA, AF.Ln, bias=QK_EPS),
                         waits=p.w("act"), inc="act")
                    fv = p.op("scalar", lambda E: E.activation(QFA, QFA, AF.Exp, scale=-1.0),
                              waits=p.w("act"), inc="act")
                    qmul = {}
                    for hf in range(2):
                        bank = 1 + hf
                        g3 = p.op("tensor", lambda E, hf=hf, bank=bank: E.matmul(
                            self.pb[bank], self.blocks2T, QFA[:, hf * 512:(hf + 1) * 512],
                            start=True, stop=True, skip_group_check=True),
                             waits=[("act", fv)], inc="pe")
                        qmul[hf] = p.op("vector", lambda E, hf=hf, bank=bank: E.tensor_tensor(
                            QBUF[:, hf * 512:(hf + 1) * 512], QBUF[:, hf * 512:(hf + 1) * 512],
                            self.pb[bank], ALU.mult), waits=[("pe", g3)], inc="dve")
                    mv = 0 if r == 0 else (2 if r == 7 else 1)
                    bs = {0: (5, 6, 7), 1: (0, 3, 4)}
                    sdn = {}
                    edn = {}
                    mdn = {}
                    pvd = {}
                    odn = {}

                    def emit_S(h):
                        oc, hh = h // 2, h % 2
                        base = 64 * hh
                        bA, bB, bO = bs[h % 2]
                        kwin = kaV[base:base + 64, oc, par, r * 128: r * 128 + 144]
                        q_ap = QBUF[base:base + 64, oc * 128:(oc + 1) * 128]
                        wts = ([("act", edn[h - 2])] if h - 2 in edn
                               else [("dve", qmul[1]), ("act", self.p1_tev)])
                        p.op("tensor", lambda E, k=kwin[:, 0:128], q=q_ap, bA=bA: E.matmul(
                            self.pb[bA][:, 0:128], k, q, start=True, stop=True,
                            skip_group_check=True), waits=wts, inc="pe")
                        sdn[h] = p.op("tensor", lambda E, k=kwin[:, 128:144], q=q_ap, bB=bB:
                                      E.matmul(self.pb[bB][0:16, 0:128], k, q, start=True,
                                               stop=True, skip_group_check=True), inc="pe")

                    def emit_tail(h):
                        bA, bB, bO = bs[h % 2]
                        wts = [("pe", pvd[h - 2] if h - 2 in pvd else sdn[h])]
                        if h - 2 in odn:
                            wts.append(("dve", mdn[h - 2]))
                        p.op("scalar", lambda E, bA=bA, ptab=PTA[h % 2]: E.activation(
                            ptab, self.pb[bA][:, 0:128], AF.Exp),
                             waits=wts + ([("act", odn[h - 2])] if h - 2 in odn else []),
                             inc="act")
                        ev = p.op("scalar", lambda E, bB=bB, ptbb=PTB[h % 2]: E.activation(
                            ptbb, self.pb[bB][0:16, 0:128], AF.Exp), inc="act")
                        edn[h] = ev
                        p.op("vector", lambda E, mv=mv, ptab=PTA[h % 2]: E.tensor_tensor(
                            ptab, ptab, self.maskAa[:, mv * 128:(mv + 1) * 128], ALU.mult),
                             waits=[("act", ev)], inc="dve")
                        mdn[h] = p.op("vector", lambda E, mv=mv, ptbb=PTB[h % 2]: E.tensor_tensor(
                            ptbb, ptbb, self.maskAb[:, mv * 128:(mv + 1) * 128], ALU.mult),
                             inc="dve")
                        ti = (h * 2 + par) * 9 + r
                        wpv = [("dve", mdn[h])]
                        if h - 2 in pvd:
                            wpv.append(("dve", odn_rd[h - 2]))
                        p.op("tensor", lambda E, ti=ti, bO=bO, ptab=PTA[h % 2]: E.matmul(
                            self.pb[bO][:, 0:65], ptab, vaV[:, ti, :], start=True, stop=False,
                            skip_group_check=True),
                             waits=wpv + ([("act", odn[h - 2])] if h - 2 in odn else []))
                        pvd[h] = p.op("tensor", lambda E, ti=ti, bO=bO, ptbb=PTB[h % 2]: E.matmul(
                            self.pb[bO][:, 0:65], ptbb, vaV[0:16, ti + 1, :], start=False,
                            stop=True, skip_group_check=True), inc="pe")
                        if self.dbg_stage == 15 and par == 0 and r == 0 and h == 0:
                            p.op("scalar", lambda E, bO=bO: E.copy(self.SQT[:, 0:65],
                                                                   self.pb[bO][:, 0:65]),
                                 waits=[("pe", pvd[h])], inc="act")
                            p.op("sync", lambda E: E.dma_start(out=self.dbg[s, 8], in_=self.SQT),
                                 waits=p.w("act"), inc="st", incn=16)
                            p.op("scalar", None, waits=p.w("st"))
                        odn_rd[h] = p.op("vector", lambda E, bO=bO, rd=RD[h % 2]: E.reciprocal(
                            rd, self.pb[bO][:, 64:65]), waits=[("pe", pvd[h])], inc="dve")
                        wo = [("dve", odn_rd[h])]
                        if h == 0 and self.p1_ttd:
                            wo.append(("pe", self.p1_ttd))
                        odn[h] = p.op("scalar", lambda E, h=h, bO=bO, rd=RD[h % 2],
                                      otm=OTM2[(par * 8 + r) % 2]:
                                      E.activation(otm[:, h * 64:(h + 1) * 64],
                                                   self.pb[bO][:, 0:64], AF.Identity, scale=rd),
                                      waits=wo, inc="act")

                    odn_rd = {}
                    emit_S(0)
                    emit_S(1)
                    for h in range(H):
                        emit_tail(h)
                        if h + 2 < H:
                            emit_S(h + 2)
                    if self.dbg_stage == 16 and par == 0 and r == 0:
                        p.op("vector", lambda E, otm=OTM2[0]: E.tensor_copy(self.SQT, otm),
                             waits=[("act", odn[H - 1])], inc="dve")
                        p.op("sync", lambda E: E.dma_start(out=self.dbg[s, 9], in_=self.SQT),
                             waits=p.w("dve"), inc="st", incn=16)
                        p.op("vector", None, waits=p.w("st"))
                    self.tile_to_fm(OTM2[(par * 8 + r) % 2], 7,
                                    OFM.rearrange("p (dc c) -> p dc c", c=128),
                                    first_waits=[("act", odn[H - 1])])
                    self.p1_ttd = p.cnt["pe"]
                    self.p1_tev = p.cnt["act"]
                    ofmV = OFM.rearrange("p (dc o) -> p dc o", o=128)
                    for nh in range(2):
                        bank = 1 + nh
                        pairs = [(ofmV[:, dc, :], woaV[:, dc, nh * 512:(nh + 1) * 512])
                                 for dc in range(NDC)]
                        self.mm_group(self.pb[bank], pairs, first_waits=p.w("act", "dmaw"))
                        p.op("tensor", lambda E, nh=nh, bank=bank: E.matmul(
                            self.pb[bank], self.ones1, self.boutA_row[:, nh * 512:(nh + 1) * 512],
                            start=False, stop=True, skip_group_check=True), inc="pe")
                        self.p1_astg_evac = p.op("scalar", lambda E, nh=nh, bank=bank: E.copy(
                            ASTG[:, nh * 512:(nh + 1) * 512], self.pb[bank]),
                             waits=p.w("pe", "dve"), inc="act")
                    if self.dbg_stage == 13:
                        ddst = self.dbg[s].rearrange("t p d -> (t p) d")[par::2].rearrange(
                            "(r p) d -> r p d", p=128)[r]
                        p.op("sync", lambda E, o=ddst: E.dma_start(out=o, in_=ASTG),
                             waits=p.w("act"), inc="st", incn=16)
                        p.op("scalar", None, waits=p.w("st"))
                    m, rstd = self.ln_stats(ASTG, waits_sc=p.w("act"), waits_ve=p.w("act"))
                    x1b = X1STG[(par * 8 + r) % 2]
                    p.op("vector", lambda E, m=m, rs=rstd: E.tensor_scalar(
                        self.SQT, ASTG, m, rs, ALU.subtract, ALU.mult), waits=p.w("act"))
                    ti_ = par * 8 + r
                    wadd = [(f"dx{px}", xv1)]
                    if ti_ - 2 in x1w:
                        wadd.append(("dmax", x1w[ti_ - 2]))
                    p.op("vector", lambda E, o=x1b, xb=xbuf: E.tensor_tensor(o, xb, self.SQT,
                                                                             ALU.add),
                         waits=wadd, inc="dve")
                    xdst = self.x1scr[s].rearrange("t p d -> (t p) d")[par::2].rearrange(
                        "(r p) d -> r p d", p=128)[r]
                    x1w[ti_] = p.op("sync", lambda E, o=xdst, i=x1b: E.dma_start(out=o, in_=i),
                         waits=p.w("dve"), inc="dmax", incn=16)
                    if self.dbg_stage == 1:
                        ddst = self.dbg[s].rearrange("t p d -> (t p) d")[par::2].rearrange(
                            "(r p) d -> r p d", p=128)[r]
                        p.op("sync", lambda E, o=ddst, i=x1b: E.dma_start(out=o, in_=i),
                             inc="st", incn=16)
            p.op("sync", None, waits=p.w("dmax"))
            p.op("gpsimd", None, waits=p.w("dmax", "dve", "act", "pe"))

    def p2(self, s):
        p, nc = self.p, self.nc
        with ExitStack() as ps:
            sb = lambda name, shape, dt: ps.enter_context(
                nc.sbuf_tensor(f"{name}_s{s}", shape, dt)).ap()
            HFM = sb("HFM", [128, NDC * 2048], BF16)
            KBc = sb("KBc", [128, 8 * 2048], BF16)
            VBc = sb("VBc", [128, 16 * 16 * 65], BF16)
            WB2 = [sb(f"WB2{i}", [128, 4096], BF16) for i in range(2)]
            WOB = sb("WOB", [128, 8192], BF16)
            QB = sb("QB", [128, 256], BF16)
            PT2 = [sb(f"PT2{i}", [128, 256], BF16) for i in range(2)]
            OFM2 = sb("OFM2", [128, 2048], BF16)
            BCS = sb("BCS", [64, 256], F32)
            DSC = sb("DSC", [128, 256], F32)
            hfmV = HFM.rearrange("p (dc t) -> p dc t", t=2048)
            kbV = KBc.rearrange("p (oc t) -> p oc t", t=2048)
            vbV = VBc.rearrange("p (h tt c) -> p h tt c", tt=16, c=65)
            wobV = WOB.rearrange("p (dc o) -> p dc o", o=1024)
            wqkvB = self.win["L0_ffn_w_qkv"]

            p.op("vector", lambda E: E.memset(vbV[:, :, :, 64:65], 1.0), inc="dve")
            p.op("gpsimd", lambda E: E.dma_start(
                out=wobV, in_=self.win["L0_ffn_w_o"].rearrange("(dc p) o -> p dc o", p=128)),
                 inc="dmaw", incn=16)

            self.p2_uw = {}
            consumed = {}
            for t in range(NT):
                wts = ([("dve", consumed[t - 2])] if t - 2 in consumed else p.w("dmax"))
                xv = p.op("sync", lambda E, o=self.XSTG[t % 2], i=self.x1scr[s, t]:
                          E.dma_start(out=o, in_=i), waits=wts, inc=f"dx{t % 2}", incn=16)
                xbuf = self.XSTG[t % 2]
                m, rstd = self.ln_stats(xbuf, waits_sc=[(f"dx{t % 2}", xv)],
                                        waits_ve=[(f"dx{t % 2}", xv)])
                ubuf = self.USTG[t % 2]
                wnu = p.w("act")
                if t - 2 in self.p2_uw:
                    wnu = wnu + [("dmax", self.p2_uw[t - 2])]
                p.op("vector", lambda E, b=xbuf, m=m, rs=rstd, u=ubuf: E.tensor_scalar(
                    u, b, m, rs, ALU.subtract, ALU.mult), waits=wnu, inc="dve")
                self.p2_uw[t] = p.op("sync", lambda E, u=ubuf, t=t: E.dma_start(
                    out=self.uscr[s, t], in_=u), waits=p.w("dve"), inc="dmax", incn=16)
                if self.dbg_stage == 21:
                    p.op("sync", lambda E, u=ubuf, t=t: E.dma_start(out=self.dbg[s, t], in_=u),
                         inc="st", incn=16)
                    p.op("vector", None, waits=p.w("st"))
                m2, rstd2 = self.ln_stats(ubuf, waits_sc=(), waits_ve=())
                consumed[t] = p.op("vector", lambda E, u=ubuf, m=m2, rs=rstd2: E.tensor_scalar(
                    self.ABF, u, m, rs, ALU.subtract, ALU.mult), waits=p.w("act"), inc="dve")
                scale = [self.lnp_c[:, dc:dc + 1] for dc in range(8)]
                bias = [self.lnp_c[:, 8 + dc:8 + dc + 1] for dc in range(8)]
                self.tile_to_fm(self.ABF, 0,
                                [hfmV[:, dc, t * 128:(t + 1) * 128] for dc in range(NDC)],
                                scale=scale, bias=bias, first_waits=p.w("dve", "act"))

            if self.p2_sub == 1:
                p.op("sync", None, waits=p.w("dmax"))
                p.op("gpsimd", None, waits=p.w("dmax", "dve", "act", "pe"))
                return
            for oc in range(8):
                wbuf = WB2[oc % 2]
                ld = p.op("gpsimd", lambda E, oc=oc, w=wbuf: E.dma_start(
                    out=w[:, 0:1024].rearrange("p (dc o) -> p dc o", o=128),
                    in_=wqkvB.rearrange("(dc p) o -> p dc o", p=128)[
                        :, :, 1024 + oc * 128:1024 + (oc + 1) * 128]),
                     waits=[("pe", p.cnt["pe"])], inc=f"dw{oc % 2}", incn=16)
                for tc in range(4):
                    bank = 1 + (tc % 2)
                    pairs = [(wbuf[:, 0:1024].rearrange("p (dc o) -> p dc o", o=128)[:, dc, :],
                              hfmV[:, dc, tc * 512:(tc + 1) * 512]) for dc in range(NDC)]
                    self.mm_group(self.pb[bank], pairs,
                                  first_waits=[(f"dw{oc % 2}", ld)] + p.w("act"))
                    p.op("scalar", lambda E, oc=oc, tc=tc, bank=bank: E.activation(
                        kbV[:, oc, tc * 512:(tc + 1) * 512], self.pb[bank], AF.Identity,
                        bias=self.bB_c[:, 8 + oc:8 + oc + 1]), waits=p.w("pe"), inc="act")
            for vp in range(2):
                wbuf = WB2[vp % 2]
                ld = p.op("gpsimd", lambda E, vp=vp, w=wbuf: E.dma_start(
                    out=w.rearrange("p (dc o) -> p dc o", o=512),
                    in_=wqkvB.rearrange("(dc p) o -> p dc o", p=128)[
                        :, :, 2048 + vp * 512:2048 + (vp + 1) * 512]),
                     waits=[("pe", p.cnt["pe"])], inc=f"dw{vp % 2}", incn=16)
                for t in range(NT):
                    bank = 1 + (t % 2)
                    pairs = [(hfmV[:, dc, t * 128:(t + 1) * 128],
                              wbuf.rearrange("p (dc o) -> p dc o", o=512)[:, dc, :])
                             for dc in range(NDC)]
                    self.mm_group(self.pb[bank], pairs,
                                  first_waits=[(f"dw{vp % 2}", ld)] + p.w("act"))
                    p.op("tensor", lambda E, vp=vp, bank=bank: E.matmul(
                        self.pb[bank], self.ones1, self.bvB_row[:, vp * 512:(vp + 1) * 512],
                        start=False, stop=True, skip_group_check=True), inc="pe")
                    dst = vbV[:, vp * 8:(vp + 1) * 8, t, 0:64]
                    src = self.pb[bank].rearrange("p (h c) -> p h c", c=64)
                    p.op("scalar", lambda E, o=dst, i=src: E.copy(o, i), waits=p.w("pe"),
                         inc="act")

            if self.dbg_stage == 28:
                for i in range(2):
                    p.op("vector", lambda E, i=i: E.tensor_copy(
                        self.SQT, hfmV[:, 0, i * 1024:(i + 1) * 1024]),
                         waits=p.w("act"), inc="dve")
                    p.op("sync", lambda E, i=i: E.dma_start(out=self.dbg[s, i], in_=self.SQT),
                         waits=p.w("dve"), inc="st", incn=16)
                    p.op("vector", None, waits=p.w("st"))
                    p.op("vector", lambda E, i=i: E.tensor_copy(
                        self.SQT, kbV[:, 0, i * 1024:(i + 1) * 1024]),
                         inc="dve")
                    p.op("sync", lambda E, i=i: E.dma_start(out=self.dbg[s, 2 + i], in_=self.SQT),
                         waits=p.w("dve"), inc="st", incn=16)
                    p.op("vector", None, waits=p.w("st"))
            if self.p2_sub == 2:
                p.op("sync", None, waits=p.w("dmax"))
                p.op("gpsimd", None, waits=p.w("dmax", "dve", "act", "pe"))
                return
            for qg in range(8):
                ts0 = qg * 2
                uv = []
                for i in range(2):
                    uv.append(p.op("sync", lambda E, i=i, ts0=ts0: E.dma_start(
                        out=self.USTG[i], in_=self.uscr[s, ts0 + i]),
                         waits=p.w("dve", "dmax"), inc=f"dx{i}", incn=16))
                for oc in range(8):
                    wbuf = WB2[oc % 2]
                    ld = p.op("gpsimd", lambda E, oc=oc, w=wbuf: E.dma_start(
                        out=w[:, 0:1024].rearrange("p (dc o) -> p dc o", o=128),
                        in_=wqkvB.rearrange("(dc p) o -> p dc o", p=128)[
                            :, :, oc * 128:(oc + 1) * 128]),
                         waits=[("pe", p.cnt["pe"])], inc=f"dw{oc % 2}", incn=16)
                    pairs = [(wbuf[:, 0:1024].rearrange("p (dc o) -> p dc o", o=128)[:, dc, :],
                              hfmV[:, dc, qg * 256:(qg + 1) * 256]) for dc in range(NDC)]
                    self.mm_group(self.pb[5][:, 0:256], pairs,
                                  first_waits=[(f"dw{oc % 2}", ld)] + p.w("dve", "act"))
                    p.op("scalar", lambda E, oc=oc: E.activation(
                        QB, self.pb[5][:, 0:256], AF.Identity, bias=self.bB_c[:, oc:oc + 1]),
                         waits=p.w("pe"), inc="act")
                    if self.dbg_stage == 28 and qg == 0 and oc == 0:
                        p.op("vector", lambda E: E.tensor_copy(self.SQT[:, 0:256], QB),
                             waits=p.w("act"), inc="dve")
                        p.op("sync", lambda E: E.dma_start(out=self.dbg[s, 4], in_=self.SQT),
                             waits=p.w("dve"), inc="st", incn=16)
                        p.op("vector", None, waits=p.w("st"))
                    if self.p2_sub == 3:
                        continue
                    for hh in range(2):
                        h = 2 * oc + hh
                        base = 64 * hh
                        s_done = {}
                        exp_done = {}
                        pv_done = {}

                        def emit_S(kt):
                            bank = 1 + (kt % 2)
                            wts = ([("act", exp_done[kt - 2])] if kt - 2 in exp_done
                                   else p.w("act", "dve"))
                            s_done[kt] = p.op("tensor", lambda E, kt=kt, bank=bank,
                                              base=base, oc=oc: E.matmul(
                                self.pb[bank][:, 0:256],
                                kbV[base:base + 64, oc, kt * 128:(kt + 1) * 128],
                                QB[base:base + 64, :], start=True, stop=True,
                                skip_group_check=True), waits=wts, inc="pe")

                        emit_S(0)
                        emit_S(1)
                        for kt in range(16):
                            bank = 1 + (kt % 2)
                            ptb = PT2[kt % 2]
                            if self.dbg_stage == 30 and qg == 0 and oc == 0 and hh == 0 and kt == 0:
                                p.op("scalar", lambda E, bank=bank: E.copy(
                                    self.SQT[:, 0:256], self.pb[bank][:, 0:256]),
                                     waits=[("pe", s_done[0])], inc="act")
                                p.op("sync", lambda E: E.dma_start(out=self.dbg[s, 6],
                                                                   in_=self.SQT),
                                     waits=p.w("act"), inc="st", incn=16)
                                p.op("scalar", None, waits=p.w("st"))
                            wexp = [("pe", pv_done[kt - 2] if kt >= 2 else s_done[kt])]
                            exp_done[kt] = p.op("scalar", lambda E, bank=bank, ptb=ptb:
                                                E.activation(ptb, self.pb[bank][:, 0:256],
                                                             AF.Exp, scale=0.125),
                                                waits=wexp, inc="act")
                            if kt + 2 < 16:
                                emit_S(kt + 2)
                            if self.dbg_stage == 29 and qg == 0 and oc == 0 and hh == 0 and kt == 0:
                                p.op("vector", lambda E, ptb=ptb: E.tensor_copy(
                                    self.SQT[:, 0:256], ptb), waits=p.w("act"), inc="dve")
                                p.op("sync", lambda E: E.dma_start(out=self.dbg[s, 5],
                                                                   in_=self.SQT),
                                     waits=p.w("dve"), inc="st", incn=16)
                                p.op("vector", None, waits=p.w("st"))
                            if self.p2_sub == 4:
                                continue
                            pv_done[kt] = p.op("tensor", lambda E, h=h, kt=kt, ptb=ptb: E.matmul(
                                self.pb[3][0:65, 0:256], vbV[:, h, kt, :],
                                ptb, start=(kt == 0), stop=(kt == 15), skip_group_check=True),
                                 waits=[("act", exp_done[kt])] + (p.w("dve") if kt == 0 else []),
                                 inc="pe")
                        if self.p2_sub == 4:
                            continue
                        if self.dbg_stage == 27 and qg == 0 and oc == 0 and hh == 0:
                            p.op("scalar", lambda E: E.copy(self.SQT[0:65, 0:256],
                                                            self.pb[3][0:65, 0:256]),
                                 waits=[("pe", pv_done[15])], inc="act")
                            p.op("sync", lambda E: E.dma_start(out=self.dbg[s, 0],
                                                               in_=self.SQT),
                                 waits=p.w("act"), inc="st", incn=16)
                            p.op("scalar", None, waits=p.w("st"))
                        p.op("scalar", lambda E: E.activation(DSC[64:65, :],
                                                              self.pb[3][64:65, 0:256], AF.Ln),
                             waits=[("pe", pv_done[15])], inc="act")
                        p.op("scalar", lambda E: E.activation(DSC[64:65, :], DSC[64:65, :],
                                                              AF.Exp, scale=-1.0),
                             waits=p.w("act"), inc="act")
                        p.op("tensor", lambda E: E.matmul(self.pb[4][0:64, 0:256],
                                                          self.onesf[64:65, :], DSC[64:65, :],
                                                          start=True, stop=True,
                                                          skip_group_check=True),
                             waits=p.w("act", "dve"), inc="pe")
                        p.op("scalar", lambda E: E.copy(BCS, self.pb[4][0:64, 0:256]),
                             waits=p.w("pe"), inc="act")
                        p.op("vector", lambda E, base=base, oc=oc: E.tensor_tensor(
                            OFM2.rearrange("p (oc t) -> p oc t", t=256)[base:base + 64, oc, :],
                            self.pb[3][0:64, 0:256], BCS, ALU.mult),
                             waits=p.w("act", "pe"), inc="dve")
                if self.dbg_stage == 26:
                    for half in range(2):
                        p.op("vector", lambda E, half=half: E.tensor_copy(
                            self.SQT, OFM2[:, half * 1024:(half + 1) * 1024]),
                             waits=p.w("dve"), inc="dve")
                        p.op("sync", lambda E, qg=qg, half=half: E.dma_start(
                            out=self.dbg[s, 2 * qg + half], in_=self.SQT),
                             waits=p.w("dve"), inc="st", incn=16)
                        p.op("vector", None, waits=p.w("st"))
                if self.p2_sub in (3, 4, 5):
                    continue
                ofuV = OFM2.rearrange("p (oc t) -> p oc t", t=256)
                for ts in range(2):
                    for nh in range(2):
                        bank = 6 + nh
                        for oc in range(8):
                            p.op("tensor", lambda E, oc=oc, ts=ts, nh=nh, bank=bank: E.matmul(
                                self.pb[bank], ofuV[:, oc, ts * 128:(ts + 1) * 128],
                                wobV[:, oc, nh * 512:(nh + 1) * 512],
                                start=(oc == 0), stop=False, skip_group_check=True),
                                 waits=p.w("dve", "dmaw") if oc == 0 else ())
                        p.op("tensor", lambda E, nh=nh, bank=bank: E.matmul(
                            self.pb[bank], self.ones1, self.boB_row[:, nh * 512:(nh + 1) * 512],
                            start=False, stop=True, skip_group_check=True), inc="pe")
                        p.op("vector", lambda E, ts=ts, nh=nh, bank=bank: E.tensor_tensor(
                            self.USTG[ts][:, nh * 512:(nh + 1) * 512],
                            self.USTG[ts][:, nh * 512:(nh + 1) * 512],
                            self.pb[bank], ALU.add),
                             waits=p.w("pe") + [(f"dx{ts}", uv[ts])], inc="dve")
                    p.op("sync", lambda E, ts=ts, ts0=ts0: E.dma_start(
                        out=self.uscr[s, ts0 + ts], in_=self.USTG[ts]),
                         waits=p.w("dve"), inc="dmax", incn=16)
                    if self.dbg_stage == 2:
                        p.op("sync", lambda E, ts=ts, ts0=ts0: E.dma_start(
                            out=self.dbg[s, ts0 + ts], in_=self.USTG[ts]),
                             inc="st", incn=16)
            p.op("sync", None, waits=p.w("dmax"))
            p.op("gpsimd", None, waits=p.w("dmax", "dve", "act", "pe"))

    def p3(self, s):
        p, nc = self.p, self.nc
        with ExitStack() as ps:
            sb = lambda name, shape, dt: ps.enter_context(
                nc.sbuf_tensor(f"{name}_s{s}", shape, dt)).ap()
            Y = sb("Y", [128, NT * 1024], F32)
            H2 = sb("H2", [128, NDC * 2048], BF16)
            W1R = sb("W1R", [128, NDC * 2048], BF16)
            W2R = sb("W2R", [128, 16 * 1024], BF16)
            HID = sb("HID", [128, 16 * 256], BF16)
            self.H2LO = HID[:, 0:1024]
            self.ABFLO = HID[:, 1024:2048]
            yV = Y.rearrange("p (t d) -> p t d", d=1024)
            h2V = H2.rearrange("p (dc t) -> p dc t", t=2048)
            w1V = W1R.rearrange("p (dc f) -> p dc f", f=2048)
            w2V = W2R.rearrange("p (fc o) -> p fc o", o=1024)
            hidV = HID.rearrange("p (fc t) -> p fc t", t=256)

            yl = p.op("sync", lambda E: E.dma_start(
                out=yV, in_=self.uscr[s].rearrange("t p d -> p t d")),
                 waits=p.w("dve"), inc="dmax", incn=16)

            for t in range(NT):
                ysl = yV[:, t, :]
                m, rstd = self.ln_stats(ysl, waits_sc=[("dmax", yl)], waits_ve=[("dmax", yl)])
                p.op("vector", lambda E, ysl=ysl, m=m, rs=rstd: E.tensor_scalar(
                    self.SQT, ysl, m, rs, ALU.subtract, ALU.mult), waits=p.w("act"), inc="dve")
                p.op("vector", lambda E: E.tensor_copy(self.ABF, self.SQT),
                     waits=p.w("dve"), inc="dve")
                p.op("vector", lambda E: E.tensor_tensor(self.ABFLO, self.SQT, self.ABF,
                                                         ALU.subtract),
                     waits=p.w("dve"), inc="dve")
                scale = [self.lnp_c[:, 16 + dc:16 + dc + 1] for dc in range(8)]
                bias = [self.lnp_c[:, 24 + dc:24 + dc + 1] for dc in range(8)]
                self.tile_to_fm(self.ABF, 0,
                                [h2V[:, dc, t * 128:(t + 1) * 128] for dc in range(NDC)],
                                scale=scale, bias=bias, first_waits=p.w("dve", "act"))
                zb = self.CONSTV[:, 6:7]
                self.tile_to_fm(self.ABFLO, 0,
                                [self.H2LO.rearrange("p (dc c) -> p dc c", c=128)[:, dc, :]
                                 for dc in range(NDC)],
                                scale=scale, bias=[zb] * 8, first_waits=p.w("dve", "act"))
                h2loV = self.H2LO.rearrange("p (dc c) -> p dc c", c=128)
                pairs = ([(h2V[:, dc, t * 128:(t + 1) * 128], self.gatew_bf[:, dc * 8:(dc + 1) * 8])
                          for dc in range(NDC)] +
                         [(h2loV[:, dc, :], self.gatew_bf[:, dc * 8:(dc + 1) * 8])
                          for dc in range(NDC)] +
                         [(h2V[:, dc, t * 128:(t + 1) * 128], self.gatew_lo[:, dc * 8:(dc + 1) * 8])
                          for dc in range(NDC)])
                self.mm_group(self.pb[3][:, 0:8], pairs, first_waits=p.w("act", "dmaw", "dve"))
                p.op("tensor", lambda E: E.matmul(self.pb[3][:, 0:8], self.ones1, self.gateb_row,
                                                  start=False, stop=True, skip_group_check=True),
                     inc="pe")
                st = self.STAT
                LG = self.SQT
                p.op("scalar", lambda E: E.copy(LG[:, 0:8], self.pb[3][:, 0:8]),
                     waits=p.w("pe"), inc="act")
                p.op("vector", lambda E: E.reduce_max(st[:, 8:9], LG[:, 0:8], axis=AX.X),
                     waits=p.w("act"), inc="dve")
                p.op("vector", lambda E: E.tensor_scalar(LG[:, 16:24], LG[:, 0:8], st[:, 8:9],
                                                         None, ALU.is_equal),
                     waits=p.w("dve"), inc="dve")
                p.op("vector", lambda E: E.tensor_scalar(LG[:, 8:16], LG[:, 16:24], -1e30, None,
                                                         ALU.mult), waits=p.w("dve"), inc="dve")
                p.op("vector", lambda E: E.tensor_tensor(LG[:, 8:16], LG[:, 8:16], LG[:, 0:8],
                                                         ALU.add), waits=p.w("dve"), inc="dve")
                p.op("vector", lambda E: E.reduce_max(st[:, 9:10], LG[:, 8:16], axis=AX.X),
                     waits=p.w("dve"), inc="dve")
                p.op("vector", lambda E: E.tensor_scalar(LG[:, 24:32], LG[:, 8:16], st[:, 9:10],
                                                         None, ALU.is_equal),
                     waits=p.w("dve"), inc="dve")
                p.op("vector", lambda E: E.tensor_tensor(st[:, 10:11], st[:, 9:10], st[:, 8:9],
                                                         ALU.subtract), waits=p.w("dve"), inc="dve")
                p.op("scalar", lambda E: E.activation(st[:, 10:11], st[:, 10:11], AF.Exp),
                     waits=p.w("dve"), inc="act")
                p.op("vector", lambda E: E.tensor_scalar(st[:, 11:12], st[:, 10:11], 1.0, None,
                                                         ALU.add), waits=p.w("act"), inc="dve")
                p.op("vector", lambda E: E.reciprocal(st[:, 12:13], st[:, 11:12]),
                     waits=p.w("dve"), inc="dve")
                p.op("vector", lambda E: E.tensor_tensor(st[:, 13:14], st[:, 10:11], st[:, 12:13],
                                                         ALU.mult), waits=p.w("dve"), inc="dve")
                ewt = self.ew_c[:, t * 8:(t + 1) * 8]
                p.op("vector", lambda E, o=ewt: E.tensor_scalar(o, LG[:, 16:24], st[:, 12:13],
                                                                None, ALU.mult),
                     waits=p.w("dve"), inc="dve")
                p.op("vector", lambda E: E.tensor_scalar(LG[:, 24:32], LG[:, 24:32], st[:, 13:14],
                                                         None, ALU.mult),
                     waits=p.w("dve"), inc="dve")
                p.op("vector", lambda E, o=ewt: E.tensor_tensor(o, o, LG[:, 24:32], ALU.add),
                     waits=p.w("dve"), inc="dve")
                p.op("vector", lambda E, i=ewt: E.tensor_copy(self.ew_bf, i),
                     waits=p.w("dve"), inc="dve")
                p.op("tensor", lambda E: E.transpose(self.pb_bf[4][0:8, 0:128], self.ew_bf,
                                                     self.id128),
                     waits=p.w("dve", "act"), inc="pe")
                p.op("scalar", lambda E: E.copy(self.ewT, self.pb_bf[4][0:8, 0:128]),
                     waits=p.w("pe"), inc="act")
                for nh in range(2):
                    p.op("tensor", lambda E, nh=nh: E.matmul(
                        self.pb[5], self.ewT, self.b2_bf[:, nh * 512:(nh + 1) * 512],
                        start=True, stop=True, skip_group_check=True),
                         waits=p.w("act", "dve"), inc="pe")
                    p.op("vector", lambda E, t=t, nh=nh: E.tensor_tensor(
                        yV[:, t, nh * 512:(nh + 1) * 512], yV[:, t, nh * 512:(nh + 1) * 512],
                        self.pb[5], ALU.add), waits=p.w("pe"), inc="dve")

            W1 = self.win["L0_ffn_W1"]
            W2 = self.win["L0_ffn_W2"]
            last_w1 = p.cnt["pe"]
            last_w2 = p.cnt["pe"]
            for e in range(E):
                for half in range(2):
                    l1 = p.op("gpsimd", lambda E_, e=e, half=half: E_.dma_start(
                        out=w1V, in_=W1[e].rearrange("(dc p) f -> p dc f", p=128)[
                            :, :, half * 2048:(half + 1) * 2048]),
                         waits=[("pe", last_w1)], inc="dmaw", incn=16)
                    l2 = p.op("gpsimd", lambda E_, e=e, half=half: E_.dma_start(
                        out=w2V, in_=W2[e].rearrange("(fc p) o -> p fc o", p=128)[
                            :, half * 16:(half + 1) * 16, :]),
                         waits=[("pe", last_w2)], inc="dmaw", incn=16)
                    for tp in range(8):
                        relu_done = {}
                        for fcl in range(16):
                            fc_g = half * 16 + fcl
                            bank = 1 + (fcl % 2)
                            wts = ([("act", relu_done[fcl - 2])] if fcl - 2 in relu_done
                                   else [("dmaw", l2)] + p.w("act", "dve"))
                            pairs = [(w1V[:, dc, fcl * 128:(fcl + 1) * 128],
                                      h2V[:, dc, tp * 256:(tp + 1) * 256]) for dc in range(NDC)]
                            g = self.mm_group(self.pb[bank][:, 0:256], pairs, first_waits=wts)
                            if tp == 7 and fcl == 15:
                                last_w1 = g
                            relu_done[fcl] = p.op(
                                "scalar", lambda E_, fcl=fcl, e=e, fc_g=fc_g, bank=bank:
                                E_.activation(hidV[:, fcl, :], self.pb[bank][:, 0:256], AF.Relu,
                                              bias=self.b1_c[:, e * 32 + fc_g:e * 32 + fc_g + 1]),
                                waits=[("pe", g)], inc="act")
                        stt_done = {}
                        for ts in range(2):
                            t = tp * 2 + ts
                            for nh in range(2):
                                bank = 5 + nh
                                key = (ts - 1, nh)
                                wts = ([("dve", stt_done[key])] if key in stt_done
                                       else [("dmaw", l2)] + p.w("dve"))
                                if ts == 0 and nh == 0:
                                    wts = wts + [("act", relu_done[15])]
                                pairs = [(hidV[:, fcl, ts * 128:(ts + 1) * 128],
                                          w2V[:, fcl, nh * 512:(nh + 1) * 512])
                                         for fcl in range(16)]
                                g2 = self.mm_group(self.pb[bank], pairs, first_waits=wts)
                                if tp == 7 and ts == 1 and nh == 1:
                                    last_w2 = g2
                                ew_col = self.ew_c[:, t * 8 + e:t * 8 + e + 1]
                                stt_done[(ts, nh)] = p.op(
                                    "vector", lambda E_, t=t, nh=nh, bank=bank, ec=ew_col:
                                    E_.scalar_tensor_tensor(
                                        yV[:, t, nh * 512:(nh + 1) * 512], self.pb[bank], ec,
                                        yV[:, t, nh * 512:(nh + 1) * 512], ALU.mult, ALU.add),
                                    waits=[("pe", g2)], inc="dve")
            if self.dbg_stage == 3:
                for t in range(NT):
                    p.op("sync", lambda E, t=t: E.dma_start(out=self.dbg[s, t], in_=yV[:, t, :]),
                         waits=p.w("dve"), inc="st", incn=16)
            if self.stop_after != 3:
                self.p4(s, yV)
            p.op("gpsimd", None, waits=p.w("dve", "act", "pe"))

    def p4(self, s, yV):
        p = self.p
        ow = {}
        for t in range(NT):
            xbuf = self.XSTG[t % 2]
            xv = p.op("sync", lambda E, o=xbuf, i=self.x1scr[s, t]: E.dma_start(out=o, in_=i),
                      waits=p.w("dve"), inc="dmax", incn=16)
            ysl = yV[:, t, :]
            m, rstd = self.ln_stats(ysl, waits_sc=p.w("dve"), waits_ve=p.w("dve"))
            obuf = self.USTG[t % 2]
            p.op("vector", lambda E, ysl=ysl, m=m, rs=rstd: E.tensor_scalar(
                self.SQT, ysl, m, rs, ALU.subtract, ALU.mult), waits=p.w("act"))
            wadd = [("dmax", xv)]
            if t - 2 in ow:
                wadd.append(("st", ow[t - 2]))
            p.op("vector", lambda E, o=obuf, xb=xbuf: E.tensor_tensor(o, xb, self.SQT, ALU.add),
                 waits=wadd, inc="dve")
            ow[t] = p.op("sync", lambda E, o=self.out[s, t], i=obuf: E.dma_start(out=o, in_=i),
                 waits=p.w("dve"), inc="st", incn=16)


_BUILT = {}


def _get_program(dbg_stage=0, stop_after=0):
    key = (dbg_stage, stop_after)
    if key not in _BUILT:
        _BUILT[key] = KB(dbg_stage, stop_after).build()
    return _BUILT[key]


def kernel(**inputs):
    aux = build_aux()
    nc = _get_program()
    x = np.ascontiguousarray(inputs["L0_attn_x"])
    in_maps = []
    for c in range(8):
        m = {"x": np.ascontiguousarray(x[2 * c:2 * c + 2].reshape(NSEQ, NT, 128, D))}
        for k, v in inputs.items():
            if k != "L0_attn_x":
                m[k] = np.ascontiguousarray(v)
        m.update(aux)
        in_maps.append(m)
    res = run_bass_kernel_spmd(nc, in_maps, core_ids=list(range(8)))
    out = np.concatenate([np.asarray(r["out"]).reshape(NSEQ, S, D) for r in res.results], axis=0)
    return out.astype(np.float32)


# version 05
# speedup vs baseline: 1.0092x; 1.0092x over previous
import numpy as np
import ml_dtypes
import concourse.bass as bass
import concourse.mybir as mybir
from concourse.bass_utils import run_bass_kernel_spmd
from contextlib import ExitStack

F32 = mybir.dt.float32
BF16 = mybir.dt.bfloat16
AF = mybir.ActivationFunctionType
ALU = mybir.AluOpType
AX = mybir.AxisListType

S = 2048
D = 1024
NT = 16
NSEQ = 2
H = 16
DH = 64
E = 8
DFF = 4096
NDC = 8
LN_EPS = 1e-5
QK_EPS = 1e-6

ENGINES = ("tensor", "scalar", "vector", "gpsimd", "sync")


class Prog:

    def __init__(self, nc, stack):
        self.nc = nc
        self.ops = {e: [] for e in ENGINES}
        self.sems = {}
        self.cnt = {}
        self.stack = stack

    def sem(self, name):
        self.sems[name] = self.stack.enter_context(self.nc.semaphore(name))
        self.cnt[name] = 0

    def op(self, eng, fn, waits=(), inc=None, incn=1):
        waits = [(s, v) for (s, v) in waits if v > 0]
        if inc in ("dmax", "dmaw", "st") and self.cnt[inc] > 0:
            waits = waits + [(inc, self.cnt[inc])]
        self.ops[eng].append((waits, fn, inc, incn))
        if inc is not None:
            self.cnt[inc] += incn
            return self.cnt[inc]
        return None

    def w(self, *names):
        return [(n, self.cnt[n]) for n in names]

    def emit(self, block):
        for eng in ENGINES:
            body = self.ops[eng]
            if not body:
                continue

            def mk(body):
                def closure(Eng):
                    last = {}
                    for waits, fn, inc, incn in body:
                        for s, v in waits:
                            if last.get(s, -1) < v:
                                Eng.wait_ge(self.sems[s], v)
                                last[s] = v
                        ins = fn(Eng) if fn is not None else None
                        if inc is not None:
                            ins.then_inc(self.sems[inc], incn)
                return closure

            getattr(block, eng)(mk(body))


def build_aux():
    bf = ml_dtypes.bfloat16
    aux = {}
    aux["id128_bf"] = np.eye(128, dtype=bf)
    aux["id128_f32"] = np.eye(128, dtype=np.float32)
    idhi = np.zeros((128, 64), dtype=np.float32)
    idhi[64:128] = np.eye(64)
    aux["idhi_bf"] = idhi.astype(bf)
    aux["ones1_bf"] = np.ones((1, 128), dtype=bf)
    aux["ones_f32"] = np.ones((128, 64), dtype=np.float32)
    blocks = np.zeros((128, 2), dtype=np.float32)
    blocks[0:64, 0] = 1.0
    blocks[64:128, 1] = 1.0
    aux["blocks2"] = blocks
    aux["blocks2T"] = np.ascontiguousarray(blocks.T)
    ma = np.zeros((144, 3 * 128), dtype=np.float32)
    for v in range(3):
        jl = np.arange(144)[:, None]
        il = np.arange(128)[None, :]
        ok = (jl >= il) & (jl <= il + 16)
        if v == 0:
            ok = ok & (jl >= 8)
        if v == 2:
            ok = ok & (jl <= 135)
        ma[:, v * 128:(v + 1) * 128] = ok.astype(np.float32)
    aux["maskA_a"] = ma[0:128].astype(bf)
    aux["maskA_b"] = ma[128:144].astype(bf)
    return aux


class KB:
    def __init__(self, dbg_stage=0, stop_after=0, p2_sub=0):
        self.dbg_stage = dbg_stage
        self.stop_after = stop_after
        self.p2_sub = p2_sub

    def mm_group(self, psum_ap, pairs, first_waits=(), inc="pe"):
        p = self.p
        n = len(pairs)
        v = None
        for i, (l, r) in enumerate(pairs):
            wts = list(first_waits) if i == 0 else ()

            def fn(Eng, o=psum_ap, l=l, r=r, st=(i == 0), sp=(i == n - 1)):
                return Eng.matmul(o, l, r, start=st, stop=sp, skip_group_check=True)

            if i == n - 1:
                v = p.op("tensor", fn, waits=wts, inc=inc)
            else:
                p.op("tensor", fn, waits=wts)
        return v

    def ln_stats(self, src_ap, waits_sc=(), waits_ve=()):
        p = self.p
        st = self.STAT
        p.op("scalar", lambda E, s=src_ap: E.activation(self.SQTR, s, AF.Square,
                                                        accum_out=st[:, 1:2]),
             waits=list(waits_sc) + p.w("dve"), inc="act")
        p.op("scalar", lambda E, s=src_ap: E.activation(self.SQTR, s, AF.Identity,
                                                        accum_out=st[:, 0:1]),
             waits=list(waits_sc) + p.w("act"), inc="act")
        p.op("vector", lambda E: E.tensor_scalar(st[:, 2:3], st[:, 0:1], 1.0 / D, None, ALU.mult),
             waits=list(waits_ve) + p.w("act"))
        p.op("vector", lambda E: E.tensor_scalar(st[:, 4:5], st[:, 1:2], 1.0 / D, None, ALU.mult),
             inc="dve")
        p.op("scalar", lambda E: E.activation(st[:, 5:6], st[:, 2:3], AF.Square),
             waits=p.w("dve"), inc="act")
        p.op("vector", lambda E: E.tensor_tensor(st[:, 6:7], st[:, 4:5], st[:, 5:6], ALU.subtract),
             waits=p.w("act"), inc="dve")
        p.op("scalar", lambda E: E.activation(st[:, 5:6], st[:, 6:7], AF.Ln, bias=LN_EPS),
             waits=p.w("dve"), inc="act")
        p.op("scalar", lambda E: E.activation(st[:, 3:4], st[:, 5:6], AF.Exp, scale=-0.5),
             waits=p.w("act"), inc="act")
        return st[:, 2:3], st[:, 3:4]

    def tile_to_fm(self, src_bf, bank, dst, first_waits=(), scale=None, bias=None):
        p = self.p
        pbb = self.pb_bf[bank]
        for dc in range(NDC):
            wts = first_waits if dc == 0 else ()

            def fn(Eng, o=pbb[:, dc * 128:(dc + 1) * 128],
                   i=src_bf[:, dc * 128:(dc + 1) * 128]):
                return Eng.transpose(o, i, self.id128)

            p.op("tensor", fn, waits=wts, inc="pe" if dc == NDC - 1 else None)
        if scale is None:
            p.op("scalar", lambda E: E.copy(dst, pbb.rearrange("p (dc c) -> p dc c", c=128)),
                 waits=p.w("pe"), inc="act")
        else:
            for dc in range(NDC):
                def fn(Eng, o=dst[dc], i=pbb[:, dc * 128:(dc + 1) * 128],
                       sc=scale[dc], bi=bias[dc]):
                    return Eng.activation(o, i, AF.Identity, scale=sc, bias=bi)

                p.op("scalar", fn, waits=p.w("pe") if dc == 0 else (),
                     inc="act" if dc == NDC - 1 else None)

    def build(self):
        nc = bass.Bass("TRN2", target_bir_lowering=False)
        self.nc = nc
        stack = ExitStack()
        self.stack = stack
        p = Prog(nc, stack)
        self.p = p
        for sname in ("pe", "act", "dve", "pool", "dmaw", "dmax", "st",
                      "dx0", "dx1", "dw0", "dw1"):
            p.sem(sname)

        dr = lambda name, shape, dt, kind: nc.dram_tensor(name, shape, dt, kind=kind).ap()
        self.x_in = dr("x", [NSEQ, NT, 128, D], F32, "ExternalInput")
        shp = {"L0_attn_w_qkv": [3 * D, D], "L0_attn_b_qkv": [3 * D],
               "L0_attn_w_out": [D, D], "L0_attn_b_out": [D],
               "L0_ffn_ln1_g": [D], "L0_ffn_ln1_b": [D],
               "L0_ffn_w_qkv": [D, 3 * D], "L0_ffn_b_qkv": [3 * D],
               "L0_ffn_w_o": [D, D], "L0_ffn_b_o": [D],
               "L0_ffn_ln2_g": [D], "L0_ffn_ln2_b": [D],
               "L0_ffn_gate_w": [D, E], "L0_ffn_gate_b": [E],
               "L0_ffn_W1": [E, D, DFF], "L0_ffn_b1": [E, DFF],
               "L0_ffn_W2": [E, DFF, D], "L0_ffn_b2": [E, D]}
        self.win = {n: dr(n, s_, F32, "ExternalInput") for n, s_ in shp.items()}
        aux_shapes = {"id128_bf": ([128, 128], BF16), "id128_f32": ([128, 128], F32),
                      "idhi_bf": ([128, 64], BF16), "ones1_bf": ([1, 128], BF16),
                      "ones_f32": ([128, 64], F32), "blocks2": ([128, 2], F32),
                      "blocks2T": ([2, 128], F32), "maskA_a": ([128, 384], BF16),
                      "maskA_b": ([16, 384], BF16)}
        self.auxin = {n: dr(n, sh, dt, "ExternalInput") for n, (sh, dt) in aux_shapes.items()}
        self.out = dr("out", [NSEQ, NT, 128, D], F32, "ExternalOutput")
        if self.dbg_stage:
            self.dbg = dr("dbg", [NSEQ, NT, 128, D], F32, "ExternalOutput")
        self.x1scr = dr("x1scr", [NSEQ, NT, 128, D], F32, "Internal")
        self.uscr = dr("uscr", [NSEQ, NT, 128, D], F32, "Internal")
        self.waT = dr("waT", [D, 3 * D], BF16, "Internal")
        self.woaT = dr("woaT", [D, D], BF16, "Internal")

        self.pb = [stack.enter_context(nc.psum_tensor(f"pb{i}", [128, 512], F32)).ap()
                   for i in range(8)]
        self.pb_bf = [b.bitcast(BF16) for b in self.pb]

        sb = lambda name, shape, dt: stack.enter_context(nc.sbuf_tensor("s_" + name, shape, dt)).ap()
        self.id128 = sb("id128", [128, 128], BF16)
        self.idf32 = sb("idf32", [128, 128], F32)
        self.idhi = sb("idhi", [128, 64], BF16)
        self.ones1 = sb("ones1", [1, 128], BF16)
        self.onesf = sb("onesf", [128, 64], F32)
        self.blocks2 = sb("blocks2", [128, 2], F32)
        self.blocks2T = sb("blocks2T", [2, 128], F32)
        self.maskAa = sb("maskAa", [128, 384], BF16)
        self.maskAb = sb("maskAb", [16, 384], BF16)
        self.bA_c = sb("bA_c", [128, 24], F32)
        self.lnp_c = sb("lnp_c", [128, 32], F32)
        self.bB_c = sb("bB_c", [128, 16], F32)
        self.b1_c = sb("b1_c", [128, 256], F32)
        self.b2_bf = sb("b2_bf", [8, 1024], BF16)
        self.boutA_row = sb("boutA_row", [1, 1024], BF16)
        self.boB_row = sb("boB_row", [1, 1024], BF16)
        self.bvB_row = sb("bvB_row", [1, 1024], BF16)
        self.gateb_row = sb("gateb_row", [1, 8], BF16)
        self.gatew_bf = sb("gatew_bf", [128, 64], BF16)
        self.gatew_f32 = sb("gatew_f32", [128, 64], F32)
        self.gatew_lo = sb("gatew_lo", [128, 64], BF16)
        self.ew_c = sb("ew_c", [128, 128], F32)
        self.STAT = sb("STAT", [128, 16], F32)
        self.SQT = sb("SQT", [128, 1024], F32)
        self.SQTR = self.SQT
        self.XSTG = [sb(f"XSTG{i}", [128, 1024], F32) for i in range(2)]
        self.USTG = [sb(f"USTG{i}", [128, 1024], F32) for i in range(2)]
        self.ABF = sb("ABF", [128, 1024], BF16)
        self.ewT = sb("ewT", [8, 128], BF16)
        self.ew_bf = sb("ew_bf", [128, 8], BF16)
        self.CONSTV = sb("CONSTV", [128, 8], F32)

        cvals = [LN_EPS, QK_EPS, -0.5, 0.5, -1.0, 0.125, 0.0]
        for j, v in enumerate(cvals):
            p.op("vector", lambda E, j=j, v=v: E.memset(self.CONSTV[:, j:j + 1], v),
                 inc="dve")
            nc.const_aps.register(F32, v, self.CONSTV[:, j:j + 1])
        cbar = p.w("dve")
        for eng in ("tensor", "scalar", "gpsimd", "sync"):
            p.op(eng, None, waits=cbar)

        self.p0_prep()
        for s_ in range(NSEQ):
            self.p1(s_)
            if self.stop_after == 1:
                continue
            self.p2(s_)
            if self.stop_after == 2:
                continue
            self.p3(s_)
        p.op("sync", None, waits=p.w("st", "dmax"))

        with nc.Block() as block:
            p.emit(block)
        stack.close()
        return nc

    def p0_prep(self):
        p, nc = self.p, self.nc
        self._p0stack = ExitStack()
        self.prow24 = self._p0stack.enter_context(nc.sbuf_tensor("s_prow24", [24, 128], F32)).ap()
        self.prow32 = self._p0stack.enter_context(nc.sbuf_tensor("s_prow32", [32, 128], F32)).ap()
        for dst, src in [(self.id128, self.auxin["id128_bf"]), (self.idf32, self.auxin["id128_f32"]),
                         (self.idhi, self.auxin["idhi_bf"]), (self.ones1, self.auxin["ones1_bf"]),
                         (self.onesf, self.auxin["ones_f32"]), (self.blocks2, self.auxin["blocks2"]),
                         (self.blocks2T, self.auxin["blocks2T"]), (self.maskAa, self.auxin["maskA_a"]),
                         (self.maskAb, self.auxin["maskA_b"])]:
            p.op("sync", lambda E, d=dst, s=src: E.dma_start(out=d, in_=s), inc="dmax", incn=16)
        for dst, src in [(self.b2_bf, self.win["L0_ffn_b2"]),
                         (self.boutA_row, self.win["L0_attn_b_out"].rearrange("(o d) -> o d", o=1)),
                         (self.boB_row, self.win["L0_ffn_b_o"].rearrange("(o d) -> o d", o=1)),
                         (self.bvB_row, self.win["L0_ffn_b_qkv"].rearrange("(o d) -> o d", o=1)[:, 2048:3072]),
                         (self.gateb_row, self.win["L0_ffn_gate_b"].rearrange("(o d) -> o d", o=1)),
                         (self.gatew_bf.rearrange("p (dc e) -> p dc e", e=8),
                          self.win["L0_ffn_gate_w"].rearrange("(dc p) e -> p dc e", p=128))]:
            p.op("gpsimd", lambda E, d=dst, s=src: E.dma_start(out=d, in_=s,
                                                               allow_slow_non_contiguous=True),
                 inc="dmaw", incn=16)
        p.op("sync", lambda E: E.dma_start(
            out=self.gatew_f32.rearrange("p (dc e) -> p dc e", e=8),
            in_=self.win["L0_ffn_gate_w"].rearrange("(dc p) e -> p dc e", p=128),
            allow_slow_non_contiguous=True), inc="dmax", incn=16)
        p.op("vector", lambda E: E.tensor_tensor(self.gatew_lo, self.gatew_f32,
                                                 self.gatew_bf, ALU.subtract),
             waits=p.w("dmax", "dmaw"), inc="dve")

        p.op("sync", lambda E: E.dma_start(
            out=self.prow24, in_=self.win["L0_attn_b_qkv"].rearrange("(c k) -> c k", k=128)),
             inc="dmax", incn=16)
        p.op("tensor", lambda E: E.transpose(self.pb[1][:, 0:24], self.prow24, self.idf32[0:24, 0:24]),
             waits=p.w("dmax"), inc="pe")
        p.op("scalar", lambda E: E.copy(self.bA_c, self.pb[1][:, 0:24]), waits=p.w("pe"), inc="act")
        for i, nm in enumerate(["L0_ffn_ln1_g", "L0_ffn_ln1_b", "L0_ffn_ln2_g", "L0_ffn_ln2_b"]):
            p.op("sync", lambda E, i=i, nm=nm: E.dma_start(
                out=self.prow32[i * 8:(i + 1) * 8, :],
                in_=self.win[nm].rearrange("(c k) -> c k", k=128)), inc="dmax", incn=16)
        p.op("tensor", lambda E: E.transpose(self.pb[1][:, 32:64], self.prow32, self.idf32[0:32, 0:32]),
             waits=p.w("dmax", "act"), inc="pe")
        p.op("scalar", lambda E: E.copy(self.lnp_c, self.pb[1][:, 32:64]), waits=p.w("pe"), inc="act")
        p.op("sync", lambda E: E.dma_start(
            out=self.prow24[0:16, :],
            in_=self.win["L0_ffn_b_qkv"].rearrange("(c k) -> c k", k=128)[0:16]),
             waits=p.w("act"), inc="dmax", incn=16)
        p.op("tensor", lambda E: E.transpose(self.pb[1][:, 64:80], self.prow24[0:16, :],
                                             self.idf32[0:16, 0:16]),
             waits=p.w("dmax", "act"), inc="pe")
        p.op("scalar", lambda E: E.copy(self.bB_c, self.pb[1][:, 64:80]), waits=p.w("pe"), inc="act")
        for e in range(E):
            p.op("sync", lambda E_, e=e: E_.dma_start(
                out=self.prow32, in_=self.win["L0_ffn_b1"][e].rearrange("(c k) -> c k", k=128)),
                 waits=p.w("act"), inc="dmax", incn=16)
            p.op("tensor", lambda E_: E_.transpose(self.pb[1][:, 96:128], self.prow32,
                                                   self.idf32[0:32, 0:32]),
                 waits=p.w("dmax", "act"), inc="pe")
            p.op("scalar", lambda E_, e=e: E_.copy(self.b1_c[:, e * 32:(e + 1) * 32],
                                                   self.pb[1][:, 96:128]),
                 waits=p.w("pe"), inc="act")

        with ExitStack() as ps:
            wstg = ps.enter_context(self.nc.sbuf_tensor("p0wstg", [128, 1024], BF16)).ap()
            wstg2 = ps.enter_context(self.nc.sbuf_tensor("p0wstg2", [128, 1024], BF16)).ap()

            for (wsrc, wdst, nog) in [(self.win["L0_attn_w_qkv"], self.waT, 24),
                                      (self.win["L0_attn_w_out"], self.woaT, 8)]:
                for og in range(nog):
                    p.op("gpsimd", lambda E, s=wsrc[og * 128:(og + 1) * 128, :]:
                         E.dma_start(out=wstg, in_=s),
                         waits=p.w("pe"), inc="dmaw", incn=16)
                    for dc in range(NDC):
                        def fn(Eng, dc=dc):
                            return Eng.transpose(self.pb_bf[0][:, dc * 128:(dc + 1) * 128],
                                                 wstg[:, dc * 128:(dc + 1) * 128], self.id128)
                        p.op("tensor", fn,
                             waits=p.w("dmaw", "act") if dc == 0 else (),
                             inc="pe" if dc == NDC - 1 else None)
                    p.op("scalar", lambda E: E.copy(wstg2, self.pb_bf[0][:]),
                         waits=p.w("pe", "dmax"), inc="act")
                    p.op("sync", lambda E, og=og, wdst=wdst: E.dma_start(
                        out=wdst.rearrange("(dc p) o -> p dc o", p=128)[:, :, og * 128:(og + 1) * 128],
                        in_=wstg2.rearrange("p (dc o) -> p dc o", o=128)),
                         waits=p.w("act"), inc="dmax", incn=16)
            p.op("sync", None, waits=p.w("dmax"))
            p.op("gpsimd", None, waits=p.w("dmax"))
        self._p0stack.close()

    def p1(self, s):
        p, nc = self.p, self.nc
        KAW = 1048
        VFW = 1160
        with ExitStack() as ps:
            sb = lambda name, shape, dt: ps.enter_context(
                nc.sbuf_tensor(f"{name}_s{s}", shape, dt)).ap()
            A_fm = sb("A_fm", [128, NDC * 2048], BF16)
            KA = sb("KA", [128, 8 * 2 * KAW], BF16)
            VA = sb("VA", [128, 16 * 2 * 9 * 65], BF16)
            WQA = sb("WQA", [128, 8192], BF16)
            WOA = sb("WOA", [128, 8192], BF16)
            WBUF = [sb(f"WBUF{i}", [128, 1024], BF16) for i in range(2)]
            VSTG = sb("VSTG", [128, 2 * VFW], BF16)
            K2 = sb("K2", [128, 512], F32)
            FAC = sb("FAC", [2, 512], F32)
            QBUF = sb("QBUF", [128, 1024], BF16)
            Q2ALL = sb("Q2ALL", [128, 1024], F32)
            QFA = Q2ALL[0:2, :]
            PTA = [sb(f"PTA{i}", [128, 128], BF16) for i in range(2)]
            PTB = [sb(f"PTB{i}", [16, 128], BF16) for i in range(2)]
            OTM2 = [sb(f"OTM{i}", [128, 1024], BF16) for i in range(2)]
            OFM = sb("OFM", [128, 1024], BF16)
            ASTG = sb("ASTG", [128, 1024], F32)
            X1STG = [sb(f"X1STG{i}", [128, 1024], F32) for i in range(2)]
            RD = [sb(f"RD{i}", [128, 1], F32) for i in range(2)]

            kaV = KA.rearrange("p (oc par c) -> p oc par c", par=2, c=KAW)
            vaV = VA.rearrange("p (t c) -> p t c", c=65)
            afmV = A_fm.rearrange("p (dc t) -> p dc t", t=2048)
            wqaV = WQA.rearrange("p (dc o) -> p dc o", o=1024)
            woaV = WOA.rearrange("p (dc o) -> p dc o", o=1024)

            p.op("gpsimd", lambda E: E.dma_start(
                out=wqaV, in_=self.waT.rearrange("(dc p) o -> p dc o", p=128)[:, :, 0:1024]),
                 waits=p.w("dmax"), inc="dmaw", incn=16)
            p.op("gpsimd", lambda E: E.dma_start(
                out=woaV, in_=self.woaT.rearrange("(dc p) o -> p dc o", p=128)),
                 inc="dmaw", incn=16)

            consumed = {}
            xv = {}
            for t in range(NT):
                wts = [("dve", consumed[t - 2])] if t - 2 in consumed else ()
                xv[t] = p.op("sync", lambda E, t=t, b=self.XSTG[t % 2]:
                             E.dma_start(out=b, in_=self.x_in[s, t]),
                             waits=wts, inc=f"dx{t % 2}", incn=16)
                buf = self.XSTG[t % 2]
                m, rstd = self.ln_stats(buf, waits_sc=[(f"dx{t % 2}", xv[t])],
                                        waits_ve=[(f"dx{t % 2}", xv[t])])
                consumed[t] = p.op("vector", lambda E, b=buf, m=m, r=rstd:
                                   E.tensor_scalar(self.ABF, b, m, r, ALU.subtract, ALU.mult),
                                   waits=p.w("act"), inc="dve")
                if self.dbg_stage == 11:
                    p.op("vector", lambda E: E.tensor_copy(self.SQT, self.ABF), inc="dve")
                    p.op("sync", lambda E, t=t: E.dma_start(out=self.dbg[s, t], in_=self.SQT),
                         waits=p.w("dve"), inc="st", incn=16)
                    p.op("vector", None, waits=p.w("st"))
                self.tile_to_fm(self.ABF, 0, afmV[:, :, t * 128:(t + 1) * 128],
                                first_waits=p.w("dve", "act"))

            p.op("vector", lambda E: E.memset(kaV[:, :, :, 0:8], 0.0), inc="dve")
            p.op("vector", lambda E: E.memset(kaV[:, :, :, 8 + 1024:KAW], 0.0), inc="dve")
            p.op("vector", lambda E: E.memset(vaV[:, :, 64:65], 1.0), inc="dve")

            self.p1_evac = {}
            for oc in range(16):
                og = 8 + oc
                wbuf = WBUF[oc % 2]
                ld = p.op("gpsimd", lambda E, og=og, w=wbuf: E.dma_start(
                    out=w.rearrange("p (dc o) -> p dc o", o=128),
                    in_=self.waT.rearrange("(dc p) o -> p dc o", p=128)[:, :, og * 128:(og + 1) * 128]),
                     waits=[("pe", p.cnt["pe"])], inc=f"dw{oc % 2}", incn=16)
                is_v = oc >= 8
                if is_v:
                    vz = p.op("vector", lambda E: E.memset(VSTG, 0.0), waits=p.w("pe", "dve"),
                              inc="dve")
                for tc in range(4):
                    bank = 1 + (tc % 2)
                    pairs = [(wbuf.rearrange("p (dc o) -> p dc o", o=128)[:, dc, :],
                              afmV[:, dc, tc * 512:(tc + 1) * 512]) for dc in range(NDC)]
                    key = (oc, tc - 2) if tc >= 2 else (oc - 1, tc + 2)
                    wts = [(f"dw{oc % 2}", ld)]
                    if key in self.p1_evac:
                        wts.append(("act", self.p1_evac[key]))
                    else:
                        wts += p.w("act", "dve")
                    self.mm_group(self.pb[bank], pairs, first_waits=wts)
                    psum = self.pb[bank]
                    bias = self.bA_c[:, og:og + 1]
                    if not is_v:
                        p.op("scalar", lambda E, ps_=psum, b=bias: E.activation(
                            K2, ps_, AF.Square, bias=b), waits=p.w("pe"), inc="act")
                        kslc = kaV[:, oc, :, 8 + tc * 256: 8 + (tc + 1) * 256]
                        pv = psum.rearrange("p (j par) -> p par j", par=2)
                        self.p1_evac[(oc, tc)] = p.op("scalar", lambda E, o=kslc, i=pv, b=bias:
                                                      E.activation(o, i, AF.Identity, bias=b),
                                                      inc="act")
                        p.op("tensor", lambda E: E.matmul(self.pb[3][0:2, :], self.blocks2, K2,
                                                          start=True, stop=True,
                                                          skip_group_check=True),
                             waits=p.w("act"), inc="pe")
                        p.op("scalar", lambda E: E.activation(FAC, self.pb[3][0:2, :], AF.Ln),
                             waits=p.w("pe"), inc="act")
                        p.op("scalar", lambda E: E.activation(FAC, FAC, AF.Exp, scale=0.5),
                             waits=p.w("act"), inc="act")
                        p.op("scalar", lambda E: E.activation(FAC, FAC, AF.Ln, bias=QK_EPS),
                             waits=p.w("act"), inc="act")
                        p.op("scalar", lambda E: E.activation(FAC, FAC, AF.Exp, scale=-1.0),
                             waits=p.w("act"), inc="act")
                        p.op("tensor", lambda E: E.matmul(self.pb[4], self.blocks2T, FAC,
                                                          start=True, stop=True,
                                                          skip_group_check=True),
                             waits=p.w("act", "dve"), inc="pe")
                        pbv = self.pb[4].rearrange("p (j par) -> p par j", par=2)
                        p.op("vector", lambda E, o=kslc, i1=pbv: E.tensor_tensor(o, o, i1, ALU.mult),
                             waits=p.w("pe"), inc="dve")
                    else:
                        vslc = VSTG.rearrange("p (par c) -> p par c", par=2)[
                            :, :, 8 + tc * 256: 8 + (tc + 1) * 256]
                        pv = psum.rearrange("p (j par) -> p par j", par=2)
                        self.p1_evac[(oc, tc)] = p.op("scalar", lambda E, o=vslc, i=pv, b=bias:
                                                      E.activation(o, i, AF.Identity, bias=b),
                                                      waits=p.w("pe") + [("dve", vz)],
                                                      inc="act")
                if is_v:
                    hpair = oc - 8
                    for hh in range(2):
                        h = 2 * hpair + hh
                        base = 64 * hh
                        ident = self.id128[0:64, 0:64] if hh == 0 else self.idhi[64:128, :]
                        for par in range(2):
                            for r in range(9):
                                ti = (h * 2 + par) * 9 + r
                                src = VSTG[base:base + 64,
                                           par * VFW + r * 128: par * VFW + (r + 1) * 128]
                                nb = (par * 9 + r) % 2
                                p.op("tensor", lambda E, o=self.pb_bf[5 + nb][:, 0:64], i=src,
                                     idt=ident: E.transpose(o, i, idt),
                                     waits=p.w("act", "dve"), inc="pe")
                                p.op("scalar", lambda E, o=vaV[:, ti, 0:64],
                                     i=self.pb_bf[5 + nb][:, 0:64]: E.copy(o, i),
                                     waits=p.w("pe"), inc="act")

            self.p1_astg_evac = 0
            self.p1_ttd = 0
            self.p1_tev = 0
            x1w = {}
            for par in range(2):
                for r in range(8):
                    xsrc = self.x_in[s].rearrange("t p d -> (t p) d")[par::2].rearrange(
                        "(r p) d -> r p d", p=128)[r]
                    xbuf = self.XSTG[(par * 8 + r) % 2]
                    px = (par * 8 + r) % 2
                    xv1 = p.op("sync", lambda E, o=xbuf, i=xsrc: E.dma_start(out=o, in_=i),
                               waits=p.w("dve"), inc=f"dx{px}", incn=16)
                    qev = {}
                    for oc in range(8):
                        bank = 1 + (oc % 2)
                        if oc >= 2:
                            wts = [("act", qev[oc - 2])]
                        else:
                            wts = [("act", self.p1_astg_evac)] if self.p1_astg_evac else \
                                  p.w("act", "dmaw", "dve")
                        pairs = [(wqaV[:, dc, oc * 128:(oc + 1) * 128],
                                  afmV[:, dc, 256 * r + par: 256 * r + par + 255: 2])
                                 for dc in range(NDC)]
                        g = self.mm_group(self.pb[bank][:, 0:128], pairs, first_waits=wts)
                        bias = self.bA_c[:, oc:oc + 1]
                        p.op("scalar", lambda E, oc=oc, b=bias, bank=bank: E.activation(
                            QBUF[:, oc * 128:(oc + 1) * 128], self.pb[bank][:, 0:128],
                            AF.Identity, bias=b), waits=[("pe", g)], inc="act")
                        qev[oc] = p.op("scalar", lambda E, oc=oc, b=bias, bank=bank: E.activation(
                            Q2ALL[:, oc * 128:(oc + 1) * 128], self.pb[bank][:, 0:128],
                            AF.Square, bias=b), inc="act")
                    ssq = []
                    for hf in range(2):
                        bank = 1 + hf
                        ssq.append(p.op("tensor", lambda E, hf=hf, bank=bank: E.matmul(
                            self.pb[bank][0:2, :], self.blocks2, Q2ALL[:, hf * 512:(hf + 1) * 512],
                            start=True, stop=True, skip_group_check=True),
                             waits=[("act", qev[7])], inc="pe"))
                    for hf in range(2):
                        p.op("scalar", lambda E, hf=hf, bank=1 + hf: E.copy(
                            QFA[:, hf * 512:(hf + 1) * 512], self.pb[bank][0:2, :]),
                             waits=[("pe", ssq[1])], inc="act")
                    p.op("scalar", lambda E: E.activation(QFA, QFA, AF.Ln), waits=p.w("act"),
                         inc="act")
                    p.op("scalar", lambda E: E.activation(QFA, QFA, AF.Exp, scale=0.5),
                         waits=p.w("act"), inc="act")
                    p.op("scalar", lambda E: E.activation(QFA, QFA, AF.Ln, bias=QK_EPS),
                         waits=p.w("act"), inc="act")
                    fv = p.op("scalar", lambda E: E.activation(QFA, QFA, AF.Exp, scale=-1.0),
                              waits=p.w("act"), inc="act")
                    qmul = {}
                    for hf in range(2):
                        bank = 1 + hf
                        g3 = p.op("tensor", lambda E, hf=hf, bank=bank: E.matmul(
                            self.pb[bank], self.blocks2T, QFA[:, hf * 512:(hf + 1) * 512],
                            start=True, stop=True, skip_group_check=True),
                             waits=[("act", fv)], inc="pe")
                        qmul[hf] = p.op("vector", lambda E, hf=hf, bank=bank: E.tensor_tensor(
                            QBUF[:, hf * 512:(hf + 1) * 512], QBUF[:, hf * 512:(hf + 1) * 512],
                            self.pb[bank], ALU.mult), waits=[("pe", g3)], inc="dve")
                    mv = 0 if r == 0 else (2 if r == 7 else 1)
                    bs = {0: (5, 6, 7), 1: (0, 3, 4)}
                    sdn = {}
                    edn = {}
                    mdn = {}
                    pvd = {}
                    odn = {}

                    def emit_S(h):
                        oc, hh = h // 2, h % 2
                        base = 64 * hh
                        bA, bB, bO = bs[h % 2]
                        kwin = kaV[base:base + 64, oc, par, r * 128: r * 128 + 144]
                        q_ap = QBUF[base:base + 64, oc * 128:(oc + 1) * 128]
                        wts = ([("act", edn[h - 2])] if h - 2 in edn
                               else [("dve", qmul[1]), ("act", self.p1_tev)])
                        p.op("tensor", lambda E, k=kwin[:, 0:128], q=q_ap, bA=bA: E.matmul(
                            self.pb[bA][:, 0:128], k, q, start=True, stop=True,
                            skip_group_check=True), waits=wts, inc="pe")
                        sdn[h] = p.op("tensor", lambda E, k=kwin[:, 128:144], q=q_ap, bB=bB:
                                      E.matmul(self.pb[bB][0:16, 0:128], k, q, start=True,
                                               stop=True, skip_group_check=True), inc="pe")

                    def emit_tail(h):
                        bA, bB, bO = bs[h % 2]
                        wts = [("pe", pvd[h - 2] if h - 2 in pvd else sdn[h])]
                        if h - 2 in odn:
                            wts.append(("dve", mdn[h - 2]))
                        p.op("scalar", lambda E, bA=bA, ptab=PTA[h % 2]: E.activation(
                            ptab, self.pb[bA][:, 0:128], AF.Exp),
                             waits=wts + ([("act", odn[h - 2])] if h - 2 in odn else []),
                             inc="act")
                        ev = p.op("scalar", lambda E, bB=bB, ptbb=PTB[h % 2]: E.activation(
                            ptbb, self.pb[bB][0:16, 0:128], AF.Exp), inc="act")
                        edn[h] = ev
                        p.op("vector", lambda E, mv=mv, ptab=PTA[h % 2]: E.tensor_tensor(
                            ptab, ptab, self.maskAa[:, mv * 128:(mv + 1) * 128], ALU.mult),
                             waits=[("act", ev)], inc="dve")
                        mdn[h] = p.op("vector", lambda E, mv=mv, ptbb=PTB[h % 2]: E.tensor_tensor(
                            ptbb, ptbb, self.maskAb[:, mv * 128:(mv + 1) * 128], ALU.mult),
                             inc="dve")
                        ti = (h * 2 + par) * 9 + r
                        wpv = [("dve", mdn[h])]
                        if h - 2 in pvd:
                            wpv.append(("dve", odn_rd[h - 2]))
                        p.op("tensor", lambda E, ti=ti, bO=bO, ptab=PTA[h % 2]: E.matmul(
                            self.pb[bO][:, 0:65], ptab, vaV[:, ti, :], start=True, stop=False,
                            skip_group_check=True),
                             waits=wpv + ([("act", odn[h - 2])] if h - 2 in odn else []))
                        pvd[h] = p.op("tensor", lambda E, ti=ti, bO=bO, ptbb=PTB[h % 2]: E.matmul(
                            self.pb[bO][:, 0:65], ptbb, vaV[0:16, ti + 1, :], start=False,
                            stop=True, skip_group_check=True), inc="pe")
                        if self.dbg_stage == 15 and par == 0 and r == 0 and h == 0:
                            p.op("scalar", lambda E, bO=bO: E.copy(self.SQT[:, 0:65],
                                                                   self.pb[bO][:, 0:65]),
                                 waits=[("pe", pvd[h])], inc="act")
                            p.op("sync", lambda E: E.dma_start(out=self.dbg[s, 8], in_=self.SQT),
                                 waits=p.w("act"), inc="st", incn=16)
                            p.op("scalar", None, waits=p.w("st"))
                        odn_rd[h] = p.op("vector", lambda E, bO=bO, rd=RD[h % 2]: E.reciprocal(
                            rd, self.pb[bO][:, 64:65]), waits=[("pe", pvd[h])], inc="dve")
                        wo = [("dve", odn_rd[h])]
                        if h == 0 and self.p1_ttd:
                            wo.append(("pe", self.p1_ttd))
                        odn[h] = p.op("scalar", lambda E, h=h, bO=bO, rd=RD[h % 2],
                                      otm=OTM2[(par * 8 + r) % 2]:
                                      E.activation(otm[:, h * 64:(h + 1) * 64],
                                                   self.pb[bO][:, 0:64], AF.Identity, scale=rd),
                                      waits=wo, inc="act")

                    odn_rd = {}
                    emit_S(0)
                    emit_S(1)
                    for h in range(H):
                        emit_tail(h)
                        if h + 2 < H:
                            emit_S(h + 2)
                    if self.dbg_stage == 16 and par == 0 and r == 0:
                        p.op("vector", lambda E, otm=OTM2[0]: E.tensor_copy(self.SQT, otm),
                             waits=[("act", odn[H - 1])], inc="dve")
                        p.op("sync", lambda E: E.dma_start(out=self.dbg[s, 9], in_=self.SQT),
                             waits=p.w("dve"), inc="st", incn=16)
                        p.op("vector", None, waits=p.w("st"))
                    self.tile_to_fm(OTM2[(par * 8 + r) % 2], 7,
                                    OFM.rearrange("p (dc c) -> p dc c", c=128),
                                    first_waits=[("act", odn[H - 1])])
                    self.p1_ttd = p.cnt["pe"]
                    self.p1_tev = p.cnt["act"]
                    ofmV = OFM.rearrange("p (dc o) -> p dc o", o=128)
                    for nh in range(2):
                        bank = 1 + nh
                        pairs = [(ofmV[:, dc, :], woaV[:, dc, nh * 512:(nh + 1) * 512])
                                 for dc in range(NDC)]
                        self.mm_group(self.pb[bank], pairs, first_waits=p.w("act", "dmaw"))
                        p.op("tensor", lambda E, nh=nh, bank=bank: E.matmul(
                            self.pb[bank], self.ones1, self.boutA_row[:, nh * 512:(nh + 1) * 512],
                            start=False, stop=True, skip_group_check=True), inc="pe")
                        self.p1_astg_evac = p.op("scalar", lambda E, nh=nh, bank=bank: E.copy(
                            ASTG[:, nh * 512:(nh + 1) * 512], self.pb[bank]),
                             waits=p.w("pe", "dve"), inc="act")
                    if self.dbg_stage == 13:
                        ddst = self.dbg[s].rearrange("t p d -> (t p) d")[par::2].rearrange(
                            "(r p) d -> r p d", p=128)[r]
                        p.op("sync", lambda E, o=ddst: E.dma_start(out=o, in_=ASTG),
                             waits=p.w("act"), inc="st", incn=16)
                        p.op("scalar", None, waits=p.w("st"))
                    m, rstd = self.ln_stats(ASTG, waits_sc=p.w("act"), waits_ve=p.w("act"))
                    x1b = X1STG[(par * 8 + r) % 2]
                    p.op("vector", lambda E, m=m, rs=rstd: E.tensor_scalar(
                        self.SQT, ASTG, m, rs, ALU.subtract, ALU.mult), waits=p.w("act"))
                    ti_ = par * 8 + r
                    wadd = [(f"dx{px}", xv1)]
                    if ti_ - 2 in x1w:
                        wadd.append(("dmax", x1w[ti_ - 2]))
                    p.op("vector", lambda E, o=x1b, xb=xbuf: E.tensor_tensor(o, xb, self.SQT,
                                                                             ALU.add),
                         waits=wadd, inc="dve")
                    xdst = self.x1scr[s].rearrange("t p d -> (t p) d")[par::2].rearrange(
                        "(r p) d -> r p d", p=128)[r]
                    x1w[ti_] = p.op("sync", lambda E, o=xdst, i=x1b: E.dma_start(out=o, in_=i),
                         waits=p.w("dve"), inc="dmax", incn=16)
                    if self.dbg_stage == 1:
                        ddst = self.dbg[s].rearrange("t p d -> (t p) d")[par::2].rearrange(
                            "(r p) d -> r p d", p=128)[r]
                        p.op("sync", lambda E, o=ddst, i=x1b: E.dma_start(out=o, in_=i),
                             inc="st", incn=16)
            p.op("sync", None, waits=p.w("dmax"))
            p.op("gpsimd", None, waits=p.w("dmax", "dve", "act", "pe"))

    def p2(self, s):
        p, nc = self.p, self.nc
        with ExitStack() as ps:
            sb = lambda name, shape, dt: ps.enter_context(
                nc.sbuf_tensor(f"{name}_s{s}", shape, dt)).ap()
            HFM = sb("HFM", [128, NDC * 2048], BF16)
            KBc = sb("KBc", [128, 8 * 2048], BF16)
            VBc = sb("VBc", [128, 16 * 16 * 65], BF16)
            WB2 = [sb(f"WB2{i}", [128, 4096], BF16) for i in range(2)]
            WOB = sb("WOB", [128, 8192], BF16)
            QB = sb("QB", [128, 256], BF16)
            PT2 = [sb(f"PT2{i}", [128, 256], BF16) for i in range(2)]
            OFM2 = sb("OFM2", [128, 2048], BF16)
            BCS = sb("BCS", [64, 256], F32)
            DSC = sb("DSC", [128, 256], F32)
            hfmV = HFM.rearrange("p (dc t) -> p dc t", t=2048)
            kbV = KBc.rearrange("p (oc t) -> p oc t", t=2048)
            vbV = VBc.rearrange("p (h tt c) -> p h tt c", tt=16, c=65)
            wobV = WOB.rearrange("p (dc o) -> p dc o", o=1024)
            wqkvB = self.win["L0_ffn_w_qkv"]

            p.op("vector", lambda E: E.memset(vbV[:, :, :, 64:65], 1.0), inc="dve")
            p.op("gpsimd", lambda E: E.dma_start(
                out=wobV, in_=self.win["L0_ffn_w_o"].rearrange("(dc p) o -> p dc o", p=128)),
                 inc="dmaw", incn=16)

            self.p2_uw = {}
            consumed = {}
            for t in range(NT):
                wts = ([("dve", consumed[t - 2])] if t - 2 in consumed else p.w("dmax"))
                xv = p.op("sync", lambda E, o=self.XSTG[t % 2], i=self.x1scr[s, t]:
                          E.dma_start(out=o, in_=i), waits=wts, inc=f"dx{t % 2}", incn=16)
                xbuf = self.XSTG[t % 2]
                m, rstd = self.ln_stats(xbuf, waits_sc=[(f"dx{t % 2}", xv)],
                                        waits_ve=[(f"dx{t % 2}", xv)])
                ubuf = self.USTG[t % 2]
                wnu = p.w("act")
                if t - 2 in self.p2_uw:
                    wnu = wnu + [("dmax", self.p2_uw[t - 2])]
                p.op("vector", lambda E, b=xbuf, m=m, rs=rstd, u=ubuf: E.tensor_scalar(
                    u, b, m, rs, ALU.subtract, ALU.mult), waits=wnu, inc="dve")
                self.p2_uw[t] = p.op("sync", lambda E, u=ubuf, t=t: E.dma_start(
                    out=self.uscr[s, t], in_=u), waits=p.w("dve"), inc="dmax", incn=16)
                if self.dbg_stage == 21:
                    p.op("sync", lambda E, u=ubuf, t=t: E.dma_start(out=self.dbg[s, t], in_=u),
                         inc="st", incn=16)
                    p.op("vector", None, waits=p.w("st"))
                m2, rstd2 = self.ln_stats(ubuf, waits_sc=(), waits_ve=())
                consumed[t] = p.op("vector", lambda E, u=ubuf, m=m2, rs=rstd2: E.tensor_scalar(
                    self.ABF, u, m, rs, ALU.subtract, ALU.mult), waits=p.w("act"), inc="dve")
                scale = [self.lnp_c[:, dc:dc + 1] for dc in range(8)]
                bias = [self.lnp_c[:, 8 + dc:8 + dc + 1] for dc in range(8)]
                self.tile_to_fm(self.ABF, 0,
                                [hfmV[:, dc, t * 128:(t + 1) * 128] for dc in range(NDC)],
                                scale=scale, bias=bias, first_waits=p.w("dve", "act"))

            if self.p2_sub == 1:
                p.op("sync", None, waits=p.w("dmax"))
                p.op("gpsimd", None, waits=p.w("dmax", "dve", "act", "pe"))
                return
            for oc in range(8):
                wbuf = WB2[oc % 2]
                ld = p.op("gpsimd", lambda E, oc=oc, w=wbuf: E.dma_start(
                    out=w[:, 0:1024].rearrange("p (dc o) -> p dc o", o=128),
                    in_=wqkvB.rearrange("(dc p) o -> p dc o", p=128)[
                        :, :, 1024 + oc * 128:1024 + (oc + 1) * 128]),
                     waits=[("pe", p.cnt["pe"])], inc=f"dw{oc % 2}", incn=16)
                for tc in range(4):
                    bank = 1 + (tc % 2)
                    pairs = [(wbuf[:, 0:1024].rearrange("p (dc o) -> p dc o", o=128)[:, dc, :],
                              hfmV[:, dc, tc * 512:(tc + 1) * 512]) for dc in range(NDC)]
                    self.mm_group(self.pb[bank], pairs,
                                  first_waits=[(f"dw{oc % 2}", ld)] + p.w("act"))
                    p.op("scalar", lambda E, oc=oc, tc=tc, bank=bank: E.activation(
                        kbV[:, oc, tc * 512:(tc + 1) * 512], self.pb[bank], AF.Identity,
                        bias=self.bB_c[:, 8 + oc:8 + oc + 1]), waits=p.w("pe"), inc="act")
            for vp in range(2):
                wbuf = WB2[vp % 2]
                ld = p.op("gpsimd", lambda E, vp=vp, w=wbuf: E.dma_start(
                    out=w.rearrange("p (dc o) -> p dc o", o=512),
                    in_=wqkvB.rearrange("(dc p) o -> p dc o", p=128)[
                        :, :, 2048 + vp * 512:2048 + (vp + 1) * 512]),
                     waits=[("pe", p.cnt["pe"])], inc=f"dw{vp % 2}", incn=16)
                for t in range(NT):
                    bank = 1 + (t % 2)
                    pairs = [(hfmV[:, dc, t * 128:(t + 1) * 128],
                              wbuf.rearrange("p (dc o) -> p dc o", o=512)[:, dc, :])
                             for dc in range(NDC)]
                    self.mm_group(self.pb[bank], pairs,
                                  first_waits=[(f"dw{vp % 2}", ld)] + p.w("act"))
                    p.op("tensor", lambda E, vp=vp, bank=bank: E.matmul(
                        self.pb[bank], self.ones1, self.bvB_row[:, vp * 512:(vp + 1) * 512],
                        start=False, stop=True, skip_group_check=True), inc="pe")
                    dst = vbV[:, vp * 8:(vp + 1) * 8, t, 0:64]
                    src = self.pb[bank].rearrange("p (h c) -> p h c", c=64)
                    p.op("scalar", lambda E, o=dst, i=src: E.copy(o, i), waits=p.w("pe"),
                         inc="act")

            if self.dbg_stage == 28:
                for i in range(2):
                    p.op("vector", lambda E, i=i: E.tensor_copy(
                        self.SQT, hfmV[:, 0, i * 1024:(i + 1) * 1024]),
                         waits=p.w("act"), inc="dve")
                    p.op("sync", lambda E, i=i: E.dma_start(out=self.dbg[s, i], in_=self.SQT),
                         waits=p.w("dve"), inc="st", incn=16)
                    p.op("vector", None, waits=p.w("st"))
                    p.op("vector", lambda E, i=i: E.tensor_copy(
                        self.SQT, kbV[:, 0, i * 1024:(i + 1) * 1024]),
                         inc="dve")
                    p.op("sync", lambda E, i=i: E.dma_start(out=self.dbg[s, 2 + i], in_=self.SQT),
                         waits=p.w("dve"), inc="st", incn=16)
                    p.op("vector", None, waits=p.w("st"))
            if self.p2_sub == 2:
                p.op("sync", None, waits=p.w("dmax"))
                p.op("gpsimd", None, waits=p.w("dmax", "dve", "act", "pe"))
                return
            for qg in range(8):
                ts0 = qg * 2
                uv = []
                for i in range(2):
                    uv.append(p.op("sync", lambda E, i=i, ts0=ts0: E.dma_start(
                        out=self.USTG[i], in_=self.uscr[s, ts0 + i]),
                         waits=p.w("dve", "dmax"), inc=f"dx{i}", incn=16))
                for oc in range(8):
                    wbuf = WB2[oc % 2]
                    ld = p.op("gpsimd", lambda E, oc=oc, w=wbuf: E.dma_start(
                        out=w[:, 0:1024].rearrange("p (dc o) -> p dc o", o=128),
                        in_=wqkvB.rearrange("(dc p) o -> p dc o", p=128)[
                            :, :, oc * 128:(oc + 1) * 128]),
                         waits=[("pe", p.cnt["pe"])], inc=f"dw{oc % 2}", incn=16)
                    pairs = [(wbuf[:, 0:1024].rearrange("p (dc o) -> p dc o", o=128)[:, dc, :],
                              hfmV[:, dc, qg * 256:(qg + 1) * 256]) for dc in range(NDC)]
                    self.mm_group(self.pb[5][:, 0:256], pairs,
                                  first_waits=[(f"dw{oc % 2}", ld)] + p.w("dve", "act"))
                    p.op("scalar", lambda E, oc=oc: E.activation(
                        QB, self.pb[5][:, 0:256], AF.Identity, bias=self.bB_c[:, oc:oc + 1]),
                         waits=p.w("pe"), inc="act")
                    if self.dbg_stage == 28 and qg == 0 and oc == 0:
                        p.op("vector", lambda E: E.tensor_copy(self.SQT[:, 0:256], QB),
                             waits=p.w("act"), inc="dve")
                        p.op("sync", lambda E: E.dma_start(out=self.dbg[s, 4], in_=self.SQT),
                             waits=p.w("dve"), inc="st", incn=16)
                        p.op("vector", None, waits=p.w("st"))
                    if self.p2_sub == 3:
                        continue
                    for hh in range(2):
                        h = 2 * oc + hh
                        base = 64 * hh
                        s_done = {}
                        exp_done = {}
                        pv_done = {}

                        def emit_S(kt):
                            bank = 1 + (kt % 2)
                            wts = ([("act", exp_done[kt - 2])] if kt - 2 in exp_done
                                   else p.w("act", "dve"))
                            s_done[kt] = p.op("tensor", lambda E, kt=kt, bank=bank,
                                              base=base, oc=oc: E.matmul(
                                self.pb[bank][:, 0:256],
                                kbV[base:base + 64, oc, kt * 128:(kt + 1) * 128],
                                QB[base:base + 64, :], start=True, stop=True,
                                skip_group_check=True), waits=wts, inc="pe")

                        emit_S(0)
                        emit_S(1)
                        for kt in range(16):
                            bank = 1 + (kt % 2)
                            ptb = PT2[kt % 2]
                            if self.dbg_stage == 30 and qg == 0 and oc == 0 and hh == 0 and kt == 0:
                                p.op("scalar", lambda E, bank=bank: E.copy(
                                    self.SQT[:, 0:256], self.pb[bank][:, 0:256]),
                                     waits=[("pe", s_done[0])], inc="act")
                                p.op("sync", lambda E: E.dma_start(out=self.dbg[s, 6],
                                                                   in_=self.SQT),
                                     waits=p.w("act"), inc="st", incn=16)
                                p.op("scalar", None, waits=p.w("st"))
                            wexp = [("pe", pv_done[kt - 2] if kt >= 2 else s_done[kt])]
                            exp_done[kt] = p.op("scalar", lambda E, bank=bank, ptb=ptb:
                                                E.activation(ptb, self.pb[bank][:, 0:256],
                                                             AF.Exp, scale=0.125),
                                                waits=wexp, inc="act")
                            if kt + 2 < 16:
                                emit_S(kt + 2)
                            if self.dbg_stage == 29 and qg == 0 and oc == 0 and hh == 0 and kt == 0:
                                p.op("vector", lambda E, ptb=ptb: E.tensor_copy(
                                    self.SQT[:, 0:256], ptb), waits=p.w("act"), inc="dve")
                                p.op("sync", lambda E: E.dma_start(out=self.dbg[s, 5],
                                                                   in_=self.SQT),
                                     waits=p.w("dve"), inc="st", incn=16)
                                p.op("vector", None, waits=p.w("st"))
                            if self.p2_sub == 4:
                                continue
                            pv_done[kt] = p.op("tensor", lambda E, h=h, kt=kt, ptb=ptb: E.matmul(
                                self.pb[3][0:65, 0:256], vbV[:, h, kt, :],
                                ptb, start=(kt == 0), stop=(kt == 15), skip_group_check=True),
                                 waits=[("act", exp_done[kt])] + (p.w("dve") if kt == 0 else []),
                                 inc="pe")
                        if self.p2_sub == 4:
                            continue
                        if self.dbg_stage == 27 and qg == 0 and oc == 0 and hh == 0:
                            p.op("scalar", lambda E: E.copy(self.SQT[0:65, 0:256],
                                                            self.pb[3][0:65, 0:256]),
                                 waits=[("pe", pv_done[15])], inc="act")
                            p.op("sync", lambda E: E.dma_start(out=self.dbg[s, 0],
                                                               in_=self.SQT),
                                 waits=p.w("act"), inc="st", incn=16)
                            p.op("scalar", None, waits=p.w("st"))
                        p.op("scalar", lambda E: E.activation(DSC[64:65, :],
                                                              self.pb[3][64:65, 0:256], AF.Ln),
                             waits=[("pe", pv_done[15])], inc="act")
                        p.op("scalar", lambda E: E.activation(DSC[64:65, :], DSC[64:65, :],
                                                              AF.Exp, scale=-1.0),
                             waits=p.w("act"), inc="act")
                        p.op("tensor", lambda E: E.matmul(self.pb[4][0:64, 0:256],
                                                          self.onesf[64:65, :], DSC[64:65, :],
                                                          start=True, stop=True,
                                                          skip_group_check=True),
                             waits=p.w("act", "dve"), inc="pe")
                        p.op("scalar", lambda E: E.copy(BCS, self.pb[4][0:64, 0:256]),
                             waits=p.w("pe"), inc="act")
                        p.op("vector", lambda E, base=base, oc=oc: E.tensor_tensor(
                            OFM2.rearrange("p (oc t) -> p oc t", t=256)[base:base + 64, oc, :],
                            self.pb[3][0:64, 0:256], BCS, ALU.mult),
                             waits=p.w("act", "pe"), inc="dve")
                if self.dbg_stage == 26:
                    for half in range(2):
                        p.op("vector", lambda E, half=half: E.tensor_copy(
                            self.SQT, OFM2[:, half * 1024:(half + 1) * 1024]),
                             waits=p.w("dve"), inc="dve")
                        p.op("sync", lambda E, qg=qg, half=half: E.dma_start(
                            out=self.dbg[s, 2 * qg + half], in_=self.SQT),
                             waits=p.w("dve"), inc="st", incn=16)
                        p.op("vector", None, waits=p.w("st"))
                if self.p2_sub in (3, 4, 5):
                    continue
                ofuV = OFM2.rearrange("p (oc t) -> p oc t", t=256)
                for ts in range(2):
                    for nh in range(2):
                        bank = 6 + nh
                        for oc in range(8):
                            p.op("tensor", lambda E, oc=oc, ts=ts, nh=nh, bank=bank: E.matmul(
                                self.pb[bank], ofuV[:, oc, ts * 128:(ts + 1) * 128],
                                wobV[:, oc, nh * 512:(nh + 1) * 512],
                                start=(oc == 0), stop=False, skip_group_check=True),
                                 waits=p.w("dve", "dmaw") if oc == 0 else ())
                        p.op("tensor", lambda E, nh=nh, bank=bank: E.matmul(
                            self.pb[bank], self.ones1, self.boB_row[:, nh * 512:(nh + 1) * 512],
                            start=False, stop=True, skip_group_check=True), inc="pe")
                        p.op("vector", lambda E, ts=ts, nh=nh, bank=bank: E.tensor_tensor(
                            self.USTG[ts][:, nh * 512:(nh + 1) * 512],
                            self.USTG[ts][:, nh * 512:(nh + 1) * 512],
                            self.pb[bank], ALU.add),
                             waits=p.w("pe") + [(f"dx{ts}", uv[ts])], inc="dve")
                    p.op("sync", lambda E, ts=ts, ts0=ts0: E.dma_start(
                        out=self.uscr[s, ts0 + ts], in_=self.USTG[ts]),
                         waits=p.w("dve"), inc="dmax", incn=16)
                    if self.dbg_stage == 2:
                        p.op("sync", lambda E, ts=ts, ts0=ts0: E.dma_start(
                            out=self.dbg[s, ts0 + ts], in_=self.USTG[ts]),
                             inc="st", incn=16)
            p.op("sync", None, waits=p.w("dmax"))
            p.op("gpsimd", None, waits=p.w("dmax", "dve", "act", "pe"))

    def p3(self, s):
        p, nc = self.p, self.nc
        with ExitStack() as ps:
            sb = lambda name, shape, dt: ps.enter_context(
                nc.sbuf_tensor(f"{name}_s{s}", shape, dt)).ap()
            Y = sb("Y", [128, NT * 1024], F32)
            H2 = sb("H2", [128, NDC * 2048], BF16)
            W1R = sb("W1R", [128, NDC * 2048], BF16)
            W2R = sb("W2R", [128, 16 * 1024], BF16)
            HID = sb("HID", [128, 16 * 256], BF16)
            self.H2LO = HID[:, 0:1024]
            self.ABFLO = HID[:, 1024:2048]
            yV = Y.rearrange("p (t d) -> p t d", d=1024)
            h2V = H2.rearrange("p (dc t) -> p dc t", t=2048)
            w1V = W1R.rearrange("p (dc f) -> p dc f", f=2048)
            w2V = W2R.rearrange("p (fc o) -> p fc o", o=1024)
            hidV = HID.rearrange("p (fc t) -> p fc t", t=256)

            yl = p.op("sync", lambda E: E.dma_start(
                out=yV, in_=self.uscr[s].rearrange("t p d -> p t d")),
                 waits=p.w("dve"), inc="dmax", incn=16)

            for t in range(NT):
                ysl = yV[:, t, :]
                m, rstd = self.ln_stats(ysl, waits_sc=[("dmax", yl)], waits_ve=[("dmax", yl)])
                p.op("vector", lambda E, ysl=ysl, m=m, rs=rstd: E.tensor_scalar(
                    self.SQT, ysl, m, rs, ALU.subtract, ALU.mult), waits=p.w("act"), inc="dve")
                p.op("vector", lambda E: E.tensor_copy(self.ABF, self.SQT),
                     waits=p.w("dve"), inc="dve")
                p.op("vector", lambda E: E.tensor_tensor(self.ABFLO, self.SQT, self.ABF,
                                                         ALU.subtract),
                     waits=p.w("dve"), inc="dve")
                scale = [self.lnp_c[:, 16 + dc:16 + dc + 1] for dc in range(8)]
                bias = [self.lnp_c[:, 24 + dc:24 + dc + 1] for dc in range(8)]
                self.tile_to_fm(self.ABF, 0,
                                [h2V[:, dc, t * 128:(t + 1) * 128] for dc in range(NDC)],
                                scale=scale, bias=bias, first_waits=p.w("dve", "act"))
                zb = self.CONSTV[:, 6:7]
                self.tile_to_fm(self.ABFLO, 0,
                                [self.H2LO.rearrange("p (dc c) -> p dc c", c=128)[:, dc, :]
                                 for dc in range(NDC)],
                                scale=scale, bias=[zb] * 8, first_waits=p.w("dve", "act"))
                h2loV = self.H2LO.rearrange("p (dc c) -> p dc c", c=128)
                pairs = ([(h2V[:, dc, t * 128:(t + 1) * 128], self.gatew_bf[:, dc * 8:(dc + 1) * 8])
                          for dc in range(NDC)] +
                         [(h2loV[:, dc, :], self.gatew_bf[:, dc * 8:(dc + 1) * 8])
                          for dc in range(NDC)] +
                         [(h2V[:, dc, t * 128:(t + 1) * 128], self.gatew_lo[:, dc * 8:(dc + 1) * 8])
                          for dc in range(NDC)])
                self.mm_group(self.pb[3][:, 0:8], pairs, first_waits=p.w("act", "dmaw", "dve"))
                p.op("tensor", lambda E: E.matmul(self.pb[3][:, 0:8], self.ones1, self.gateb_row,
                                                  start=False, stop=True, skip_group_check=True),
                     inc="pe")
                st = self.STAT
                LG = self.SQT
                p.op("scalar", lambda E: E.copy(LG[:, 0:8], self.pb[3][:, 0:8]),
                     waits=p.w("pe"), inc="act")
                p.op("vector", lambda E: E.reduce_max(st[:, 8:9], LG[:, 0:8], axis=AX.X),
                     waits=p.w("act"), inc="dve")
                p.op("vector", lambda E: E.tensor_scalar(LG[:, 16:24], LG[:, 0:8], st[:, 8:9],
                                                         None, ALU.is_equal),
                     waits=p.w("dve"), inc="dve")
                p.op("vector", lambda E: E.tensor_scalar(LG[:, 8:16], LG[:, 16:24], -1e30, None,
                                                         ALU.mult), waits=p.w("dve"), inc="dve")
                p.op("vector", lambda E: E.tensor_tensor(LG[:, 8:16], LG[:, 8:16], LG[:, 0:8],
                                                         ALU.add), waits=p.w("dve"), inc="dve")
                p.op("vector", lambda E: E.reduce_max(st[:, 9:10], LG[:, 8:16], axis=AX.X),
                     waits=p.w("dve"), inc="dve")
                p.op("vector", lambda E: E.tensor_scalar(LG[:, 24:32], LG[:, 8:16], st[:, 9:10],
                                                         None, ALU.is_equal),
                     waits=p.w("dve"), inc="dve")
                p.op("vector", lambda E: E.tensor_tensor(st[:, 10:11], st[:, 9:10], st[:, 8:9],
                                                         ALU.subtract), waits=p.w("dve"), inc="dve")
                p.op("scalar", lambda E: E.activation(st[:, 10:11], st[:, 10:11], AF.Exp),
                     waits=p.w("dve"), inc="act")
                p.op("vector", lambda E: E.tensor_scalar(st[:, 11:12], st[:, 10:11], 1.0, None,
                                                         ALU.add), waits=p.w("act"), inc="dve")
                p.op("vector", lambda E: E.reciprocal(st[:, 12:13], st[:, 11:12]),
                     waits=p.w("dve"), inc="dve")
                p.op("vector", lambda E: E.tensor_tensor(st[:, 13:14], st[:, 10:11], st[:, 12:13],
                                                         ALU.mult), waits=p.w("dve"), inc="dve")
                ewt = self.ew_c[:, t * 8:(t + 1) * 8]
                p.op("vector", lambda E, o=ewt: E.tensor_scalar(o, LG[:, 16:24], st[:, 12:13],
                                                                None, ALU.mult),
                     waits=p.w("dve"), inc="dve")
                p.op("vector", lambda E: E.tensor_scalar(LG[:, 24:32], LG[:, 24:32], st[:, 13:14],
                                                         None, ALU.mult),
                     waits=p.w("dve"), inc="dve")
                p.op("vector", lambda E, o=ewt: E.tensor_tensor(o, o, LG[:, 24:32], ALU.add),
                     waits=p.w("dve"), inc="dve")
                p.op("vector", lambda E, i=ewt: E.tensor_copy(self.ew_bf, i),
                     waits=p.w("dve"), inc="dve")
                p.op("tensor", lambda E: E.transpose(self.pb_bf[4][0:8, 0:128], self.ew_bf,
                                                     self.id128),
                     waits=p.w("dve", "act"), inc="pe")
                p.op("scalar", lambda E: E.copy(self.ewT, self.pb_bf[4][0:8, 0:128]),
                     waits=p.w("pe"), inc="act")
                for nh in range(2):
                    p.op("tensor", lambda E, nh=nh: E.matmul(
                        self.pb[5], self.ewT, self.b2_bf[:, nh * 512:(nh + 1) * 512],
                        start=True, stop=True, skip_group_check=True),
                         waits=p.w("act", "dve"), inc="pe")
                    p.op("vector", lambda E, t=t, nh=nh: E.tensor_tensor(
                        yV[:, t, nh * 512:(nh + 1) * 512], yV[:, t, nh * 512:(nh + 1) * 512],
                        self.pb[5], ALU.add), waits=p.w("pe"), inc="dve")

            W1 = self.win["L0_ffn_W1"]
            W2 = self.win["L0_ffn_W2"]
            last_w1 = p.cnt["pe"]
            last_w2 = p.cnt["pe"]
            for e in range(E):
                for half in range(2):
                    l1 = p.op("gpsimd", lambda E_, e=e, half=half: E_.dma_start(
                        out=w1V, in_=W1[e].rearrange("(dc p) f -> p dc f", p=128)[
                            :, :, half * 2048:(half + 1) * 2048]),
                         waits=[("pe", last_w1)], inc="dmaw", incn=16)
                    l2 = p.op("gpsimd", lambda E_, e=e, half=half: E_.dma_start(
                        out=w2V, in_=W2[e].rearrange("(fc p) o -> p fc o", p=128)[
                            :, half * 16:(half + 1) * 16, :]),
                         waits=[("pe", last_w2)], inc="dmaw", incn=16)
                    for tp in range(8):
                        relu_done = {}
                        for fcl in range(16):
                            fc_g = half * 16 + fcl
                            bank = 1 + (fcl % 2)
                            wts = ([("act", relu_done[fcl - 2])] if fcl - 2 in relu_done
                                   else [("dmaw", l2)] + p.w("act", "dve"))
                            pairs = [(w1V[:, dc, fcl * 128:(fcl + 1) * 128],
                                      h2V[:, dc, tp * 256:(tp + 1) * 256]) for dc in range(NDC)]
                            g = self.mm_group(self.pb[bank][:, 0:256], pairs, first_waits=wts)
                            if tp == 7 and fcl == 15:
                                last_w1 = g
                            relu_done[fcl] = p.op(
                                "scalar", lambda E_, fcl=fcl, e=e, fc_g=fc_g, bank=bank:
                                E_.activation(hidV[:, fcl, :], self.pb[bank][:, 0:256], AF.Relu,
                                              bias=self.b1_c[:, e * 32 + fc_g:e * 32 + fc_g + 1]),
                                waits=[("pe", g)], inc="act")
                        stt_done = {}
                        for ts in range(2):
                            t = tp * 2 + ts
                            for nh in range(2):
                                bank = 5 + nh
                                key = (ts - 1, nh)
                                wts = ([("dve", stt_done[key])] if key in stt_done
                                       else [("dmaw", l2)] + p.w("dve"))
                                if ts == 0 and nh == 0:
                                    wts = wts + [("act", relu_done[15])]
                                pairs = [(hidV[:, fcl, ts * 128:(ts + 1) * 128],
                                          w2V[:, fcl, nh * 512:(nh + 1) * 512])
                                         for fcl in range(16)]
                                g2 = self.mm_group(self.pb[bank], pairs, first_waits=wts)
                                if tp == 7 and ts == 1 and nh == 1:
                                    last_w2 = g2
                                ew_col = self.ew_c[:, t * 8 + e:t * 8 + e + 1]
                                stt_done[(ts, nh)] = p.op(
                                    "vector", lambda E_, t=t, nh=nh, bank=bank, ec=ew_col:
                                    E_.scalar_tensor_tensor(
                                        yV[:, t, nh * 512:(nh + 1) * 512], self.pb[bank], ec,
                                        yV[:, t, nh * 512:(nh + 1) * 512], ALU.mult, ALU.add),
                                    waits=[("pe", g2)], inc="dve")
            if self.dbg_stage == 3:
                for t in range(NT):
                    p.op("sync", lambda E, t=t: E.dma_start(out=self.dbg[s, t], in_=yV[:, t, :]),
                         waits=p.w("dve"), inc="st", incn=16)
            if self.stop_after != 3:
                self.p4(s, yV)
            p.op("gpsimd", None, waits=p.w("dve", "act", "pe"))

    def p4(self, s, yV):
        p = self.p
        ow = {}
        for t in range(NT):
            xbuf = self.XSTG[t % 2]
            xv = p.op("sync", lambda E, o=xbuf, i=self.x1scr[s, t]: E.dma_start(out=o, in_=i),
                      waits=p.w("dve"), inc="dmax", incn=16)
            ysl = yV[:, t, :]
            m, rstd = self.ln_stats(ysl, waits_sc=p.w("dve"), waits_ve=p.w("dve"))
            obuf = self.USTG[t % 2]
            p.op("vector", lambda E, ysl=ysl, m=m, rs=rstd: E.tensor_scalar(
                self.SQT, ysl, m, rs, ALU.subtract, ALU.mult), waits=p.w("act"))
            wadd = [("dmax", xv)]
            if t - 2 in ow:
                wadd.append(("st", ow[t - 2]))
            p.op("vector", lambda E, o=obuf, xb=xbuf: E.tensor_tensor(o, xb, self.SQT, ALU.add),
                 waits=wadd, inc="dve")
            ow[t] = p.op("sync", lambda E, o=self.out[s, t], i=obuf: E.dma_start(out=o, in_=i),
                 waits=p.w("dve"), inc="st", incn=16)


_BUILT = {}


def _get_program(dbg_stage=0, stop_after=0):
    key = (dbg_stage, stop_after)
    if key not in _BUILT:
        _BUILT[key] = KB(dbg_stage, stop_after).build()
    return _BUILT[key]


def kernel(**inputs):
    aux = build_aux()
    nc = _get_program()
    x = np.ascontiguousarray(inputs["L0_attn_x"])
    in_maps = []
    for c in range(8):
        m = {"x": np.ascontiguousarray(x[2 * c:2 * c + 2].reshape(NSEQ, NT, 128, D))}
        for k, v in inputs.items():
            if k != "L0_attn_x":
                m[k] = np.ascontiguousarray(v)
        m.update(aux)
        in_maps.append(m)
    res = run_bass_kernel_spmd(nc, in_maps, core_ids=list(range(8)))
    out = np.concatenate([np.asarray(r["out"]).reshape(NSEQ, S, D) for r in res.results], axis=0)
    return out.astype(np.float32)


# version 06 (final)
# speedup vs baseline: 1.0122x; 1.0030x over previous
import numpy as np
import ml_dtypes
import concourse.bass as bass
import concourse.mybir as mybir
from concourse.bass_utils import run_bass_kernel_spmd
from contextlib import ExitStack

F32 = mybir.dt.float32
BF16 = mybir.dt.bfloat16
AF = mybir.ActivationFunctionType
ALU = mybir.AluOpType
AX = mybir.AxisListType

S = 2048
D = 1024
NT = 16
NSEQ = 2
H = 16
DH = 64
E = 8
DFF = 4096
NDC = 8
LN_EPS = 1e-5
QK_EPS = 1e-6

ENGINES = ("tensor", "scalar", "vector", "gpsimd", "sync")


class Prog:

    def __init__(self, nc, stack):
        self.nc = nc
        self.ops = {e: [] for e in ENGINES}
        self.sems = {}
        self.cnt = {}
        self.stack = stack

    def sem(self, name):
        self.sems[name] = self.stack.enter_context(self.nc.semaphore(name))
        self.cnt[name] = 0

    def op(self, eng, fn, waits=(), inc=None, incn=1):
        waits = [(s, v) for (s, v) in waits if v > 0]
        if inc in ("dmax", "dmaw", "st") and self.cnt[inc] > 0:
            waits = waits + [(inc, self.cnt[inc])]
        self.ops[eng].append((waits, fn, inc, incn))
        if inc is not None:
            self.cnt[inc] += incn
            return self.cnt[inc]
        return None

    def w(self, *names):
        return [(n, self.cnt[n]) for n in names]

    def emit(self, block):
        for eng in ENGINES:
            body = self.ops[eng]
            if not body:
                continue

            def mk(body):
                def closure(Eng):
                    last = {}
                    for waits, fn, inc, incn in body:
                        for s, v in waits:
                            if last.get(s, -1) < v:
                                Eng.wait_ge(self.sems[s], v)
                                last[s] = v
                        ins = fn(Eng) if fn is not None else None
                        if inc is not None:
                            ins.then_inc(self.sems[inc], incn)
                return closure

            getattr(block, eng)(mk(body))


def build_aux():
    bf = ml_dtypes.bfloat16
    aux = {}
    aux["id128_bf"] = np.eye(128, dtype=bf)
    aux["id128_f32"] = np.eye(128, dtype=np.float32)
    idhi = np.zeros((128, 64), dtype=np.float32)
    idhi[64:128] = np.eye(64)
    aux["idhi_bf"] = idhi.astype(bf)
    aux["ones1_bf"] = np.ones((1, 128), dtype=bf)
    aux["ones_f32"] = np.ones((128, 64), dtype=np.float32)
    blocks = np.zeros((128, 2), dtype=np.float32)
    blocks[0:64, 0] = 1.0
    blocks[64:128, 1] = 1.0
    aux["blocks2"] = blocks
    aux["blocks2T"] = np.ascontiguousarray(blocks.T)
    ma = np.zeros((144, 3 * 128), dtype=np.float32)
    for v in range(3):
        jl = np.arange(144)[:, None]
        il = np.arange(128)[None, :]
        ok = (jl >= il) & (jl <= il + 16)
        if v == 0:
            ok = ok & (jl >= 8)
        if v == 2:
            ok = ok & (jl <= 135)
        ma[:, v * 128:(v + 1) * 128] = ok.astype(np.float32)
    aux["maskA_a"] = ma[0:128].astype(bf)
    aux["maskA_b"] = ma[128:144].astype(bf)
    return aux


class KB:
    def __init__(self, dbg_stage=0, stop_after=0, p2_sub=0):
        self.dbg_stage = dbg_stage
        self.stop_after = stop_after
        self.p2_sub = p2_sub

    def mm_group(self, psum_ap, pairs, first_waits=(), inc="pe"):
        p = self.p
        n = len(pairs)
        v = None
        for i, (l, r) in enumerate(pairs):
            wts = list(first_waits) if i == 0 else ()

            def fn(Eng, o=psum_ap, l=l, r=r, st=(i == 0), sp=(i == n - 1)):
                return Eng.matmul(o, l, r, start=st, stop=sp, skip_group_check=True)

            if i == n - 1:
                v = p.op("tensor", fn, waits=wts, inc=inc)
            else:
                p.op("tensor", fn, waits=wts)
        return v

    def ln_stats(self, src_ap, waits_sc=(), waits_ve=()):
        p = self.p
        st = self.STAT
        p.op("scalar", lambda E, s=src_ap: E.activation(self.SQTR, s, AF.Square,
                                                        accum_out=st[:, 1:2]),
             waits=list(waits_sc) + p.w("dve"), inc="act")
        p.op("scalar", lambda E, s=src_ap: E.activation(self.SQTR, s, AF.Identity,
                                                        accum_out=st[:, 0:1]),
             waits=list(waits_sc) + p.w("act"), inc="act")
        p.op("vector", lambda E: E.tensor_scalar(st[:, 2:3], st[:, 0:1], 1.0 / D, None, ALU.mult),
             waits=list(waits_ve) + p.w("act"))
        p.op("vector", lambda E: E.tensor_scalar(st[:, 4:5], st[:, 1:2], 1.0 / D, None, ALU.mult),
             inc="dve")
        p.op("scalar", lambda E: E.activation(st[:, 5:6], st[:, 2:3], AF.Square),
             waits=p.w("dve"), inc="act")
        p.op("vector", lambda E: E.tensor_tensor(st[:, 6:7], st[:, 4:5], st[:, 5:6], ALU.subtract),
             waits=p.w("act"), inc="dve")
        p.op("scalar", lambda E: E.activation(st[:, 5:6], st[:, 6:7], AF.Ln, bias=LN_EPS),
             waits=p.w("dve"), inc="act")
        p.op("scalar", lambda E: E.activation(st[:, 3:4], st[:, 5:6], AF.Exp, scale=-0.5),
             waits=p.w("act"), inc="act")
        return st[:, 2:3], st[:, 3:4]

    def tile_to_fm(self, src_bf, bank, dst, first_waits=(), scale=None, bias=None):
        p = self.p
        pbb = self.pb_bf[bank]
        for dc in range(NDC):
            wts = first_waits if dc == 0 else ()

            def fn(Eng, o=pbb[:, dc * 128:(dc + 1) * 128],
                   i=src_bf[:, dc * 128:(dc + 1) * 128]):
                return Eng.transpose(o, i, self.id128)

            p.op("tensor", fn, waits=wts, inc="pe" if dc == NDC - 1 else None)
        if scale is None:
            p.op("scalar", lambda E: E.copy(dst, pbb.rearrange("p (dc c) -> p dc c", c=128)),
                 waits=p.w("pe"), inc="act")
        else:
            for dc in range(NDC):
                def fn(Eng, o=dst[dc], i=pbb[:, dc * 128:(dc + 1) * 128],
                       sc=scale[dc], bi=bias[dc]):
                    return Eng.activation(o, i, AF.Identity, scale=sc, bias=bi)

                p.op("scalar", fn, waits=p.w("pe") if dc == 0 else (),
                     inc="act" if dc == NDC - 1 else None)

    def build(self):
        nc = bass.Bass("TRN2", target_bir_lowering=False)
        self.nc = nc
        stack = ExitStack()
        self.stack = stack
        p = Prog(nc, stack)
        self.p = p
        for sname in ("pe", "act", "dve", "pool", "dmaw", "dmax", "st",
                      "dx0", "dx1", "dw0", "dw1"):
            p.sem(sname)

        dr = lambda name, shape, dt, kind: nc.dram_tensor(name, shape, dt, kind=kind).ap()
        self.x_in = dr("x", [NSEQ, NT, 128, D], F32, "ExternalInput")
        shp = {"L0_attn_w_qkv": [3 * D, D], "L0_attn_b_qkv": [3 * D],
               "L0_attn_w_out": [D, D], "L0_attn_b_out": [D],
               "L0_ffn_ln1_g": [D], "L0_ffn_ln1_b": [D],
               "L0_ffn_w_qkv": [D, 3 * D], "L0_ffn_b_qkv": [3 * D],
               "L0_ffn_w_o": [D, D], "L0_ffn_b_o": [D],
               "L0_ffn_ln2_g": [D], "L0_ffn_ln2_b": [D],
               "L0_ffn_gate_w": [D, E], "L0_ffn_gate_b": [E],
               "L0_ffn_W1": [E, D, DFF], "L0_ffn_b1": [E, DFF],
               "L0_ffn_W2": [E, DFF, D], "L0_ffn_b2": [E, D]}
        self.win = {n: dr(n, s_, F32, "ExternalInput") for n, s_ in shp.items()}
        aux_shapes = {"id128_bf": ([128, 128], BF16), "id128_f32": ([128, 128], F32),
                      "idhi_bf": ([128, 64], BF16), "ones1_bf": ([1, 128], BF16),
                      "ones_f32": ([128, 64], F32), "blocks2": ([128, 2], F32),
                      "blocks2T": ([2, 128], F32), "maskA_a": ([128, 384], BF16),
                      "maskA_b": ([16, 384], BF16)}
        self.auxin = {n: dr(n, sh, dt, "ExternalInput") for n, (sh, dt) in aux_shapes.items()}
        self.out = dr("out", [NSEQ, NT, 128, D], F32, "ExternalOutput")
        if self.dbg_stage:
            self.dbg = dr("dbg", [NSEQ, NT, 128, D], F32, "ExternalOutput")
        self.x1scr = dr("x1scr", [NSEQ, NT, 128, D], F32, "Internal")
        self.uscr = dr("uscr", [NSEQ, NT, 128, D], F32, "Internal")
        self.waT = dr("waT", [D, 3 * D], BF16, "Internal")
        self.woaT = dr("woaT", [D, D], BF16, "Internal")

        self.pb = [stack.enter_context(nc.psum_tensor(f"pb{i}", [128, 512], F32)).ap()
                   for i in range(8)]
        self.pb_bf = [b.bitcast(BF16) for b in self.pb]

        sb = lambda name, shape, dt: stack.enter_context(nc.sbuf_tensor("s_" + name, shape, dt)).ap()
        self.id128 = sb("id128", [128, 128], BF16)
        self.idf32 = sb("idf32", [128, 128], F32)
        self.idhi = sb("idhi", [128, 64], BF16)
        self.ones1 = sb("ones1", [1, 128], BF16)
        self.onesf = sb("onesf", [128, 64], F32)
        self.blocks2 = sb("blocks2", [128, 2], F32)
        self.blocks2T = sb("blocks2T", [2, 128], F32)
        self.maskAa = sb("maskAa", [128, 384], BF16)
        self.maskAb = sb("maskAb", [16, 384], BF16)
        self.bA_c = sb("bA_c", [128, 24], F32)
        self.lnp_c = sb("lnp_c", [128, 32], F32)
        self.bB_c = sb("bB_c", [128, 16], F32)
        self.b1_c = sb("b1_c", [128, 256], F32)
        self.b2_bf = sb("b2_bf", [8, 1024], BF16)
        self.boutA_row = sb("boutA_row", [1, 1024], BF16)
        self.boB_row = sb("boB_row", [1, 1024], BF16)
        self.bvB_row = sb("bvB_row", [1, 1024], BF16)
        self.gateb_row = sb("gateb_row", [1, 8], BF16)
        self.gatew_bf = sb("gatew_bf", [128, 64], BF16)
        self.gatew_f32 = sb("gatew_f32", [128, 64], F32)
        self.gatew_lo = sb("gatew_lo", [128, 64], BF16)
        self.ew_c = sb("ew_c", [128, 128], F32)
        self.STAT = sb("STAT", [128, 16], F32)
        self.SQT = sb("SQT", [128, 1024], F32)
        self.SQTR = self.SQT
        self.XSTG = [sb(f"XSTG{i}", [128, 1024], F32) for i in range(2)]
        self.USTG = [sb(f"USTG{i}", [128, 1024], F32) for i in range(2)]
        self.ABF = sb("ABF", [128, 1024], BF16)
        self.ewT = sb("ewT", [8, 128], BF16)
        self.ew_bf = sb("ew_bf", [128, 8], BF16)
        self.CONSTV = sb("CONSTV", [128, 8], F32)

        cvals = [LN_EPS, QK_EPS, -0.5, 0.5, -1.0, 0.125, 0.0]
        for j, v in enumerate(cvals):
            p.op("vector", lambda E, j=j, v=v: E.memset(self.CONSTV[:, j:j + 1], v),
                 inc="dve")
            nc.const_aps.register(F32, v, self.CONSTV[:, j:j + 1])
        cbar = p.w("dve")
        for eng in ("tensor", "scalar", "gpsimd", "sync"):
            p.op(eng, None, waits=cbar)

        self.p0_prep()
        for s_ in range(NSEQ):
            self.p1(s_)
            if self.stop_after == 1:
                continue
            self.p2(s_)
            if self.stop_after == 2:
                continue
            self.p3(s_)
        p.op("sync", None, waits=p.w("st", "dmax"))

        with nc.Block() as block:
            p.emit(block)
        stack.close()
        return nc

    def p0_prep(self):
        p, nc = self.p, self.nc
        self._p0stack = ExitStack()
        self.prow24 = self._p0stack.enter_context(nc.sbuf_tensor("s_prow24", [24, 128], F32)).ap()
        self.prow32 = self._p0stack.enter_context(nc.sbuf_tensor("s_prow32", [32, 128], F32)).ap()
        for dst, src in [(self.id128, self.auxin["id128_bf"]), (self.idf32, self.auxin["id128_f32"]),
                         (self.idhi, self.auxin["idhi_bf"]), (self.ones1, self.auxin["ones1_bf"]),
                         (self.onesf, self.auxin["ones_f32"]), (self.blocks2, self.auxin["blocks2"]),
                         (self.blocks2T, self.auxin["blocks2T"]), (self.maskAa, self.auxin["maskA_a"]),
                         (self.maskAb, self.auxin["maskA_b"])]:
            p.op("sync", lambda E, d=dst, s=src: E.dma_start(out=d, in_=s), inc="dmax", incn=16)
        for dst, src in [(self.b2_bf, self.win["L0_ffn_b2"]),
                         (self.boutA_row, self.win["L0_attn_b_out"].rearrange("(o d) -> o d", o=1)),
                         (self.boB_row, self.win["L0_ffn_b_o"].rearrange("(o d) -> o d", o=1)),
                         (self.bvB_row, self.win["L0_ffn_b_qkv"].rearrange("(o d) -> o d", o=1)[:, 2048:3072]),
                         (self.gateb_row, self.win["L0_ffn_gate_b"].rearrange("(o d) -> o d", o=1)),
                         (self.gatew_bf.rearrange("p (dc e) -> p dc e", e=8),
                          self.win["L0_ffn_gate_w"].rearrange("(dc p) e -> p dc e", p=128))]:
            p.op("gpsimd", lambda E, d=dst, s=src: E.dma_start(out=d, in_=s,
                                                               allow_slow_non_contiguous=True),
                 inc="dmaw", incn=16)
        p.op("sync", lambda E: E.dma_start(
            out=self.gatew_f32.rearrange("p (dc e) -> p dc e", e=8),
            in_=self.win["L0_ffn_gate_w"].rearrange("(dc p) e -> p dc e", p=128),
            allow_slow_non_contiguous=True), inc="dmax", incn=16)
        p.op("vector", lambda E: E.tensor_tensor(self.gatew_lo, self.gatew_f32,
                                                 self.gatew_bf, ALU.subtract),
             waits=p.w("dmax", "dmaw"), inc="dve")

        p.op("sync", lambda E: E.dma_start(
            out=self.prow24, in_=self.win["L0_attn_b_qkv"].rearrange("(c k) -> c k", k=128)),
             inc="dmax", incn=16)
        p.op("tensor", lambda E: E.transpose(self.pb[1][:, 0:24], self.prow24, self.idf32[0:24, 0:24]),
             waits=p.w("dmax"), inc="pe")
        p.op("scalar", lambda E: E.copy(self.bA_c, self.pb[1][:, 0:24]), waits=p.w("pe"), inc="act")
        for i, nm in enumerate(["L0_ffn_ln1_g", "L0_ffn_ln1_b", "L0_ffn_ln2_g", "L0_ffn_ln2_b"]):
            p.op("sync", lambda E, i=i, nm=nm: E.dma_start(
                out=self.prow32[i * 8:(i + 1) * 8, :],
                in_=self.win[nm].rearrange("(c k) -> c k", k=128)), inc="dmax", incn=16)
        p.op("tensor", lambda E: E.transpose(self.pb[1][:, 32:64], self.prow32, self.idf32[0:32, 0:32]),
             waits=p.w("dmax", "act"), inc="pe")
        p.op("scalar", lambda E: E.copy(self.lnp_c, self.pb[1][:, 32:64]), waits=p.w("pe"), inc="act")
        p.op("sync", lambda E: E.dma_start(
            out=self.prow24[0:16, :],
            in_=self.win["L0_ffn_b_qkv"].rearrange("(c k) -> c k", k=128)[0:16]),
             waits=p.w("act"), inc="dmax", incn=16)
        p.op("tensor", lambda E: E.transpose(self.pb[1][:, 64:80], self.prow24[0:16, :],
                                             self.idf32[0:16, 0:16]),
             waits=p.w("dmax", "act"), inc="pe")
        p.op("scalar", lambda E: E.copy(self.bB_c, self.pb[1][:, 64:80]), waits=p.w("pe"), inc="act")
        for e in range(E):
            p.op("sync", lambda E_, e=e: E_.dma_start(
                out=self.prow32, in_=self.win["L0_ffn_b1"][e].rearrange("(c k) -> c k", k=128)),
                 waits=p.w("act"), inc="dmax", incn=16)
            p.op("tensor", lambda E_: E_.transpose(self.pb[1][:, 96:128], self.prow32,
                                                   self.idf32[0:32, 0:32]),
                 waits=p.w("dmax", "act"), inc="pe")
            p.op("scalar", lambda E_, e=e: E_.copy(self.b1_c[:, e * 32:(e + 1) * 32],
                                                   self.pb[1][:, 96:128]),
                 waits=p.w("pe"), inc="act")

        with ExitStack() as ps:
            wstg = ps.enter_context(self.nc.sbuf_tensor("p0wstg", [128, 1024], BF16)).ap()
            wstg2 = ps.enter_context(self.nc.sbuf_tensor("p0wstg2", [128, 1024], BF16)).ap()

            for (wsrc, wdst, nog) in [(self.win["L0_attn_w_qkv"], self.waT, 24),
                                      (self.win["L0_attn_w_out"], self.woaT, 8)]:
                for og in range(nog):
                    p.op("gpsimd", lambda E, s=wsrc[og * 128:(og + 1) * 128, :]:
                         E.dma_start(out=wstg, in_=s),
                         waits=p.w("pe"), inc="dmaw", incn=16)
                    for dc in range(NDC):
                        def fn(Eng, dc=dc):
                            return Eng.transpose(self.pb_bf[0][:, dc * 128:(dc + 1) * 128],
                                                 wstg[:, dc * 128:(dc + 1) * 128], self.id128)
                        p.op("tensor", fn,
                             waits=p.w("dmaw", "act") if dc == 0 else (),
                             inc="pe" if dc == NDC - 1 else None)
                    p.op("scalar", lambda E: E.copy(wstg2, self.pb_bf[0][:]),
                         waits=p.w("pe", "dmax"), inc="act")
                    p.op("sync", lambda E, og=og, wdst=wdst: E.dma_start(
                        out=wdst.rearrange("(dc p) o -> p dc o", p=128)[:, :, og * 128:(og + 1) * 128],
                        in_=wstg2.rearrange("p (dc o) -> p dc o", o=128)),
                         waits=p.w("act"), inc="dmax", incn=16)
            p.op("sync", None, waits=p.w("dmax"))
            p.op("gpsimd", None, waits=p.w("dmax"))
        self._p0stack.close()

    def p1(self, s):
        p, nc = self.p, self.nc
        KAW = 1048
        VFW = 1160
        with ExitStack() as ps:
            sb = lambda name, shape, dt: ps.enter_context(
                nc.sbuf_tensor(f"{name}_s{s}", shape, dt)).ap()
            A_fm = sb("A_fm", [128, NDC * 2048], BF16)
            KA = sb("KA", [128, 8 * 2 * KAW], BF16)
            VA = sb("VA", [128, 16 * 2 * 9 * 65], BF16)
            WQA = sb("WQA", [128, 8192], BF16)
            WOA = sb("WOA", [128, 8192], BF16)
            WBUF = [sb(f"WBUF{i}", [128, 1024], BF16) for i in range(2)]
            VSTG = sb("VSTG", [128, 2 * VFW], BF16)
            K2 = sb("K2", [128, 512], F32)
            FAC = sb("FAC", [2, 512], F32)
            QBUF = sb("QBUF", [128, 1024], BF16)
            Q2ALL = sb("Q2ALL", [128, 1024], F32)
            QFA = Q2ALL[0:2, :]
            PTA = [sb(f"PTA{i}", [128, 128], BF16) for i in range(2)]
            PTB = [sb(f"PTB{i}", [16, 128], BF16) for i in range(2)]
            OTM2 = [sb(f"OTM{i}", [128, 1024], BF16) for i in range(2)]
            OFM = sb("OFM", [128, 1024], BF16)
            ASTG = sb("ASTG", [128, 1024], F32)
            X1STG = [sb(f"X1STG{i}", [128, 1024], F32) for i in range(2)]
            RD = [sb(f"RD{i}", [128, 1], F32) for i in range(2)]

            kaV = KA.rearrange("p (oc par c) -> p oc par c", par=2, c=KAW)
            vaV = VA.rearrange("p (t c) -> p t c", c=65)
            afmV = A_fm.rearrange("p (dc t) -> p dc t", t=2048)
            wqaV = WQA.rearrange("p (dc o) -> p dc o", o=1024)
            woaV = WOA.rearrange("p (dc o) -> p dc o", o=1024)

            p.op("gpsimd", lambda E: E.dma_start(
                out=wqaV, in_=self.waT.rearrange("(dc p) o -> p dc o", p=128)[:, :, 0:1024]),
                 waits=p.w("dmax"), inc="dmaw", incn=16)
            p.op("gpsimd", lambda E: E.dma_start(
                out=woaV, in_=self.woaT.rearrange("(dc p) o -> p dc o", p=128)),
                 inc="dmaw", incn=16)

            consumed = {}
            xv = {}
            for t in range(NT):
                wts = [("dve", consumed[t - 2])] if t - 2 in consumed else ()
                xv[t] = p.op("sync", lambda E, t=t, b=self.XSTG[t % 2]:
                             E.dma_start(out=b, in_=self.x_in[s, t]),
                             waits=wts, inc=f"dx{t % 2}", incn=16)
                buf = self.XSTG[t % 2]
                m, rstd = self.ln_stats(buf, waits_sc=[(f"dx{t % 2}", xv[t])],
                                        waits_ve=[(f"dx{t % 2}", xv[t])])
                consumed[t] = p.op("vector", lambda E, b=buf, m=m, r=rstd:
                                   E.tensor_scalar(self.ABF, b, m, r, ALU.subtract, ALU.mult),
                                   waits=p.w("act"), inc="dve")
                if self.dbg_stage == 11:
                    p.op("vector", lambda E: E.tensor_copy(self.SQT, self.ABF), inc="dve")
                    p.op("sync", lambda E, t=t: E.dma_start(out=self.dbg[s, t], in_=self.SQT),
                         waits=p.w("dve"), inc="st", incn=16)
                    p.op("vector", None, waits=p.w("st"))
                self.tile_to_fm(self.ABF, 0, afmV[:, :, t * 128:(t + 1) * 128],
                                first_waits=p.w("dve", "act"))

            p.op("vector", lambda E: E.memset(kaV[:, :, :, 0:8], 0.0), inc="dve")
            p.op("vector", lambda E: E.memset(kaV[:, :, :, 8 + 1024:KAW], 0.0), inc="dve")
            p.op("vector", lambda E: E.memset(vaV[:, :, 64:65], 1.0), inc="dve")

            self.p1_evac = {}
            for oc in range(16):
                og = 8 + oc
                wbuf = WBUF[oc % 2]
                ld = p.op("gpsimd", lambda E, og=og, w=wbuf: E.dma_start(
                    out=w.rearrange("p (dc o) -> p dc o", o=128),
                    in_=self.waT.rearrange("(dc p) o -> p dc o", p=128)[:, :, og * 128:(og + 1) * 128]),
                     waits=[("pe", p.cnt["pe"])], inc=f"dw{oc % 2}", incn=16)
                is_v = oc >= 8
                if is_v:
                    vz = p.op("vector", lambda E: E.memset(VSTG, 0.0), waits=p.w("pe", "dve"),
                              inc="dve")
                for tc in range(4):
                    bank = 1 + (tc % 2)
                    pairs = [(wbuf.rearrange("p (dc o) -> p dc o", o=128)[:, dc, :],
                              afmV[:, dc, tc * 512:(tc + 1) * 512]) for dc in range(NDC)]
                    key = (oc, tc - 2) if tc >= 2 else (oc - 1, tc + 2)
                    wts = [(f"dw{oc % 2}", ld)]
                    if key in self.p1_evac:
                        wts.append(("act", self.p1_evac[key]))
                    else:
                        wts += p.w("act", "dve")
                    self.mm_group(self.pb[bank], pairs, first_waits=wts)
                    psum = self.pb[bank]
                    bias = self.bA_c[:, og:og + 1]
                    if not is_v:
                        p.op("scalar", lambda E, ps_=psum, b=bias: E.activation(
                            K2, ps_, AF.Square, bias=b), waits=p.w("pe"), inc="act")
                        kslc = kaV[:, oc, :, 8 + tc * 256: 8 + (tc + 1) * 256]
                        pv = psum.rearrange("p (j par) -> p par j", par=2)
                        self.p1_evac[(oc, tc)] = p.op("scalar", lambda E, o=kslc, i=pv, b=bias:
                                                      E.activation(o, i, AF.Identity, bias=b),
                                                      inc="act")
                        p.op("tensor", lambda E: E.matmul(self.pb[3][0:2, :], self.blocks2, K2,
                                                          start=True, stop=True,
                                                          skip_group_check=True),
                             waits=p.w("act"), inc="pe")
                        p.op("scalar", lambda E: E.activation(FAC, self.pb[3][0:2, :], AF.Ln),
                             waits=p.w("pe"), inc="act")
                        p.op("scalar", lambda E: E.activation(FAC, FAC, AF.Exp, scale=0.5),
                             waits=p.w("act"), inc="act")
                        p.op("scalar", lambda E: E.activation(FAC, FAC, AF.Ln, bias=QK_EPS),
                             waits=p.w("act"), inc="act")
                        p.op("scalar", lambda E: E.activation(FAC, FAC, AF.Exp, scale=-1.0),
                             waits=p.w("act"), inc="act")
                        p.op("tensor", lambda E: E.matmul(self.pb[4], self.blocks2T, FAC,
                                                          start=True, stop=True,
                                                          skip_group_check=True),
                             waits=p.w("act", "dve"), inc="pe")
                        pbv = self.pb[4].rearrange("p (j par) -> p par j", par=2)
                        p.op("vector", lambda E, o=kslc, i1=pbv: E.tensor_tensor(o, o, i1, ALU.mult),
                             waits=p.w("pe"), inc="dve")
                    else:
                        vslc = VSTG.rearrange("p (par c) -> p par c", par=2)[
                            :, :, 8 + tc * 256: 8 + (tc + 1) * 256]
                        pv = psum.rearrange("p (j par) -> p par j", par=2)
                        self.p1_evac[(oc, tc)] = p.op("scalar", lambda E, o=vslc, i=pv, b=bias:
                                                      E.activation(o, i, AF.Identity, bias=b),
                                                      waits=p.w("pe") + [("dve", vz)],
                                                      inc="act")
                if is_v:
                    hpair = oc - 8
                    for hh in range(2):
                        h = 2 * hpair + hh
                        base = 64 * hh
                        ident = self.id128[0:64, 0:64] if hh == 0 else self.idhi[64:128, :]
                        for par in range(2):
                            for r in range(9):
                                ti = (h * 2 + par) * 9 + r
                                src = VSTG[base:base + 64,
                                           par * VFW + r * 128: par * VFW + (r + 1) * 128]
                                nb = (par * 9 + r) % 2
                                p.op("tensor", lambda E, o=self.pb_bf[5 + nb][:, 0:64], i=src,
                                     idt=ident: E.transpose(o, i, idt),
                                     waits=p.w("act", "dve"), inc="pe")
                                p.op("scalar", lambda E, o=vaV[:, ti, 0:64],
                                     i=self.pb_bf[5 + nb][:, 0:64]: E.copy(o, i),
                                     waits=p.w("pe"), inc="act")

            self.p1_astg_evac = 0
            self.p1_ttd = 0
            self.p1_tev = 0
            x1w = {}
            for par in range(2):
                for r in range(8):
                    xsrc = self.x_in[s].rearrange("t p d -> (t p) d")[par::2].rearrange(
                        "(r p) d -> r p d", p=128)[r]
                    xbuf = self.XSTG[(par * 8 + r) % 2]
                    px = (par * 8 + r) % 2
                    xv1 = p.op("sync", lambda E, o=xbuf, i=xsrc: E.dma_start(out=o, in_=i),
                               waits=p.w("dve"), inc=f"dx{px}", incn=16)
                    qev = {}
                    for oc in range(8):
                        bank = 1 + (oc % 2)
                        if oc >= 2:
                            wts = [("act", qev[oc - 2])]
                        else:
                            wts = [("act", self.p1_astg_evac)] if self.p1_astg_evac else \
                                  p.w("act", "dmaw", "dve")
                        pairs = [(wqaV[:, dc, oc * 128:(oc + 1) * 128],
                                  afmV[:, dc, 256 * r + par: 256 * r + par + 255: 2])
                                 for dc in range(NDC)]
                        g = self.mm_group(self.pb[bank][:, 0:128], pairs, first_waits=wts)
                        bias = self.bA_c[:, oc:oc + 1]
                        p.op("scalar", lambda E, oc=oc, b=bias, bank=bank: E.activation(
                            QBUF[:, oc * 128:(oc + 1) * 128], self.pb[bank][:, 0:128],
                            AF.Identity, bias=b), waits=[("pe", g)], inc="act")
                        qev[oc] = p.op("scalar", lambda E, oc=oc, b=bias, bank=bank: E.activation(
                            Q2ALL[:, oc * 128:(oc + 1) * 128], self.pb[bank][:, 0:128],
                            AF.Square, bias=b), inc="act")
                    ssq = []
                    for hf in range(2):
                        bank = 1 + hf
                        ssq.append(p.op("tensor", lambda E, hf=hf, bank=bank: E.matmul(
                            self.pb[bank][0:2, :], self.blocks2, Q2ALL[:, hf * 512:(hf + 1) * 512],
                            start=True, stop=True, skip_group_check=True),
                             waits=[("act", qev[7])], inc="pe"))
                    for hf in range(2):
                        p.op("scalar", lambda E, hf=hf, bank=1 + hf: E.copy(
                            QFA[:, hf * 512:(hf + 1) * 512], self.pb[bank][0:2, :]),
                             waits=[("pe", ssq[1])], inc="act")
                    p.op("scalar", lambda E: E.activation(QFA, QFA, AF.Ln), waits=p.w("act"),
                         inc="act")
                    p.op("scalar", lambda E: E.activation(QFA, QFA, AF.Exp, scale=0.5),
                         waits=p.w("act"), inc="act")
                    p.op("scalar", lambda E: E.activation(QFA, QFA, AF.Ln, bias=QK_EPS),
                         waits=p.w("act"), inc="act")
                    fv = p.op("scalar", lambda E: E.activation(QFA, QFA, AF.Exp, scale=-1.0),
                              waits=p.w("act"), inc="act")
                    qmul = {}
                    for hf in range(2):
                        bank = 1 + hf
                        g3 = p.op("tensor", lambda E, hf=hf, bank=bank: E.matmul(
                            self.pb[bank], self.blocks2T, QFA[:, hf * 512:(hf + 1) * 512],
                            start=True, stop=True, skip_group_check=True),
                             waits=[("act", fv)], inc="pe")
                        qmul[hf] = p.op("vector", lambda E, hf=hf, bank=bank: E.tensor_tensor(
                            QBUF[:, hf * 512:(hf + 1) * 512], QBUF[:, hf * 512:(hf + 1) * 512],
                            self.pb[bank], ALU.mult), waits=[("pe", g3)], inc="dve")
                    mv = 0 if r == 0 else (2 if r == 7 else 1)
                    bs = {0: (5, 6, 7), 1: (0, 3, 4)}
                    sdn = {}
                    edn = {}
                    mdn = {}
                    pvd = {}
                    odn = {}

                    def emit_S(h):
                        oc, hh = h // 2, h % 2
                        base = 64 * hh
                        bA, bB, bO = bs[h % 2]
                        kwin = kaV[base:base + 64, oc, par, r * 128: r * 128 + 144]
                        q_ap = QBUF[base:base + 64, oc * 128:(oc + 1) * 128]
                        wts = ([("act", edn[h - 2])] if h - 2 in edn
                               else [("dve", qmul[1]), ("act", self.p1_tev)])
                        p.op("tensor", lambda E, k=kwin[:, 0:128], q=q_ap, bA=bA: E.matmul(
                            self.pb[bA][:, 0:128], k, q, start=True, stop=True,
                            skip_group_check=True), waits=wts, inc="pe")
                        sdn[h] = p.op("tensor", lambda E, k=kwin[:, 128:144], q=q_ap, bB=bB:
                                      E.matmul(self.pb[bB][0:16, 0:128], k, q, start=True,
                                               stop=True, skip_group_check=True), inc="pe")

                    def emit_tail(h):
                        bA, bB, bO = bs[h % 2]
                        wts = [("pe", pvd[h - 2] if h - 2 in pvd else sdn[h])]
                        if h - 2 in odn:
                            wts.append(("dve", mdn[h - 2]))
                        p.op("scalar", lambda E, bA=bA, ptab=PTA[h % 2]: E.activation(
                            ptab, self.pb[bA][:, 0:128], AF.Exp),
                             waits=wts + ([("act", odn[h - 2])] if h - 2 in odn else []),
                             inc="act")
                        ev = p.op("scalar", lambda E, bB=bB, ptbb=PTB[h % 2]: E.activation(
                            ptbb, self.pb[bB][0:16, 0:128], AF.Exp), inc="act")
                        edn[h] = ev
                        p.op("vector", lambda E, mv=mv, ptab=PTA[h % 2]: E.tensor_tensor(
                            ptab, ptab, self.maskAa[:, mv * 128:(mv + 1) * 128], ALU.mult),
                             waits=[("act", ev)], inc="dve")
                        mdn[h] = p.op("vector", lambda E, mv=mv, ptbb=PTB[h % 2]: E.tensor_tensor(
                            ptbb, ptbb, self.maskAb[:, mv * 128:(mv + 1) * 128], ALU.mult),
                             inc="dve")
                        ti = (h * 2 + par) * 9 + r
                        wpv = [("dve", mdn[h])]
                        if h - 2 in pvd:
                            wpv.append(("dve", odn_rd[h - 2]))
                        p.op("tensor", lambda E, ti=ti, bO=bO, ptab=PTA[h % 2]: E.matmul(
                            self.pb[bO][:, 0:65], ptab, vaV[:, ti, :], start=True, stop=False,
                            skip_group_check=True),
                             waits=wpv + ([("act", odn[h - 2])] if h - 2 in odn else []))
                        pvd[h] = p.op("tensor", lambda E, ti=ti, bO=bO, ptbb=PTB[h % 2]: E.matmul(
                            self.pb[bO][:, 0:65], ptbb, vaV[0:16, ti + 1, :], start=False,
                            stop=True, skip_group_check=True), inc="pe")
                        if self.dbg_stage == 15 and par == 0 and r == 0 and h == 0:
                            p.op("scalar", lambda E, bO=bO: E.copy(self.SQT[:, 0:65],
                                                                   self.pb[bO][:, 0:65]),
                                 waits=[("pe", pvd[h])], inc="act")
                            p.op("sync", lambda E: E.dma_start(out=self.dbg[s, 8], in_=self.SQT),
                                 waits=p.w("act"), inc="st", incn=16)
                            p.op("scalar", None, waits=p.w("st"))
                        odn_rd[h] = p.op("vector", lambda E, bO=bO, rd=RD[h % 2]: E.reciprocal(
                            rd, self.pb[bO][:, 64:65]), waits=[("pe", pvd[h])], inc="dve")
                        wo = [("dve", odn_rd[h])]
                        if h == 0 and self.p1_ttd:
                            wo.append(("pe", self.p1_ttd))
                        odn[h] = p.op("scalar", lambda E, h=h, bO=bO, rd=RD[h % 2],
                                      otm=OTM2[(par * 8 + r) % 2]:
                                      E.activation(otm[:, h * 64:(h + 1) * 64],
                                                   self.pb[bO][:, 0:64], AF.Identity, scale=rd),
                                      waits=wo, inc="act")

                    odn_rd = {}
                    emit_S(0)
                    emit_S(1)
                    for h in range(H):
                        emit_tail(h)
                        if h + 2 < H:
                            emit_S(h + 2)
                    if self.dbg_stage == 16 and par == 0 and r == 0:
                        p.op("vector", lambda E, otm=OTM2[0]: E.tensor_copy(self.SQT, otm),
                             waits=[("act", odn[H - 1])], inc="dve")
                        p.op("sync", lambda E: E.dma_start(out=self.dbg[s, 9], in_=self.SQT),
                             waits=p.w("dve"), inc="st", incn=16)
                        p.op("vector", None, waits=p.w("st"))
                    self.tile_to_fm(OTM2[(par * 8 + r) % 2], 7,
                                    OFM.rearrange("p (dc c) -> p dc c", c=128),
                                    first_waits=[("act", odn[H - 1])])
                    self.p1_ttd = p.cnt["pe"]
                    self.p1_tev = p.cnt["act"]
                    ofmV = OFM.rearrange("p (dc o) -> p dc o", o=128)
                    for nh in range(2):
                        bank = 1 + nh
                        pairs = [(ofmV[:, dc, :], woaV[:, dc, nh * 512:(nh + 1) * 512])
                                 for dc in range(NDC)]
                        self.mm_group(self.pb[bank], pairs, first_waits=p.w("act", "dmaw"))
                        p.op("tensor", lambda E, nh=nh, bank=bank: E.matmul(
                            self.pb[bank], self.ones1, self.boutA_row[:, nh * 512:(nh + 1) * 512],
                            start=False, stop=True, skip_group_check=True), inc="pe")
                        self.p1_astg_evac = p.op("scalar", lambda E, nh=nh, bank=bank: E.copy(
                            ASTG[:, nh * 512:(nh + 1) * 512], self.pb[bank]),
                             waits=p.w("pe", "dve"), inc="act")
                    if self.dbg_stage == 13:
                        ddst = self.dbg[s].rearrange("t p d -> (t p) d")[par::2].rearrange(
                            "(r p) d -> r p d", p=128)[r]
                        p.op("sync", lambda E, o=ddst: E.dma_start(out=o, in_=ASTG),
                             waits=p.w("act"), inc="st", incn=16)
                        p.op("scalar", None, waits=p.w("st"))
                    m, rstd = self.ln_stats(ASTG, waits_sc=p.w("act"), waits_ve=p.w("act"))
                    x1b = X1STG[(par * 8 + r) % 2]
                    nv = p.op("vector", lambda E, m=m, rs=rstd: E.tensor_scalar(
                        self.SQT, ASTG, m, rs, ALU.subtract, ALU.mult), waits=p.w("act"),
                        inc="dve")
                    ti_ = par * 8 + r
                    wadd = [(f"dx{px}", xv1), ("dve", nv)]
                    if ti_ - 2 in x1w:
                        wadd.append(("dmax", x1w[ti_ - 2]))
                    p.op("vector", lambda E, o=x1b, xb=xbuf: E.tensor_tensor(o, xb, self.SQT,
                                                                             ALU.add),
                         waits=wadd, inc="dve")
                    xdst = self.x1scr[s].rearrange("t p d -> (t p) d")[par::2].rearrange(
                        "(r p) d -> r p d", p=128)[r]
                    x1w[ti_] = p.op("sync", lambda E, o=xdst, i=x1b: E.dma_start(out=o, in_=i),
                         waits=p.w("dve"), inc="dmax", incn=16)
                    if self.dbg_stage == 1:
                        ddst = self.dbg[s].rearrange("t p d -> (t p) d")[par::2].rearrange(
                            "(r p) d -> r p d", p=128)[r]
                        p.op("sync", lambda E, o=ddst, i=x1b: E.dma_start(out=o, in_=i),
                             inc="st", incn=16)
            p.op("sync", None, waits=p.w("dmax"))
            p.op("gpsimd", None, waits=p.w("dmax", "dve", "act", "pe"))

    def p2(self, s):
        p, nc = self.p, self.nc
        with ExitStack() as ps:
            sb = lambda name, shape, dt: ps.enter_context(
                nc.sbuf_tensor(f"{name}_s{s}", shape, dt)).ap()
            HFM = sb("HFM", [128, NDC * 2048], BF16)
            KBc = sb("KBc", [128, 8 * 2048], BF16)
            VBc = sb("VBc", [128, 16 * 16 * 65], BF16)
            WB2 = [sb(f"WB2{i}", [128, 4096], BF16) for i in range(2)]
            WOB = sb("WOB", [128, 8192], BF16)
            QB = sb("QB", [128, 256], BF16)
            PT2 = [sb(f"PT2{i}", [128, 256], BF16) for i in range(2)]
            OFM2 = sb("OFM2", [128, 2048], BF16)
            BCS = sb("BCS", [64, 256], F32)
            DSC = sb("DSC", [128, 256], F32)
            hfmV = HFM.rearrange("p (dc t) -> p dc t", t=2048)
            kbV = KBc.rearrange("p (oc t) -> p oc t", t=2048)
            vbV = VBc.rearrange("p (h tt c) -> p h tt c", tt=16, c=65)
            wobV = WOB.rearrange("p (dc o) -> p dc o", o=1024)
            wqkvB = self.win["L0_ffn_w_qkv"]

            p.op("vector", lambda E: E.memset(vbV[:, :, :, 64:65], 1.0), inc="dve")
            p.op("gpsimd", lambda E: E.dma_start(
                out=wobV, in_=self.win["L0_ffn_w_o"].rearrange("(dc p) o -> p dc o", p=128)),
                 inc="dmaw", incn=16)

            self.p2_uw = {}
            consumed = {}
            for t in range(NT):
                wts = ([("dve", consumed[t - 2])] if t - 2 in consumed else p.w("dmax"))
                xv = p.op("sync", lambda E, o=self.XSTG[t % 2], i=self.x1scr[s, t]:
                          E.dma_start(out=o, in_=i), waits=wts, inc=f"dx{t % 2}", incn=16)
                xbuf = self.XSTG[t % 2]
                m, rstd = self.ln_stats(xbuf, waits_sc=[(f"dx{t % 2}", xv)],
                                        waits_ve=[(f"dx{t % 2}", xv)])
                ubuf = self.USTG[t % 2]
                wnu = p.w("act")
                if t - 2 in self.p2_uw:
                    wnu = wnu + [("dmax", self.p2_uw[t - 2])]
                p.op("vector", lambda E, b=xbuf, m=m, rs=rstd, u=ubuf: E.tensor_scalar(
                    u, b, m, rs, ALU.subtract, ALU.mult), waits=wnu, inc="dve")
                self.p2_uw[t] = p.op("sync", lambda E, u=ubuf, t=t: E.dma_start(
                    out=self.uscr[s, t], in_=u), waits=p.w("dve"), inc="dmax", incn=16)
                if self.dbg_stage == 21:
                    p.op("sync", lambda E, u=ubuf, t=t: E.dma_start(out=self.dbg[s, t], in_=u),
                         inc="st", incn=16)
                    p.op("vector", None, waits=p.w("st"))
                st2 = self.STAT
                p.op("vector", lambda E: E.tensor_tensor(st2[:, 7:8], st2[:, 3:4], st2[:, 3:4],
                                                         ALU.mult), waits=p.w("act"), inc="dve")
                p.op("vector", lambda E: E.tensor_tensor(st2[:, 7:8], st2[:, 7:8], st2[:, 6:7],
                                                         ALU.mult), waits=p.w("dve"), inc="dve")
                p.op("scalar", lambda E: E.activation(st2[:, 7:8], st2[:, 7:8], AF.Ln,
                                                      bias=LN_EPS), waits=p.w("dve"), inc="act")
                p.op("scalar", lambda E: E.activation(st2[:, 7:8], st2[:, 7:8], AF.Exp,
                                                      scale=-0.5), waits=p.w("act"), inc="act")
                rs2v = p.op("vector", lambda E: E.tensor_tensor(st2[:, 8:9], st2[:, 7:8],
                                                                st2[:, 3:4], ALU.mult),
                            waits=p.w("act"), inc="dve")
                consumed[t] = p.op("vector", lambda E, b=xbuf, m=m: E.tensor_scalar(
                    self.ABF, b, m, st2[:, 8:9], ALU.subtract, ALU.mult),
                    waits=[("dve", rs2v)], inc="dve")
                scale = [self.lnp_c[:, dc:dc + 1] for dc in range(8)]
                bias = [self.lnp_c[:, 8 + dc:8 + dc + 1] for dc in range(8)]
                self.tile_to_fm(self.ABF, 0,
                                [hfmV[:, dc, t * 128:(t + 1) * 128] for dc in range(NDC)],
                                scale=scale, bias=bias, first_waits=p.w("dve", "act"))

            if self.p2_sub == 1:
                p.op("sync", None, waits=p.w("dmax"))
                p.op("gpsimd", None, waits=p.w("dmax", "dve", "act", "pe"))
                return
            for oc in range(8):
                wbuf = WB2[oc % 2]
                ld = p.op("gpsimd", lambda E, oc=oc, w=wbuf: E.dma_start(
                    out=w[:, 0:1024].rearrange("p (dc o) -> p dc o", o=128),
                    in_=wqkvB.rearrange("(dc p) o -> p dc o", p=128)[
                        :, :, 1024 + oc * 128:1024 + (oc + 1) * 128]),
                     waits=[("pe", p.cnt["pe"])], inc=f"dw{oc % 2}", incn=16)
                for tc in range(4):
                    bank = 1 + (tc % 2)
                    pairs = [(wbuf[:, 0:1024].rearrange("p (dc o) -> p dc o", o=128)[:, dc, :],
                              hfmV[:, dc, tc * 512:(tc + 1) * 512]) for dc in range(NDC)]
                    self.mm_group(self.pb[bank], pairs,
                                  first_waits=[(f"dw{oc % 2}", ld)] + p.w("act"))
                    p.op("scalar", lambda E, oc=oc, tc=tc, bank=bank: E.activation(
                        kbV[:, oc, tc * 512:(tc + 1) * 512], self.pb[bank], AF.Identity,
                        bias=self.bB_c[:, 8 + oc:8 + oc + 1]), waits=p.w("pe"), inc="act")
            for vp in range(2):
                wbuf = WB2[vp % 2]
                ld = p.op("gpsimd", lambda E, vp=vp, w=wbuf: E.dma_start(
                    out=w.rearrange("p (dc o) -> p dc o", o=512),
                    in_=wqkvB.rearrange("(dc p) o -> p dc o", p=128)[
                        :, :, 2048 + vp * 512:2048 + (vp + 1) * 512]),
                     waits=[("pe", p.cnt["pe"])], inc=f"dw{vp % 2}", incn=16)
                for t in range(NT):
                    bank = 1 + (t % 2)
                    pairs = [(hfmV[:, dc, t * 128:(t + 1) * 128],
                              wbuf.rearrange("p (dc o) -> p dc o", o=512)[:, dc, :])
                             for dc in range(NDC)]
                    self.mm_group(self.pb[bank], pairs,
                                  first_waits=[(f"dw{vp % 2}", ld)] + p.w("act"))
                    p.op("tensor", lambda E, vp=vp, bank=bank: E.matmul(
                        self.pb[bank], self.ones1, self.bvB_row[:, vp * 512:(vp + 1) * 512],
                        start=False, stop=True, skip_group_check=True), inc="pe")
                    dst = vbV[:, vp * 8:(vp + 1) * 8, t, 0:64]
                    src = self.pb[bank].rearrange("p (h c) -> p h c", c=64)
                    p.op("scalar", lambda E, o=dst, i=src: E.copy(o, i), waits=p.w("pe"),
                         inc="act")

            if self.dbg_stage == 28:
                for i in range(2):
                    p.op("vector", lambda E, i=i: E.tensor_copy(
                        self.SQT, hfmV[:, 0, i * 1024:(i + 1) * 1024]),
                         waits=p.w("act"), inc="dve")
                    p.op("sync", lambda E, i=i: E.dma_start(out=self.dbg[s, i], in_=self.SQT),
                         waits=p.w("dve"), inc="st", incn=16)
                    p.op("vector", None, waits=p.w("st"))
                    p.op("vector", lambda E, i=i: E.tensor_copy(
                        self.SQT, kbV[:, 0, i * 1024:(i + 1) * 1024]),
                         inc="dve")
                    p.op("sync", lambda E, i=i: E.dma_start(out=self.dbg[s, 2 + i], in_=self.SQT),
                         waits=p.w("dve"), inc="st", incn=16)
                    p.op("vector", None, waits=p.w("st"))
            if self.p2_sub == 2:
                p.op("sync", None, waits=p.w("dmax"))
                p.op("gpsimd", None, waits=p.w("dmax", "dve", "act", "pe"))
                return
            for qg in range(8):
                ts0 = qg * 2
                uv = []
                for i in range(2):
                    uv.append(p.op("sync", lambda E, i=i, ts0=ts0: E.dma_start(
                        out=self.USTG[i], in_=self.uscr[s, ts0 + i]),
                         waits=p.w("dve", "dmax"), inc=f"dx{i}", incn=16))
                for oc in range(8):
                    wbuf = WB2[oc % 2]
                    ld = p.op("gpsimd", lambda E, oc=oc, w=wbuf: E.dma_start(
                        out=w[:, 0:1024].rearrange("p (dc o) -> p dc o", o=128),
                        in_=wqkvB.rearrange("(dc p) o -> p dc o", p=128)[
                            :, :, oc * 128:(oc + 1) * 128]),
                         waits=[("pe", p.cnt["pe"])], inc=f"dw{oc % 2}", incn=16)
                    pairs = [(wbuf[:, 0:1024].rearrange("p (dc o) -> p dc o", o=128)[:, dc, :],
                              hfmV[:, dc, qg * 256:(qg + 1) * 256]) for dc in range(NDC)]
                    self.mm_group(self.pb[5][:, 0:256], pairs,
                                  first_waits=[(f"dw{oc % 2}", ld)] + p.w("dve", "act"))
                    p.op("scalar", lambda E, oc=oc: E.activation(
                        QB, self.pb[5][:, 0:256], AF.Identity, bias=self.bB_c[:, oc:oc + 1]),
                         waits=p.w("pe"), inc="act")
                    if self.dbg_stage == 28 and qg == 0 and oc == 0:
                        p.op("vector", lambda E: E.tensor_copy(self.SQT[:, 0:256], QB),
                             waits=p.w("act"), inc="dve")
                        p.op("sync", lambda E: E.dma_start(out=self.dbg[s, 4], in_=self.SQT),
                             waits=p.w("dve"), inc="st", incn=16)
                        p.op("vector", None, waits=p.w("st"))
                    if self.p2_sub == 3:
                        continue
                    for hh in range(2):
                        h = 2 * oc + hh
                        base = 64 * hh
                        s_done = {}
                        exp_done = {}
                        pv_done = {}

                        def emit_S(kt):
                            bank = 1 + (kt % 2)
                            wts = ([("act", exp_done[kt - 2])] if kt - 2 in exp_done
                                   else p.w("act", "dve"))
                            s_done[kt] = p.op("tensor", lambda E, kt=kt, bank=bank,
                                              base=base, oc=oc: E.matmul(
                                self.pb[bank][:, 0:256],
                                kbV[base:base + 64, oc, kt * 128:(kt + 1) * 128],
                                QB[base:base + 64, :], start=True, stop=True,
                                skip_group_check=True), waits=wts, inc="pe")

                        emit_S(0)
                        emit_S(1)
                        for kt in range(16):
                            bank = 1 + (kt % 2)
                            ptb = PT2[kt % 2]
                            if self.dbg_stage == 30 and qg == 0 and oc == 0 and hh == 0 and kt == 0:
                                p.op("scalar", lambda E, bank=bank: E.copy(
                                    self.SQT[:, 0:256], self.pb[bank][:, 0:256]),
                                     waits=[("pe", s_done[0])], inc="act")
                                p.op("sync", lambda E: E.dma_start(out=self.dbg[s, 6],
                                                                   in_=self.SQT),
                                     waits=p.w("act"), inc="st", incn=16)
                                p.op("scalar", None, waits=p.w("st"))
                            wexp = [("pe", pv_done[kt - 2] if kt >= 2 else s_done[kt])]
                            exp_done[kt] = p.op("scalar", lambda E, bank=bank, ptb=ptb:
                                                E.activation(ptb, self.pb[bank][:, 0:256],
                                                             AF.Exp, scale=0.125),
                                                waits=wexp, inc="act")
                            if kt + 2 < 16:
                                emit_S(kt + 2)
                            if self.dbg_stage == 29 and qg == 0 and oc == 0 and hh == 0 and kt == 0:
                                p.op("vector", lambda E, ptb=ptb: E.tensor_copy(
                                    self.SQT[:, 0:256], ptb), waits=p.w("act"), inc="dve")
                                p.op("sync", lambda E: E.dma_start(out=self.dbg[s, 5],
                                                                   in_=self.SQT),
                                     waits=p.w("dve"), inc="st", incn=16)
                                p.op("vector", None, waits=p.w("st"))
                            if self.p2_sub == 4:
                                continue
                            pv_done[kt] = p.op("tensor", lambda E, h=h, kt=kt, ptb=ptb: E.matmul(
                                self.pb[3][0:65, 0:256], vbV[:, h, kt, :],
                                ptb, start=(kt == 0), stop=(kt == 15), skip_group_check=True),
                                 waits=[("act", exp_done[kt])] + (p.w("dve") if kt == 0 else []),
                                 inc="pe")
                        if self.p2_sub == 4:
                            continue
                        if self.dbg_stage == 27 and qg == 0 and oc == 0 and hh == 0:
                            p.op("scalar", lambda E: E.copy(self.SQT[0:65, 0:256],
                                                            self.pb[3][0:65, 0:256]),
                                 waits=[("pe", pv_done[15])], inc="act")
                            p.op("sync", lambda E: E.dma_start(out=self.dbg[s, 0],
                                                               in_=self.SQT),
                                 waits=p.w("act"), inc="st", incn=16)
                            p.op("scalar", None, waits=p.w("st"))
                        p.op("scalar", lambda E: E.activation(DSC[64:65, :],
                                                              self.pb[3][64:65, 0:256], AF.Ln),
                             waits=[("pe", pv_done[15])], inc="act")
                        p.op("scalar", lambda E: E.activation(DSC[64:65, :], DSC[64:65, :],
                                                              AF.Exp, scale=-1.0),
                             waits=p.w("act"), inc="act")
                        p.op("tensor", lambda E: E.matmul(self.pb[4][0:64, 0:256],
                                                          self.onesf[64:65, :], DSC[64:65, :],
                                                          start=True, stop=True,
                                                          skip_group_check=True),
                             waits=p.w("act", "dve"), inc="pe")
                        p.op("scalar", lambda E: E.copy(BCS, self.pb[4][0:64, 0:256]),
                             waits=p.w("pe"), inc="act")
                        p.op("vector", lambda E, base=base, oc=oc: E.tensor_tensor(
                            OFM2.rearrange("p (oc t) -> p oc t", t=256)[base:base + 64, oc, :],
                            self.pb[3][0:64, 0:256], BCS, ALU.mult),
                             waits=p.w("act", "pe"), inc="dve")
                if self.dbg_stage == 26:
                    for half in range(2):
                        p.op("vector", lambda E, half=half: E.tensor_copy(
                            self.SQT, OFM2[:, half * 1024:(half + 1) * 1024]),
                             waits=p.w("dve"), inc="dve")
                        p.op("sync", lambda E, qg=qg, half=half: E.dma_start(
                            out=self.dbg[s, 2 * qg + half], in_=self.SQT),
                             waits=p.w("dve"), inc="st", incn=16)
                        p.op("vector", None, waits=p.w("st"))
                if self.p2_sub in (3, 4, 5):
                    continue
                ofuV = OFM2.rearrange("p (oc t) -> p oc t", t=256)
                for ts in range(2):
                    for nh in range(2):
                        bank = 6 + nh
                        for oc in range(8):
                            p.op("tensor", lambda E, oc=oc, ts=ts, nh=nh, bank=bank: E.matmul(
                                self.pb[bank], ofuV[:, oc, ts * 128:(ts + 1) * 128],
                                wobV[:, oc, nh * 512:(nh + 1) * 512],
                                start=(oc == 0), stop=False, skip_group_check=True),
                                 waits=p.w("dve", "dmaw") if oc == 0 else ())
                        p.op("tensor", lambda E, nh=nh, bank=bank: E.matmul(
                            self.pb[bank], self.ones1, self.boB_row[:, nh * 512:(nh + 1) * 512],
                            start=False, stop=True, skip_group_check=True), inc="pe")
                        p.op("vector", lambda E, ts=ts, nh=nh, bank=bank: E.tensor_tensor(
                            self.USTG[ts][:, nh * 512:(nh + 1) * 512],
                            self.USTG[ts][:, nh * 512:(nh + 1) * 512],
                            self.pb[bank], ALU.add),
                             waits=p.w("pe") + [(f"dx{ts}", uv[ts])], inc="dve")
                    p.op("sync", lambda E, ts=ts, ts0=ts0: E.dma_start(
                        out=self.uscr[s, ts0 + ts], in_=self.USTG[ts]),
                         waits=p.w("dve"), inc="dmax", incn=16)
                    if self.dbg_stage == 2:
                        p.op("sync", lambda E, ts=ts, ts0=ts0: E.dma_start(
                            out=self.dbg[s, ts0 + ts], in_=self.USTG[ts]),
                             inc="st", incn=16)
            p.op("sync", None, waits=p.w("dmax"))
            p.op("gpsimd", None, waits=p.w("dmax", "dve", "act", "pe"))

    def p3(self, s):
        p, nc = self.p, self.nc
        with ExitStack() as ps:
            sb = lambda name, shape, dt: ps.enter_context(
                nc.sbuf_tensor(f"{name}_s{s}", shape, dt)).ap()
            Y = sb("Y", [128, NT * 1024], F32)
            H2 = sb("H2", [128, NDC * 2048], BF16)
            W1R = sb("W1R", [128, NDC * 2048], BF16)
            W2R = sb("W2R", [128, 16 * 1024], BF16)
            HID = sb("HID", [128, 16 * 256], BF16)
            self.H2LO = HID[:, 0:1024]
            self.ABFLO = HID[:, 1024:2048]
            yV = Y.rearrange("p (t d) -> p t d", d=1024)
            h2V = H2.rearrange("p (dc t) -> p dc t", t=2048)
            w1V = W1R.rearrange("p (dc f) -> p dc f", f=2048)
            w2V = W2R.rearrange("p (fc o) -> p fc o", o=1024)
            hidV = HID.rearrange("p (fc t) -> p fc t", t=256)

            yl = p.op("sync", lambda E: E.dma_start(
                out=yV, in_=self.uscr[s].rearrange("t p d -> p t d")),
                 waits=p.w("dve"), inc="dmax", incn=16)

            for t in range(NT):
                ysl = yV[:, t, :]
                m, rstd = self.ln_stats(ysl, waits_sc=[("dmax", yl)], waits_ve=[("dmax", yl)])
                p.op("vector", lambda E, ysl=ysl, m=m, rs=rstd: E.tensor_scalar(
                    self.SQT, ysl, m, rs, ALU.subtract, ALU.mult), waits=p.w("act"), inc="dve")
                p.op("vector", lambda E: E.tensor_copy(self.ABF, self.SQT),
                     waits=p.w("dve"), inc="dve")
                p.op("vector", lambda E: E.tensor_tensor(self.ABFLO, self.SQT, self.ABF,
                                                         ALU.subtract),
                     waits=p.w("dve"), inc="dve")
                scale = [self.lnp_c[:, 16 + dc:16 + dc + 1] for dc in range(8)]
                bias = [self.lnp_c[:, 24 + dc:24 + dc + 1] for dc in range(8)]
                self.tile_to_fm(self.ABF, 0,
                                [h2V[:, dc, t * 128:(t + 1) * 128] for dc in range(NDC)],
                                scale=scale, bias=bias, first_waits=p.w("dve", "act"))
                zb = self.CONSTV[:, 6:7]
                self.tile_to_fm(self.ABFLO, 0,
                                [self.H2LO.rearrange("p (dc c) -> p dc c", c=128)[:, dc, :]
                                 for dc in range(NDC)],
                                scale=scale, bias=[zb] * 8, first_waits=p.w("dve", "act"))
                h2loV = self.H2LO.rearrange("p (dc c) -> p dc c", c=128)
                pairs = ([(h2V[:, dc, t * 128:(t + 1) * 128], self.gatew_bf[:, dc * 8:(dc + 1) * 8])
                          for dc in range(NDC)] +
                         [(h2loV[:, dc, :], self.gatew_bf[:, dc * 8:(dc + 1) * 8])
                          for dc in range(NDC)] +
                         [(h2V[:, dc, t * 128:(t + 1) * 128], self.gatew_lo[:, dc * 8:(dc + 1) * 8])
                          for dc in range(NDC)])
                self.mm_group(self.pb[3][:, 0:8], pairs, first_waits=p.w("act", "dmaw", "dve"))
                p.op("tensor", lambda E: E.matmul(self.pb[3][:, 0:8], self.ones1, self.gateb_row,
                                                  start=False, stop=True, skip_group_check=True),
                     inc="pe")
                st = self.STAT
                LG = self.SQT
                p.op("scalar", lambda E: E.copy(LG[:, 0:8], self.pb[3][:, 0:8]),
                     waits=p.w("pe"), inc="act")
                p.op("vector", lambda E: E.reduce_max(st[:, 8:9], LG[:, 0:8], axis=AX.X),
                     waits=p.w("act"), inc="dve")
                p.op("vector", lambda E: E.tensor_scalar(LG[:, 16:24], LG[:, 0:8], st[:, 8:9],
                                                         None, ALU.is_equal),
                     waits=p.w("dve"), inc="dve")
                p.op("vector", lambda E: E.tensor_scalar(LG[:, 8:16], LG[:, 16:24], -1e30, None,
                                                         ALU.mult), waits=p.w("dve"), inc="dve")
                p.op("vector", lambda E: E.tensor_tensor(LG[:, 8:16], LG[:, 8:16], LG[:, 0:8],
                                                         ALU.add), waits=p.w("dve"), inc="dve")
                p.op("vector", lambda E: E.reduce_max(st[:, 9:10], LG[:, 8:16], axis=AX.X),
                     waits=p.w("dve"), inc="dve")
                p.op("vector", lambda E: E.tensor_scalar(LG[:, 24:32], LG[:, 8:16], st[:, 9:10],
                                                         None, ALU.is_equal),
                     waits=p.w("dve"), inc="dve")
                p.op("vector", lambda E: E.tensor_tensor(st[:, 10:11], st[:, 9:10], st[:, 8:9],
                                                         ALU.subtract), waits=p.w("dve"), inc="dve")
                p.op("scalar", lambda E: E.activation(st[:, 10:11], st[:, 10:11], AF.Exp),
                     waits=p.w("dve"), inc="act")
                p.op("vector", lambda E: E.tensor_scalar(st[:, 11:12], st[:, 10:11], 1.0, None,
                                                         ALU.add), waits=p.w("act"), inc="dve")
                p.op("vector", lambda E: E.reciprocal(st[:, 12:13], st[:, 11:12]),
                     waits=p.w("dve"), inc="dve")
                p.op("vector", lambda E: E.tensor_tensor(st[:, 13:14], st[:, 10:11], st[:, 12:13],
                                                         ALU.mult), waits=p.w("dve"), inc="dve")
                ewt = self.ew_c[:, t * 8:(t + 1) * 8]
                p.op("vector", lambda E, o=ewt: E.tensor_scalar(o, LG[:, 16:24], st[:, 12:13],
                                                                None, ALU.mult),
                     waits=p.w("dve"), inc="dve")
                p.op("vector", lambda E: E.tensor_scalar(LG[:, 24:32], LG[:, 24:32], st[:, 13:14],
                                                         None, ALU.mult),
                     waits=p.w("dve"), inc="dve")
                p.op("vector", lambda E, o=ewt: E.tensor_tensor(o, o, LG[:, 24:32], ALU.add),
                     waits=p.w("dve"), inc="dve")
                p.op("vector", lambda E, i=ewt: E.tensor_copy(self.ew_bf, i),
                     waits=p.w("dve"), inc="dve")
                p.op("tensor", lambda E: E.transpose(self.pb_bf[4][0:8, 0:128], self.ew_bf,
                                                     self.id128),
                     waits=p.w("dve", "act"), inc="pe")
                p.op("scalar", lambda E: E.copy(self.ewT, self.pb_bf[4][0:8, 0:128]),
                     waits=p.w("pe"), inc="act")
                for nh in range(2):
                    p.op("tensor", lambda E, nh=nh: E.matmul(
                        self.pb[5], self.ewT, self.b2_bf[:, nh * 512:(nh + 1) * 512],
                        start=True, stop=True, skip_group_check=True),
                         waits=p.w("act", "dve"), inc="pe")
                    p.op("vector", lambda E, t=t, nh=nh: E.tensor_tensor(
                        yV[:, t, nh * 512:(nh + 1) * 512], yV[:, t, nh * 512:(nh + 1) * 512],
                        self.pb[5], ALU.add), waits=p.w("pe"), inc="dve")

            W1 = self.win["L0_ffn_W1"]
            W2 = self.win["L0_ffn_W2"]
            last_w1 = p.cnt["pe"]
            last_w2 = p.cnt["pe"]
            for e in range(E):
                for half in range(2):
                    l1 = p.op("gpsimd", lambda E_, e=e, half=half: E_.dma_start(
                        out=w1V, in_=W1[e].rearrange("(dc p) f -> p dc f", p=128)[
                            :, :, half * 2048:(half + 1) * 2048]),
                         waits=[("pe", last_w1)], inc="dmaw", incn=16)
                    l2 = p.op("gpsimd", lambda E_, e=e, half=half: E_.dma_start(
                        out=w2V, in_=W2[e].rearrange("(fc p) o -> p fc o", p=128)[
                            :, half * 16:(half + 1) * 16, :]),
                         waits=[("pe", last_w2)], inc="dmaw", incn=16)
                    for tp in range(8):
                        relu_done = {}
                        for fcl in range(16):
                            fc_g = half * 16 + fcl
                            bank = 1 + (fcl % 2)
                            wts = ([("act", relu_done[fcl - 2])] if fcl - 2 in relu_done
                                   else [("dmaw", l2)] + p.w("act", "dve"))
                            pairs = [(w1V[:, dc, fcl * 128:(fcl + 1) * 128],
                                      h2V[:, dc, tp * 256:(tp + 1) * 256]) for dc in range(NDC)]
                            g = self.mm_group(self.pb[bank][:, 0:256], pairs, first_waits=wts)
                            if tp == 7 and fcl == 15:
                                last_w1 = g
                            relu_done[fcl] = p.op(
                                "scalar", lambda E_, fcl=fcl, e=e, fc_g=fc_g, bank=bank:
                                E_.activation(hidV[:, fcl, :], self.pb[bank][:, 0:256], AF.Relu,
                                              bias=self.b1_c[:, e * 32 + fc_g:e * 32 + fc_g + 1]),
                                waits=[("pe", g)], inc="act")
                        stt_done = {}
                        for ts in range(2):
                            t = tp * 2 + ts
                            for nh in range(2):
                                bank = 5 + nh
                                key = (ts - 1, nh)
                                wts = ([("dve", stt_done[key])] if key in stt_done
                                       else [("dmaw", l2)] + p.w("dve"))
                                if ts == 0 and nh == 0:
                                    wts = wts + [("act", relu_done[15])]
                                pairs = [(hidV[:, fcl, ts * 128:(ts + 1) * 128],
                                          w2V[:, fcl, nh * 512:(nh + 1) * 512])
                                         for fcl in range(16)]
                                g2 = self.mm_group(self.pb[bank], pairs, first_waits=wts)
                                if tp == 7 and ts == 1 and nh == 1:
                                    last_w2 = g2
                                ew_col = self.ew_c[:, t * 8 + e:t * 8 + e + 1]
                                stt_done[(ts, nh)] = p.op(
                                    "vector", lambda E_, t=t, nh=nh, bank=bank, ec=ew_col:
                                    E_.scalar_tensor_tensor(
                                        yV[:, t, nh * 512:(nh + 1) * 512], self.pb[bank], ec,
                                        yV[:, t, nh * 512:(nh + 1) * 512], ALU.mult, ALU.add),
                                    waits=[("pe", g2)], inc="dve")
            if self.dbg_stage == 3:
                for t in range(NT):
                    p.op("sync", lambda E, t=t: E.dma_start(out=self.dbg[s, t], in_=yV[:, t, :]),
                         waits=p.w("dve"), inc="st", incn=16)
            if self.stop_after != 3:
                self.p4(s, yV)
            p.op("gpsimd", None, waits=p.w("dve", "act", "pe"))

    def p4(self, s, yV):
        p = self.p
        ow = {}
        for t in range(NT):
            xbuf = self.XSTG[t % 2]
            xv = p.op("sync", lambda E, o=xbuf, i=self.x1scr[s, t]: E.dma_start(out=o, in_=i),
                      waits=p.w("dve"), inc="dmax", incn=16)
            ysl = yV[:, t, :]
            m, rstd = self.ln_stats(ysl, waits_sc=p.w("dve"), waits_ve=p.w("dve"))
            obuf = self.USTG[t % 2]
            nv = p.op("vector", lambda E, ysl=ysl, m=m, rs=rstd: E.tensor_scalar(
                self.SQT, ysl, m, rs, ALU.subtract, ALU.mult), waits=p.w("act"), inc="dve")
            wadd = [("dmax", xv), ("dve", nv)]
            if t - 2 in ow:
                wadd.append(("st", ow[t - 2]))
            p.op("vector", lambda E, o=obuf, xb=xbuf: E.tensor_tensor(o, xb, self.SQT, ALU.add),
                 waits=wadd, inc="dve")
            ow[t] = p.op("sync", lambda E, o=self.out[s, t], i=obuf: E.dma_start(out=o, in_=i),
                 waits=p.w("dve"), inc="st", incn=16)


_BUILT = {}


def _get_program(dbg_stage=0, stop_after=0):
    key = (dbg_stage, stop_after)
    if key not in _BUILT:
        _BUILT[key] = KB(dbg_stage, stop_after).build()
    return _BUILT[key]


def kernel(**inputs):
    aux = build_aux()
    nc = _get_program()
    x = np.ascontiguousarray(inputs["L0_attn_x"])
    in_maps = []
    for c in range(8):
        m = {"x": np.ascontiguousarray(x[2 * c:2 * c + 2].reshape(NSEQ, NT, 128, D))}
        for k, v in inputs.items():
            if k != "L0_attn_x":
                m[k] = np.ascontiguousarray(v)
        m.update(aux)
        in_maps.append(m)
    res = run_bass_kernel_spmd(nc, in_maps, core_ids=list(range(8)))
    out = np.concatenate([np.asarray(r["out"]).reshape(NSEQ, S, D) for r in res.results], axis=0)
    return out.astype(np.float32)
